# version 18
# baseline (speedup 1.0000x reference)
from contextlib import ExitStack

import numpy as np
import ml_dtypes
import concourse.bass as bass
import concourse.mybir as mybir
from concourse.bass_utils import run_bass_kernel_spmd

F32 = mybir.dt.float32
BF16 = mybir.dt.bfloat16
ALU = mybir.AluOpType
AF = mybir.ActivationFunctionType
AX = mybir.AxisListType
BF = ml_dtypes.bfloat16

MAGIC = 12582912.0
QMAX = 32767.0

import concourse.dve_ops as dve_ops
from concourse.dve_op import DveOp
from concourse.dve_spec import Spec, Src0, Src1, C0, C1, C2, lower as dve_lower


def _register_op(name, body, ref):
    for o in dve_ops.OPS:
        if o.name == name:
            return o
    sha = dve_lower(Spec(body=body), ver="v3", perf_en=False, name=name).sha("v3")
    op = DveOp(name, Spec(body=body), subdim=False, uops_sha={"v3": sha},
               reference=ref)
    dve_ops.OPS.append(op)
    dve_ops.CUSTOM_DVE_SPECS[name] = op.spec
    dve_ops.CUSTOM_DVE_REFS[name] = op.ref
    dve_ops._SUB_OPCODE_FOR_NAME[name] = (
        dve_ops._CUSTOM_DVE_ROW_BASE + len(dve_ops._SUB_OPCODE_FOR_NAME))
    assert max(dve_ops._SUB_OPCODE_FOR_NAME.values()) < 0x20
    return op


QROUND = _register_op(
    "ANT_QROUND_SCALE",
    ((Src0 * C0 + C2) - C2) * C1,
    lambda in0, in1, s0, s1, imm2: np.rint(in0 * s0) * s1,
)
QROUND2 = _register_op(
    "ANT_QROUND2_MUL",
    ((Src0 * C0 + C2) - C2) * ((Src1 * C1 + C2) - C2),
    lambda in0, in1, s0, s1, imm2: np.rint(in0 * s0) * np.rint(in1 * s1),
)

D_MODEL = 1024
N_HEADS = 32
D_INNER = 2048
PROJ = 4256
EPS = 1e-5
NCORES = 8
KB = 128
CH = 512
TIL = KB * KB
AGW_UP = 0
AGW_GATE = 3 * 8 * TIL
AGW_DOWN = 2 * 3 * 8 * TIL
AGW_N = AGW_DOWN + 24 * TIL
ZT0 = 0
XT0 = 4 * 8 * (128 * 512)
DT0 = XT0 + 17 * 8 * TIL
WINB_N = DT0 + 8 * 128 * 32
SC_S0, SC_R0, SC_S1, SC_R1, SC_S2, SC_R2, SC_F3R3, SC_S3S1R4X, SC_F4R4, SC_S4 = range(10)

MAXKINDS = ("xn", "up", "ga", "t3", "t4")
NSLOT = 4



_SEM_ALIAS = {}
for _n in ("ones_v an_s an_v dx_t dx_v dx_v2 dq hcv hcs hch hch2 hnv htr "
           "hev trmm trv sge egtt egtv mb_init").split():
    _SEM_ALIAS[_n] = "p0c"
for _n in "wsh_ld hld xnh_wr xnh_ld sgl hsll winb_wr woutb_wr agw_wr".split():
    _SEM_ALIAS[_n] = "p0h"
for _n in "dtraw dts1 dts2 dts3 dtv1 dtv2 dtv3 dtv4 esl_s".split():
    _SEM_ALIAS[_n] = "dtch"
for _n in "xsv0 xss0".split():
    _SEM_ALIAS[_n] = "xst"
for _n in "xsv1 xss1".split():
    _SEM_ALIAS[_n] = "xst1"
_SEM_ALIAS["xrsd"] = "xch"
_SEM_ALIAS["xch0"] = "xch"
_SEM_ALIAS["xld0"] = "xld"
_SEM_ALIAS["xnbv0"] = "xnbv"
_SEM_ALIAS["xrsd"] = "xch"
for _n in "yfx1 yfx2 ygv yfx_f yfpe fxlf0 fxlf1".split():
    _SEM_ALIAS[_n] = "yfxc"
for _n in "ysqs rch rss ynv".split():
    _SEM_ALIAS[_n] = "rmsc"
for _k in ("xn", "up", "ga", "t3", "t4"):
    for _sfx, _c in (("_l", "fmL"), ("_l2", "fmL"), ("_v", "fmV"),
                     ("_v2", "fmV"), ("_v3", "fmV"), ("_v4", "fmV"),
                     ("_t", "fmT"), ("_r", "fmR")):
        _SEM_ALIAS[f"fm_{_k}{_sfx}"] = _c
for _n in "ccl0 ccl1 ccl2 ccl3".split():
    _SEM_ALIAS[_n] = "cclh"
for _n in "ccv0 ccv1 ccv2 ccv3 c2v c3v".split():
    _SEM_ALIAS[_n] = "cclv"
for _n in "cc0 cc1 cc2 cc3 ccagw ccah".split():
    _SEM_ALIAS[_n] = "ccp"
for _n in "cc0_wr cc1_wr cc2_wr cc3_wr".split():
    _SEM_ALIAS[_n] = "ccw"
_SEM_ALIAS.update(dict(
    dpe="eexp", gfree="ymm", ype="ylv", hape="ht2", datf="ht2", eglf="ht2",
    c1qf="c1tv", ope="opev", dnpe="opev", zpe="zev", xbpe="xbev",
    halope="haloev", srof="dar", xncf="xbmm", bcse="bcev", hslf="heffv",
    ftpe="ytev", obf="fxmm", sconst="p0c", scmm="p0c",
    wzbf0="zmm", wxbf0="xbmm", wxbf1="xbmm",
    ssbf0="ytev", ssbf1="ytev", t4f0="ytev", t4f1="ytev",
    wobf0="fxmm", wobf1="fxmm", xtbf0="ytev", xtbf1="ytev",
    cvt0="cvt", cvt1="cvt", cvt2="cvt", cvt3="cvt", cvt4="cvt",
    wstg0="swl0", wstg1="swl1", w2l0="swl0", w2l1="swl1",
    wstgl0="swl0", wstgl1="swl1",
    cloc="csp_wr", hbfc="dar", hfbc="opev",
    yttr="fxmm", opmm="fxmm", dnmm="fxmm", ftmm="fxmm",
    obv="ytev", fv="ytev", ov="ytev",
    gev="opev",
    sspwr="fxl0", wdtl="fxl0", outwr="fxl0",
    amx1="amx0", amx2="amx0", amx3="amx0", xntr1="xntr0",
    hpub_wr="ylwr",
    bttr="xttr", fttr="xttr", datm="xttr",
    btev="xtev", ftev="xtev",
    wtbf0="ugmm0", wtbf1="ugmm0", ugmm1="ugmm0", ugsp1="ugsp0",
    mxchain="fmR", hupd="ht2",
    cvt="xcsev", dtch="xbev",
))


class Prog:
    def __init__(self, nc, es):
        self.nc = nc
        self.es = es
        self._sems = {}
        self._pinst = None
        self._pend = {}

    def _res(self, name):
        while name in _SEM_ALIAS:
            name = _SEM_ALIAS[name]
        return name

    def sem(self, name):
        name = self._res(name)
        if name not in self._sems:
            self._sems[name] = [self.es.enter_context(self.nc.semaphore(name)), 0]
        return self._sems[name][0]

    def _flush(self):
        if self._pinst is not None:
            if len(self._pend) > 1:
                import os
                if os.environ.get("SEM_DEBUG"):
                    print("MULTI-INC:", sorted(self._pend.keys()), flush=True)
            for nm, k in self._pend.items():
                self._pinst.then_inc(self._sems[nm][0], k)
        self._pinst = None
        self._pend = {}

    def inc(self, name, inst, k=1):
        self.sem(name)
        nm = self._res(name)
        ent = self._sems[nm]
        if inst is not self._pinst:
            self._flush()
            self._pinst = inst
        self._pend[nm] = self._pend.get(nm, 0) + k
        ent[1] += k
        return ent[1]

    def val(self, name):
        self.sem(name)
        return self._sems[self._res(name)][1]

    def wait(self, eng, name, v=None):
        h = self.sem(name)
        eng.wait_ge(h, self.val(name) if v is None else v)


def build_nc(T=4096):
    nc = bass.Bass("TRN2", target_bir_lowering=False, debug=False)
    _gblk_cm = nc.Block()
    _gblk = _gblk_cm.__enter__()

    class _BlkProxy:
        def __enter__(self):
            return _gblk

        def __exit__(self, *a):
            return False

    def BLOCK():
        return _BlkProxy()
    NCH = T // CH
    NTT = T // KB

    def din(name, shape, dt=F32):
        return nc.dram_tensor(name, shape, dt, kind="ExternalInput").ap()

    def dint(name, shape, dt=F32):
        return nc.dram_tensor(name, shape, dt).ap()

    x_in = din("x", [T, D_MODEL])
    xh_in = din("x_halo", [3, D_MODEL])
    w_in = din("w_in", [D_MODEL, PROJ])
    w_out = din("w_out", [D_INNER, D_MODEL])
    wup_sh = din("wup_sh", [384, 1024])
    wgate_sh = din("wgate_sh", [384, 1024])
    wdown_sh = din("wdown_sh", [128, 3072])
    conv_w_in = din("conv_w_sb", [128, 68])
    conv_b_in = din("conv_b_sb", [128, 17])
    normw_in = din("normw_sb", [128, 16])
    alog_in = din("a_log", [32, 1])
    dtb_in = din("dt_bias", [32, 1])
    dpar_in = din("d_par", [1, 32])
    hmask_in = din("hmask", [64, 8])
    idbf_in = din("ident_bf", [128, 128], BF16)
    idf_in = din("ident_f32", [128, 128], F32)
    trineg_in = din("trineg4", [128, 512], BF16)

    out_t = nc.dram_tensor("out", [T, D_MODEL], F32, kind="ExternalOutput").ap()

    XNT = dint("XNT", [8, NTT, KB, KB], BF16)
    XNH = dint("XNH", [8, KB, 4], BF16)
    WINB = dint("WINB", [1, WINB_N], BF16)
    WOUTB = dint("WOUTB", [8, 16, KB, KB], BF16)
    AGW_I = dint("AGW_I", [1, AGW_N], BF16)
    AGW_O = dint("AGW_O", [NCORES, AGW_N], BF16)
    CC_I = [dint(f"CC{i}_I", [1, 2 if i == 1 else 1]) for i in range(4)]
    CC_O = [dint(f"CC{i}_O", [1, 2 if i == 1 else 1]) for i in range(4)]
    HPUB = dint("HPUB", [1, 64 * D_INNER])
    HALL = dint("HALL", [NCORES, 64 * D_INNER])
    UPRAW = dint("UPRAW", [24, NCH, KB, CH], BF16)
    GRAW = dint("GRAW", [24, NCH, KB, CH], BF16)
    T3S = dint("T3S", [24, NCH, KB, CH], BF16)
    T4S = dint("T4S", [24, NCH, KB, CH], BF16)
    ZS = dint("ZS", [NTT, 4, KB, CH], BF16)
    YL = dint("YL", [NTT, 4, KB, CH], BF16)
    CSP = dint("CSP", [64, T], BF16)
    SGH = dint("SGH", [32, T])
    SLOC = dint("SLOC", [32, T])
    SSP = dint("SSP", [8, NCH, KB, CH], BF16)


    es = ExitStack()
    P = Prog(nc, es)

    def sb(name, shape, dt=F32, st=None):
        return (st or es).enter_context(nc.sbuf_tensor(name, shape, dt))

    def pst(name, shape, dt=F32, st=None):
        return (st or es).enter_context(nc.psum_tensor(name, shape, dt))

    idbf = sb("idbf", [128, 128], BF16)
    idf32 = sb("idf32", [128, 128], F32)
    trineg = sb("trineg", [128, 512], BF16)
    ones_r = sb("ones_r", [1, 512], F32)
    negr = sb("negr", [1, 128], F32)
    convw = sb("convw", [128, 68], F32)
    convb = sb("convb", [128, 17], F32)
    normw = sb("normw", [128, 16], F32)
    alog = sb("alog", [32, 1], F32)
    aneg = sb("aneg", [32, 1], F32)
    dtbias = sb("dtbias", [32, 1], F32)
    dparr = sb("dparr", [1, 32], F32)
    hmask = sb("hmask_sb", [64, 8], F32)
    scal = sb("scal", [128, 12], F32)
    ccs = sb("ccs", [1, 12], F32)
    mred = sb("mred", [128, 12], F32)
    mrow = sb("mrow", [1, 8], F32)
    _mcb = [sb(f"maxc_{i}", [128, 200], F32) for i in range(3)]
    MAXCOL = {"xn": _mcb[0], "up": _mcb[1], "ga": _mcb[2],
              "t3": _mcb[0], "t4": _mcb[1]}
    ps_b = pst("ps_b", [128, 128], F32)
    ps_tb = pst("ps_tb", [128, 512], BF16)

    def accmax(v, kind, idx, src):
        assert idx < 200
        last = v.tensor_reduce(MAXCOL[kind][:, idx:idx + 1], src, AX.X,
                               ALU.max, apply_absolute_value=True)
        P.inc("amx0", last)
        return last

    def finalize_max(kind, outcol, waits, nt):
        c2 = 2 * outcol
        tg = f"fm_{kind}"
        with BLOCK() as blk:
            @blk.vector
            def _(v):
                for w in waits:
                    P.wait(v, w)
                P.wait(v, "amx0", P.val("amx0"))
                P.inc(f"{tg}_v", v.tensor_reduce(
                    mred[:, c2:c2 + 1], MAXCOL[kind][:, 0:nt], AX.X, ALU.max))

            @blk.tensor
            def _(t):
                P.wait(t, f"{tg}_v")
                P.wait(t, "mxchain", P.val("mxchain"))
                P.wait(t, "scbc", P.val("scbc"))
                P.inc(f"{tg}_t", t.transpose(ps_b[0:1, 0:128],
                                             mred[:, c2:c2 + 1], idf32[:]))

            @blk.vector
            def _(v):
                P.wait(v, f"{tg}_t")
                last = v.tensor_reduce(mrow[0:1, outcol:outcol + 1],
                                       ps_b[0:1, 0:128], AX.X, ALU.max)
                P.inc(f"{tg}_r", last)
                P.inc("mxchain", last)

    def bcast_scal(pairs, waits):
        with BLOCK() as blk:
            @blk.tensor
            def _(t):
                for w in waits:
                    P.wait(t, w)
                P.wait(t, "scbc", P.val("scbc"))
                last = None
                for sc_, dc_ in pairs:
                    last = t.matmul(ps_b[:, dc_:dc_ + 1], ones_r[:, 0:128],
                                    ccs[0:1, sc_:sc_ + 1], start=True, stop=True)
                P.inc("scmm", last)

            @blk.vector
            def _(v):
                P.wait(v, "scmm")
                last = None
                for sc_, dc_ in pairs:
                    last = v.tensor_copy(scal[:, dc_:dc_ + 1],
                                         ps_b[:, dc_:dc_ + 1])
                P.inc("scbc", last)

    def vchain(sem, *steps):
        for stp in steps:
            with BLOCK() as blk:
                @blk.vector
                def _(v, stp=stp):
                    P.wait(v, sem)
                    last = stp(v)
                    P.inc(sem, last)

    with BLOCK() as blk:
        @blk.sync
        def _(sy):
            for ap_, src in ((idbf[:], idbf_in), (idf32[:], idf_in),
                             (trineg[:], trineg_in), (convw[:], conv_w_in),
                             (convb[:], conv_b_in), (normw[:], normw_in),
                             (alog[:], alog_in), (dtbias[:], dtb_in),
                             (dparr[:], dpar_in), (hmask[:], hmask_in)):
                P.wait(sy, "c_ld", P.val("c_ld"))
                P.inc("c_ld", sy.dma_start(out=ap_, in_=src), 16)

        @blk.vector
        def _(v):
            v.memset(negr[:], -1.0)
            P.inc("ones_v", v.memset(ones_r[:], 1.0))

        @blk.scalar
        def _(s):
            P.wait(s, "c_ld")
            P.inc("an_s", s.activation(aneg[:], alog[:], AF.Exp))

    with BLOCK() as blk:
        @blk.vector
        def _(v):
            P.wait(v, "an_s")
            P.inc("an_v", v.tensor_scalar(aneg[:], aneg[:], -1.0, None, ALU.mult))

        @blk.tensor
        def _(t):
            P.wait(t, "c_ld")
            P.wait(t, "ones_v")
            P.inc("dx_t", t.matmul(ps_b[:, 0:32], ones_r[:, 0:128], dparr[:],
                                   start=True, stop=True))

    d32 = sb("d32", [128, 32], F32)
    with BLOCK() as blk:
        @blk.vector
        def _(v):
            P.wait(v, "dx_t")
            P.inc("dx_v", v.tensor_copy(d32[:], ps_b[:, 0:32]))

    ph0 = ExitStack()
    wstgf = [sb(f"wstgf{i}", [128, PROJ], F32, ph0) for i in range(2)]
    wstg = [sb(f"wstg{i}", [128, PROJ], BF16, ph0) for i in range(2)]
    for fb in range(8):
        bi = fb % 2
        with BLOCK() as blk:
            @blk.sync
            def _(sy, fb=fb, bi=bi):
                P.wait(sy, f"wstgl{bi}", P.val(f"wstgl{bi}"))
                P.inc(f"wstgl{bi}", sy.dma_start(
                    out=wstgf[bi][:], in_=w_in[fb * 128:(fb + 1) * 128, :]),
                    16)

            @blk.scalar
            def _(s, fb=fb, bi=bi):
                P.wait(s, f"wstgl{bi}")
                P.wait(s, "winb_wr", P.val("winb_wr"))
                P.inc(f"wstg{bi}", s.activation(wstg[bi][:], wstgf[bi][:],
                                                AF.Copy))

            @blk.sync
            def _(sy, fb=fb, bi=bi):
                P.wait(sy, f"wstg{bi}")
                P.wait(sy, "winb_wr", P.val("winb_wr"))
                P.inc("winb_wr", sy.dma_start(
                    out=WINB[0:1, ZT0:ZT0 + 4 * 8 * 128 * 512].rearrange(
                        "o (z e p q) -> (z e) p (o q)", z=4, e=8, p=128
                    )[fb::8].transpose([1, 0, 2]),
                    in_=wstg[bi][:, 0:2048].rearrange(
                        "p (z q) -> p z q", z=4)), 16)
                off = XT0 + fb * TIL
                P.wait(sy, "winb_wr", P.val("winb_wr"))
                P.inc("winb_wr", sy.dma_start(
                    out=WINB[0:1, XT0:XT0 + 17 * 8 * TIL].rearrange(
                        "o (b e p q) -> (b e) p (o q)", b=17, e=8, p=128
                    )[fb::8].transpose([1, 0, 2]),
                    in_=wstg[bi][:, 2048:2048 + 17 * 128].rearrange(
                        "p (b q) -> p b q", b=17)), 16)
                off = DT0 + fb * (128 * 32)
                P.wait(sy, "winb_wr", P.val("winb_wr"))
                P.inc("winb_wr", sy.dma_start(
                    out=WINB[0:1, off:off + 128 * 32].rearrange(
                        "o (p q) -> p (o q)", p=128),
                    in_=wstg[bi][:, 4224:4256]), 16)

    wstg2f = [sb(f"wstg2f{i}", [128, 1024], F32, ph0) for i in range(2)]
    wstg2 = [sb(f"wstg2{i}", [128, 1024], BF16, ph0) for i in range(2)]
    for kb in range(16):
        bi = kb % 2
        with BLOCK() as blk:
            @blk.sync
            def _(sy, kb=kb, bi=bi):
                P.wait(sy, f"w2l{bi}", P.val(f"w2l{bi}"))
                P.inc(f"w2l{bi}", sy.dma_start(
                    out=wstg2f[bi][:], in_=w_out[kb * 128:(kb + 1) * 128, :]),
                    16)

            @blk.vector
            def _(v, kb=kb, bi=bi):
                P.wait(v, f"w2l{bi}")
                P.wait(v, "c_ld")
                P.wait(v, "woutb_wr", P.val("woutb_wr"))
                P.inc(f"w2n{bi}", v.tensor_scalar(
                    wstg2[bi][:], wstg2f[bi][:], normw[:, kb:kb + 1], None,
                    ALU.mult))

            @blk.sync
            def _(sy, kb=kb, bi=bi):
                P.wait(sy, f"w2n{bi}")
                P.wait(sy, "woutb_wr", P.val("woutb_wr"))
                P.inc("woutb_wr", sy.dma_start(
                    out=WOUTB[:, kb, :, :].transpose([1, 0, 2]),
                    in_=wstg2[bi][:].rearrange("p (b q) -> p b q", b=8)), 16)

    ph1 = ExitStack()
    wsh = sb("wsh", [128, 6144], F32, ph1)
    wshd = sb("wshd", [128, 3072], F32, ph1)
    rqb = sb("rqb", [128, 6144], F32, ph1)
    rqbd = sb("rqbd", [128, 3072], F32, ph1)
    wqb = sb("wqb", [128, 6144], BF16, ph1)
    wqbd = sb("wqbd", [128, 3072], BF16, ph1)
    amax = sb("amax", [128, 288], F32, ph1)
    sc7 = sb("sc7", [128, 288], F32, ph1)
    rcp7 = sb("rcp7", [128, 288], F32, ph1)
    trsb = sb("trsb", [128, 512], BF16, ph1)
    ps_tr = [pst(f"ps_tr{i}", [128, 512], BF16, ph1) for i in range(2)]

    with BLOCK() as blk:
        @blk.sync
        def _(sy):
            for i in range(3):
                P.wait(sy, "wsh_ld", P.val("wsh_ld"))
                P.inc("wsh_ld", sy.dma_start(
                    out=wsh[:, i * 1024:(i + 1) * 1024],
                    in_=wup_sh[i * 128:(i + 1) * 128, :]), 16)
                P.wait(sy, "wsh_ld", P.val("wsh_ld"))
                P.inc("wsh_ld", sy.dma_start(
                    out=wsh[:, (3 + i) * 1024:(4 + i) * 1024],
                    in_=wgate_sh[i * 128:(i + 1) * 128, :]), 16)
            P.wait(sy, "wsh_ld", P.val("wsh_ld"))
            P.inc("wsh_ld", sy.dma_start(out=wshd[:], in_=wdown_sh), 16)

        @blk.vector
        def _(v):
            P.wait(v, "wsh_ld")
            v.tensor_reduce(amax[:, 0:192].unsqueeze(2),
                            wsh[:].rearrange("p (b e) -> p b e", e=32),
                            AX.X, ALU.max, apply_absolute_value=True)
            P.inc("dq", v.tensor_reduce(
                amax[:, 192:288].unsqueeze(2),
                wshd[:].rearrange("p (b e) -> p b e", e=32),
                AX.X, ALU.max, apply_absolute_value=True))

    vchain(
        "dq",
        lambda v: v.tensor_scalar(sc7[:], amax[:], 1.0 / 7.0, 1e-12,
                                  ALU.mult, ALU.max),
        lambda v: v.reciprocal(rcp7[:], sc7[:]),
        lambda v: (v.tensor_tensor(
            rqb[:].rearrange("p (b e) -> p b e", e=32),
            wsh[:].rearrange("p (b e) -> p b e", e=32),
            rcp7[:, 0:192].unsqueeze(2).broadcast_to([128, 192, 32]), ALU.mult),
            v.tensor_tensor(
                rqbd[:].rearrange("p (b e) -> p b e", e=32),
                wshd[:].rearrange("p (b e) -> p b e", e=32),
                rcp7[:, 192:288].unsqueeze(2).broadcast_to([128, 96, 32]),
                ALU.mult))[-1],
        lambda v: (v._custom_dve(QROUND, out=rqb[:], in0=rqb[:], s0=1.0,
                                 s1=1.0, imm2=MAGIC),
                   v._custom_dve(QROUND, out=rqbd[:], in0=rqbd[:], s0=1.0,
                                 s1=1.0, imm2=MAGIC))[-1],
        lambda v: (v.tensor_tensor(
            wqb[:].rearrange("p (b e) -> p b e", e=32),
            rqb[:].rearrange("p (b e) -> p b e", e=32),
            sc7[:, 0:192].unsqueeze(2).broadcast_to([128, 192, 32]), ALU.mult),
            v.tensor_tensor(
                wqbd[:].rearrange("p (b e) -> p b e", e=32),
                rqbd[:].rearrange("p (b e) -> p b e", e=32),
                sc7[:, 192:288].unsqueeze(2).broadcast_to([128, 96, 32]),
                ALU.mult))[-1],
    )

    tiles = []
    for pb in range(6):
        mat, m3 = pb // 3, pb % 3
        for kbi in range(8):
            tiles.append((wqb[:, pb * 1024 + kbi * 128:pb * 1024 + (kbi + 1) * 128],
                          (AGW_UP if mat == 0 else AGW_GATE) + (m3 * 8 + kbi) * TIL))
    for kbi in range(24):
        tiles.append((wqbd[:, kbi * 128:(kbi + 1) * 128], AGW_DOWN + kbi * TIL))

    for gi in range(0, len(tiles), 4):
        grp = tiles[gi:gi + 4]
        bank = ps_tr[(gi // 4) % 2]
        with BLOCK() as blk:
            @blk.tensor
            def _(t, grp=grp, bank=bank, gi=gi):
                P.wait(t, "dq")
                P.wait(t, "trv", P.val("trv"))
                last = None
                for j, (src, off) in enumerate(grp):
                    last = t.transpose(bank[:, j * 128:(j + 1) * 128], src,
                                       idbf[:])
                P.inc("trmm", last)

            @blk.vector
            def _(v, grp=grp, bank=bank):
                P.wait(v, "trmm")
                P.wait(v, "agw_wr", P.val("agw_wr"))
                P.inc("trv", v.tensor_copy(trsb[:, 0:len(grp) * 128],
                                           bank[:, 0:len(grp) * 128]))

            @blk.sync
            def _(sy, grp=grp):
                P.wait(sy, "trv")
                for j, (src, off) in enumerate(grp):
                    P.wait(sy, "agw_wr", P.val("agw_wr"))
                    P.inc("agw_wr", sy.dma_start(
                        out=AGW_I[0:1, off:off + TIL].rearrange(
                            "o (a b) -> a (o b)", a=128),
                        in_=trsb[:, j * 128:(j + 1) * 128]), 16)

    with BLOCK() as blk:
        @blk.gpsimd
        def _(g):
            P.wait(g, "agw_wr")
            P.inc("ccagw", g.collective_compute(
                "AllGather", ALU.bypass, replica_groups=[list(range(NCORES))],
                ins=[AGW_I.opt()], outs=[AGW_O.opt()]))

    ph1.close()
    ph0.close()

    ph2 = ExitStack()
    xgrp2 = [sb(f"xgrp{i}", [128, 4096], F32, ph2) for i in range(2)]
    xnb2 = [sb(f"xnb{i}", [128, 4096], BF16, ph2) for i in range(2)]
    stats2 = [sb(f"stats{i}", [128, 24], F32, ph2) for i in range(2)]
    sqjunk = sb("sqjunk", [128, 1024], BF16, ph2)
    xnev = sb("xnev", [128, 1024], BF16, ph2)
    ps_xn = [pst(f"ps_xn{i}", [128, 512], BF16, ph2) for i in range(2)]

    xn_tile_ctr = [0]
    NG = max(1, NTT // 4)
    for g_i in range(NG):
        pi = g_i % 2
        xgrp, xnb, stats = xgrp2[pi], xnb2[pi], stats2[pi]
        with BLOCK() as blk:
            @blk.sync
            def _(sy, g_i=g_i, pi=pi, xgrp=xgrp):
                if g_i == 0:
                    P.wait(sy, "winb_wr")
                    P.wait(sy, "woutb_wr")
                if g_i >= 2:
                    P.wait(sy, f"xnbv{pi}", P.val(f"xnbv{pi}"))
                    P.wait(sy, "xntr0", P.val("xntr0"))
                    P.wait(sy, "xntr1", P.val("xntr1"))
                    P.wait(sy, "amx0", P.val("amx0"))
                for j in range(4):
                    ti = g_i * 4 + j
                    P.wait(sy, f"xld{pi}", P.val(f"xld{pi}"))
                    P.inc(f"xld{pi}", sy.dma_start(
                        out=xgrp[:, j * 1024:(j + 1) * 1024],
                        in_=x_in[ti * 128:(ti + 1) * 128, :]), 16)

            @blk.vector
            def _(v, pi=pi, xgrp=xgrp, stats=stats):
                P.wait(v, f"xld{pi}")
                last = None
                for j in range(4):
                    last = v.tensor_reduce(stats[:, j * 6:j * 6 + 1],
                                           xgrp[:, j * 1024:(j + 1) * 1024],
                                           AX.X, ALU.add)
                P.inc(f"xsv{pi}", last)

            @blk.scalar
            def _(s, pi=pi, xgrp=xgrp, stats=stats):
                P.wait(s, f"xld{pi}")
                P.wait(s, "xss0", P.val("xss0"))
                P.wait(s, "xss1", P.val("xss1"))
                for j in range(4):
                    if j:
                        P.wait(s, f"xss{pi}")
                    P.inc(f"xss{pi}", s.activation(
                        sqjunk[:], xgrp[:, j * 1024:(j + 1) * 1024],
                        AF.Square,
                        accum_out=stats[:, j * 6 + 1:j * 6 + 2]))

        with BLOCK() as blk:
            @blk.vector
            def _(v, pi=pi, stats=stats):
                P.wait(v, f"xsv{pi}")
                P.wait(v, f"xss{pi}")
                last = None
                for j in range(4):
                    last = v.tensor_scalar(stats[:, j * 6 + 2:j * 6 + 3],
                                           stats[:, j * 6:j * 6 + 1],
                                           1.0 / 1024.0, None, ALU.mult)
                P.inc(f"xch{pi}", last)
        vchain(
            f"xch{pi}",
            lambda v, stats=stats: [v.tensor_tensor(
                stats[:, j * 6 + 3:j * 6 + 4],
                stats[:, j * 6 + 2:j * 6 + 3],
                stats[:, j * 6 + 2:j * 6 + 3], ALU.mult)
                for j in range(4)][-1],
            lambda v, stats=stats: [v.scalar_tensor_tensor(
                stats[:, j * 6 + 4:j * 6 + 5],
                stats[:, j * 6 + 1:j * 6 + 2],
                1.0 / 1024.0,
                stats[:, j * 6 + 3:j * 6 + 4],
                ALU.mult, ALU.subtract)
                for j in range(4)][-1],
            lambda v, stats=stats: [v.tensor_scalar(
                stats[:, j * 6 + 4:j * 6 + 5],
                stats[:, j * 6 + 4:j * 6 + 5], EPS,
                None, ALU.add) for j in range(4)][-1],
            lambda v, stats=stats: [v.reciprocal(
                stats[:, j * 6 + 5:j * 6 + 6],
                stats[:, j * 6 + 4:j * 6 + 5])
                for j in range(4)][-1],
        )
        with BLOCK() as blk:
            @blk.scalar
            def _(s, pi=pi, stats=stats):
                P.wait(s, f"xch{pi}")
                last = None
                for j in range(4):
                    last = s.activation(stats[:, j * 6 + 4:j * 6 + 5],
                                        stats[:, j * 6 + 5:j * 6 + 6], AF.Sqrt)
                P.inc(f"xch{pi}", last)

            @blk.vector
            def _(v, pi=pi, g_i=g_i, xgrp=xgrp, xnb=xnb, stats=stats):
                P.wait(v, f"xch{pi}")
                if g_i >= 2:
                    P.wait(v, "xntr0", P.val("xntr0"))
                    P.wait(v, "xntr1", P.val("xntr1"))
                    P.wait(v, "amx0", P.val("amx0"))
                last = None
                for j in range(4):
                    last = v.tensor_scalar(xnb[:, j * 1024:(j + 1) * 1024],
                                           xgrp[:, j * 1024:(j + 1) * 1024],
                                           stats[:, j * 6 + 2:j * 6 + 3],
                                           stats[:, j * 6 + 4:j * 6 + 5],
                                           ALU.subtract, ALU.mult)
                P.inc(f"xnbv{pi}", last)

        for j in range(4):
            ti = g_i * 4 + j
            for half in range(2):
                bank = ps_xn[half]
                with BLOCK() as blk:
                    @blk.tensor
                    def _(t, j=j, half=half, bank=bank, pi=pi, xnb=xnb):
                        P.wait(t, f"xnbv{pi}", P.val(f"xnbv{pi}"))
                        P.wait(t, f"xnevd{half}", P.val(f"xnevd{half}"))
                        last = None
                        for fb4 in range(4):
                            fb = half * 4 + fb4
                            last = t.transpose(
                                bank[:, fb4 * 128:(fb4 + 1) * 128],
                                xnb[:, j * 1024 + fb * 128:
                                    j * 1024 + (fb + 1) * 128],
                                idbf[:])
                        P.inc(f"xntr{half}", last)

                    @blk.vector
                    def _(v, half=half, bank=bank):
                        P.wait(v, f"xntr{half}")
                        P.wait(v, "xnt_wr", P.val("xnt_wr"))
                        P.inc(f"xnevd{half}", v.tensor_copy(
                            xnev[:, half * 512:(half + 1) * 512], bank[:]))

                    @blk.sync
                    def _(sy, ti=ti, half=half):
                        P.wait(sy, f"xnevd{half}")
                        P.wait(sy, "xnt_wr", P.val("xnt_wr"))
                        last = sy.dma_start(
                            out=XNT[half * 4:(half + 1) * 4, ti, :, :]
                            .transpose([1, 0, 2]),
                            in_=xnev[:, half * 512:(half + 1) * 512].rearrange(
                                "p (f q) -> p f q", f=4))
                        P.inc("xnt_wr", last, 16)

        with BLOCK() as blk:
            @blk.vector
            def _(v, pi=pi, xnb=xnb):
                P.wait(v, f"xnbv{pi}", P.val(f"xnbv{pi}"))
                for a in range(8):
                    accmax(v, "xn", xn_tile_ctr[0],
                           xnb[:, a * 512:(a + 1) * 512])
                    xn_tile_ctr[0] += 1

    xhb = sb("xhb", [3, 1024], F32, ph2)
    xhn = sb("xhn", [3, 1024], BF16, ph2)
    hstt = sb("hstt", [3, 8], F32, ph2)
    xhT = sb("xhT", [128, 32], BF16, ph2)
    with BLOCK() as blk:
        @blk.sync
        def _(sy):
            P.wait(sy, "hld", P.val("hld"))
            P.inc("hld", sy.dma_start(out=xhb[:], in_=xh_in), 16)

        @blk.vector
        def _(v):
            P.wait(v, "hld")
            P.inc("hcv", v.tensor_reduce(hstt[:, 0:1], xhb[:], AX.X, ALU.add))

        @blk.scalar
        def _(s):
            P.wait(s, "hld")
            P.inc("hcs", s.activation(sqjunk[0:3, :], xhb[:], AF.Square,
                                      accum_out=hstt[:, 1:2]))

    with BLOCK() as blk:
        @blk.vector
        def _(v):
            P.wait(v, "hcv")
            P.wait(v, "hcs")
            P.inc("hch", v.tensor_scalar(hstt[:, 2:3], hstt[:, 0:1],
                                         1.0 / 1024.0, None, ALU.mult))
    vchain(
        "hch",
        lambda v: v.tensor_tensor(hstt[:, 3:4], hstt[:, 2:3], hstt[:, 2:3],
                                  ALU.mult),
        lambda v: v.scalar_tensor_tensor(hstt[:, 4:5], hstt[:, 1:2],
                                         1.0 / 1024.0, hstt[:, 3:4],
                                         ALU.mult, ALU.subtract),
        lambda v: v.tensor_scalar(hstt[:, 4:5], hstt[:, 4:5], EPS, None,
                                  ALU.add),
        lambda v: v.reciprocal(hstt[:, 5:6], hstt[:, 4:5]),
    )
    with BLOCK() as blk:
        @blk.scalar
        def _(s):
            P.wait(s, "hch")
            P.inc("hch2", s.activation(hstt[:, 4:5], hstt[:, 5:6], AF.Sqrt))

        @blk.vector
        def _(v):
            P.wait(v, "hch2")
            P.inc("hnv", v.tensor_scalar(xhn[:], xhb[:], hstt[:, 2:3],
                                         hstt[:, 4:5], ALU.subtract, ALU.mult))

    with BLOCK() as blk:
        @blk.vector
        def _(v):
            P.inc("hev", v.memset(xhT[:], 0.0))
    for fb in range(8):
        with BLOCK() as blk:
            @blk.tensor
            def _(t, fb=fb):
                P.wait(t, "hnv")
                P.wait(t, "hev", P.val("hev"))
                P.inc("htr", t.transpose(ps_xn[0][:, 0:3],
                                         xhn[:, fb * 128:(fb + 1) * 128],
                                         idbf[0:3, 0:3]))

            @blk.vector
            def _(v, fb=fb):
                P.wait(v, "htr")
                P.inc("hev", v.tensor_copy(xhT[:, fb * 4:fb * 4 + 3],
                                           ps_xn[0][:, 0:3]))
    with BLOCK() as blk:
        @blk.sync
        def _(sy):
            P.wait(sy, "hev")
            P.wait(sy, "xnh_wr", P.val("xnh_wr"))
            P.inc("xnh_wr", sy.dma_start(
                out=XNH.transpose([1, 0, 2]),
                in_=xhT[:].rearrange("p (f q) -> p f q", f=8)), 16)

    finalize_max("xn", 0, ["xng_done"], xn_tile_ctr[0])
    with BLOCK() as blk:
        @blk.sync
        def _(sy):
            P.wait(sy, "fm_xn_r")
            P.wait(sy, "cc0_wr", P.val("cc0_wr"))
            P.inc("cc0_wr", sy.dma_start(out=CC_I[0][0:1, 0:1],
                                         in_=mrow[0:1, 0:1]), 16)

        @blk.gpsimd
        def _(g):
            P.wait(g, "cc0_wr")
            P.inc("cc0", g.collective_compute(
                "AllReduce", ALU.max, replica_groups=[list(range(NCORES))],
                ins=[CC_I[0].opt()], outs=[CC_O[0].opt()]))

    def load_cc(idx, specs, ccsem):
        with BLOCK() as blk:
            @blk.sync
            def _(sy):
                P.wait(sy, ccsem)
                P.wait(sy, "scmm", P.val("scmm"))
                P.wait(sy, f"ccl{idx}", P.val(f"ccl{idx}"))
                P.inc(f"ccl{idx}", sy.dma_start(
                    out=ccs[0:1, 8:8 + CC_O[idx].shape[1]],
                    in_=CC_O[idx]), 16)
        for i, (srcc, scs, scr, fac) in enumerate(specs):
            with BLOCK() as blk:
                @blk.vector
                def _(v, i=i, srcc=srcc, fac=fac):
                    P.wait(v, f"ccl{idx}")
                    P.wait(v, f"ccv{idx}", P.val(f"ccv{idx}"))
                    P.inc(f"ccv{idx}", v.tensor_scalar(
                        ccs[0:1, 2 * i:2 * i + 1], ccs[0:1, 8 + srcc:9 + srcc],
                        fac / QMAX, 1e-12, ALU.mult, ALU.max))
            with BLOCK() as blk:
                @blk.vector
                def _(v, i=i):
                    P.wait(v, f"ccv{idx}")
                    P.inc(f"ccv{idx}", v.reciprocal(
                        ccs[0:1, 2 * i + 1:2 * i + 2],
                        ccs[0:1, 2 * i:2 * i + 1]))
            bcast_scal([(2 * i, scs), (2 * i + 1, scr)], [f"ccv{idx}"])

    load_cc(0, [(0, SC_S0, SC_R0, 1.0)], "cc0")

    ph2.close()

    ph4 = ExitStack()
    xq_sb = sb("xq", [128, 8 * T], BF16, ph4)
    xcv = sb("xcv", [128, 17 * 520], BF16, ph4)
    xnst = xcv
    wtb = [sb(f"wtb{i}", [128, 1024], BF16, ph4) for i in range(2)]
    sp_t = [sb(f"sp_t{i}", [128, 512], BF16, ph4) for i in range(2)]
    ps_ug = [pst(f"ps_ug{i}", [128, 512], F32, ph4) for i in range(2)]

    for fb in range(8):
        with BLOCK() as blk:
            @blk.sync
            def _(sy, fb=fb):
                if fb >= 1:
                    P.wait(sy, "xqv", fb)
                P.wait(sy, "xnt_wr")
                P.wait(sy, "xnl", P.val("xnl"))
                P.inc("xnl", sy.dma_start(
                    out=xnst[:, 0:T].rearrange("p (n q) -> p n q", q=128),
                    in_=XNT[fb, :, :, :].transpose([1, 0, 2])), 16)

            @blk.vector
            def _(v, fb=fb):
                P.wait(v, "xnl")
                P.wait(v, "scbc")
                P.inc("xqv", v._custom_dve(
                    QROUND, out=xq_sb[:, fb * T:(fb + 1) * T],
                    in0=xnst[:, 0:T],
                    s0=scal[:, SC_R0:SC_R0 + 1], s1=scal[:, SC_S0:SC_S0 + 1],
                    imm2=MAGIC))

    ugctr = {"up": [0], "ga": [0]}
    UGIT = [0]

    def ug_m(gidx):
        wi = gidx // 24
        kind, SP, base = (("up", UPRAW, AGW_UP),
                          ("ga", GRAW, AGW_GATE))[wi]
        m = gidx % 24
        if True:
            rank, m3 = m // 3, m % 3
            bi = m % 2
            with BLOCK() as blk:
                @blk.sync
                def _(sy, rank=rank, m3=m3, bi=bi, base=base, wi=wi, m=m):
                    P.wait(sy, f"wtbf{bi}", P.val(f"wtbf{bi}"))
                    if wi == 0 and m == 0:
                        P.wait(sy, "ccagw")
                    off = base + m3 * 8 * TIL
                    P.wait(sy, f"wtb{bi}", P.val(f"wtb{bi}"))
                    P.inc(f"wtb{bi}", sy.dma_start(
                        out=wtb[bi][:].rearrange("p (k q) -> p k q", k=8),
                        in_=AGW_O[rank, off:off + 8 * TIL].rearrange(
                            "(k p q) -> k p q", k=8, p=128).transpose(
                                [1, 0, 2])), 16)
            for n in range(NCH):
                slot = UGIT[0] % 2
                pslot = UGIT[0] % 2
                with BLOCK() as blk:
                    @blk.tensor
                    def _(t, bi=bi, slot=slot, pslot=pslot, n=n, m=m):
                        P.wait(t, f"wtb{bi}")
                        P.wait(t, "xqv", 8)
                        P.wait(t, "ugsp0", max(0, P.val("ugsp0") - 1))
                        P.wait(t, "xbev", P.val("xbev"))
                        last = None
                        for kbi in range(8):
                            last = t.matmul(
                                ps_ug[pslot][:],
                                wtb[bi][:, kbi * 128:(kbi + 1) * 128],
                                xq_sb[:, kbi * T + n * CH:kbi * T + (n + 1) * CH],
                                start=(kbi == 0), stop=(kbi == 7))
                        P.inc("ugmm0", last)
                        if n == NCH - 1:
                            P.inc(f"wtbf{bi}", last)

                    @blk.scalar
                    def _(s, slot=slot, pslot=pslot, kind=kind):
                        P.wait(s, "ugmm0")
                        P.wait(s, "ugwr", max(0, P.val("ugwr") - 16))
                        P.wait(s, "amx0", max(0, P.val("amx0") - 1))
                        P.inc("ugsp0", s.copy(sp_t[slot][:], ps_ug[pslot][:]))

                    @blk.sync
                    def _(sy, m=m, n=n, slot=slot, SP=SP):
                        P.wait(sy, "ugsp0")
                        P.wait(sy, "ugwr", P.val("ugwr"))
                        P.inc("ugwr", sy.dma_start(out=SP[m, n, :, :],
                                                   in_=sp_t[slot][:]), 16)

                    @blk.vector
                    def _(v, slot=slot, kind=kind):
                        P.wait(v, "ugsp0")
                        accmax(v, kind, ugctr[kind][0], sp_t[slot][:])
                        ugctr[kind][0] += 1
                UGIT[0] += 1

    def emit_m1m2():
        finalize_max("up", 1, [], ugctr["up"][0])
        finalize_max("ga", 2, [], ugctr["ga"][0])
        with BLOCK() as blk:
            @blk.sync
            def _(sy):
                P.wait(sy, "fm_up_r")
                P.wait(sy, "fm_ga_r")
                P.wait(sy, "cc1_wr", P.val("cc1_wr"))
                P.inc("cc1_wr", sy.dma_start(out=CC_I[1][0:1, 0:2],
                                             in_=mrow[0:1, 1:3]), 16)

            @blk.gpsimd
            def _(g):
                P.wait(g, "cc1_wr")
                P.inc("cc1", g.collective_compute(
                    "AllReduce", ALU.max,
                    replica_groups=[list(range(NCORES))],
                    ins=[CC_I[1].opt()], outs=[CC_O[1].opt()]))

        load_cc(1, [(0, SC_S1, SC_R1, 1.0), (1, SC_S2, SC_R2, 1.0)], "cc1")

    xnc = sb("xnc", [128, 8 * CH], BF16, ph4)
    xnh_sb = sb("xnh_sb", [128, 32], BF16, ph4)
    wzb = [sb(f"wzb{i}", [128, 4 * 512], BF16, ph4) for i in range(3)]
    wxb = [sb(f"wxb{i}", [128, 8 * 128], BF16, ph4) for i in range(2)]
    cacc = sb("cacc", [128, 4 * CH], F32, ph4)
    xcs = sb("xcs", [128, 17 * CH], BF16, ph4)
    zsb = sb("zsb", [128, 1024], BF16, ph4)
    dtc = sb("dtc", [32, CH], F32, ph4)
    sdt = sb("sdt", [32, CH], F32, ph4)
    slc = sb("slc", [32, CH], F32, ph4)
    sgoff = sb("sgoff", [32, 2], F32, ph4)
    eslc = sb("eslc", [32, CH], F32, ph4)
    ones32 = sb("ones32", [32, 128], F32, ph4)
    sro = sb("sro", [1, 4096], F32, ph4)
    dtT = sb("dtT", [128, 32], F32, ph4)
    eslT = sb("eslT", [128, 32], F32, ph4)
    egl = sb("egl", [128, 32], BF16, ph4)
    darow = sb("darow", [1, 32], F32, ph4)
    dat = sb("dat", [64, 32], F32, ph4)
    e4b2 = [sb(f"e4b{i}", [128, 512], BF16, ph4) for i in range(2)]
    g4b2 = [sb(f"g4b{i}", [128, 512], BF16, ph4) for i in range(2)]
    bcs = sb("bcs", [128, 512], BF16, ph4)
    xtok = sb("xtok", [128, D_INNER], BF16, ph4)
    xsc = sb("xsc", [128, D_INNER], BF16, ph4)
    btok = sb("btok", [128, 64], BF16, ph4)
    csb = sb("csb", [64, CH], BF16, ph4)
    bsc = sb("bsc", [128, D_INNER], BF16, ph4)
    hcur = sb("hcur0", [64, D_INNER], F32, ph4)
    hbf = sb("hbf0", [64, D_INNER], BF16, ph4)
    ylsb = sb("ylsb", [128, D_INNER], BF16, ph4)
    yint = sb("yint", [128, 512], F32, ph4)
    ps_zz = pst("ps_zz", [128, 2048], F32, ph4)

    with BLOCK() as blk:
        @blk.vector
        def _(v):
            P.wait(v, "xnh_wr")
            P.wait(v, "xqv", 8)
            v.memset(hcur[:], 0.0)
            v.memset(hbf[:], 0.0)
            v.memset(sgoff[:], 0.0)
            v.memset(ones32[:], 1.0)
            P.inc("mb_init", v.memset(xcv[:], 0.0))

        @blk.sync
        def _(sy):
            P.wait(sy, "xnh_wr")
            P.wait(sy, "xnh_ld", P.val("xnh_ld"))
            P.inc("xnh_ld", sy.dma_start(
                out=xnh_sb[:].rearrange("p (f q) -> p f q", f=8),
                in_=XNH.transpose([1, 0, 2])), 16)

    UGPC = (48 + NCH - 1) // NCH
    for n in range(NCH):
        for gidx in range(n * UGPC, min(48, (n + 1) * UGPC)):
            ug_m(gidx)
        if n == NCH - 1:
            emit_m1m2()
        with BLOCK() as blk:
            @blk.sync
            def _(sy, n=n):
                P.wait(sy, "xncf", P.val("xncf"))
                P.wait(sy, "xnt_wr")
                for f in range(8):
                    P.wait(sy, "xncl", P.val("xncl"))
                    P.inc("xncl", sy.dma_start(
                        out=xnc[:, f * 512:(f + 1) * 512].rearrange(
                            "p (j q) -> p j q", j=4),
                        in_=XNT[f, n * 4:(n + 1) * 4, :, :]
                        .transpose([1, 0, 2])), 16)

        for zb in range(4):
            zs0 = (n * 8 + zb * 2) % 3
            zs1 = (n * 8 + zb * 2 + 1) % 3
            with BLOCK() as blk:
                @blk.sync
                def _(sy, zb=zb, n=n, zs0=zs0, zs1=zs1):
                    P.wait(sy, "wzbf0", max(0, P.val("wzbf0") - 1))
                    P.wait(sy, "winb_wr")
                    for hf, zs_ in ((0, zs0), (1, zs1)):
                        off = ZT0 + (zb * 8 + hf * 4) * (128 * 512)
                        P.wait(sy, f"wzbl{hf}", P.val(f"wzbl{hf}"))
                        P.inc(f"wzbl{hf}", sy.dma_start(
                            out=wzb[zs_][:].rearrange("p (k q) -> p k q", k=4),
                            in_=WINB[0:1, off:off + 4 * 128 * 512].rearrange(
                                "o (k p q) -> k (o p) q", k=4, p=128)
                            .transpose([1, 0, 2])), 16)

                @blk.tensor
                def _(t, zb=zb, n=n, zs0=zs0, zs1=zs1):
                    P.wait(t, "wzbl0")
                    P.wait(t, "wzbl1")
                    P.wait(t, "xncl")
                    P.wait(t, "zpe", P.val("zpe"))
                    P.wait(t, "ype", P.val("ype"))
                    P.wait(t, "dpe", P.val("dpe"))
                    P.wait(t, "bcev", P.val("bcev"))
                    last = None
                    for tt in range(4):
                        for kbi in range(8):
                            last = t.matmul(
                                ps_zz[:, tt * 512:(tt + 1) * 512],
                                xnc[:, kbi * CH + tt * 128:
                                    kbi * CH + (tt + 1) * 128],
                                wzb[(zs0, zs1)[kbi // 4]][:, (kbi % 4) * 512:
                                                          (kbi % 4 + 1) * 512],
                                start=(kbi == 0), stop=(kbi == 7))
                    P.inc("zmm", last)
                    P.inc("wzbf0", last)

                @blk.scalar
                def _(s, zb=zb, n=n, zwbase=None):
                    P.wait(s, "zmm")
                    zwbase = P.val("zswr")
                    for tt in range(4):
                        P.wait(s, "zswr",
                               zwbase if tt < 2 else zwbase + 16 * (tt - 1))
                        P.inc("zev", s.activation(
                            zsb[:, (tt % 2) * 512:(tt % 2 + 1) * 512],
                            ps_zz[:, tt * 512:(tt + 1) * 512],
                            AF.Silu))

                @blk.sync
                def _(sy, zb=zb, n=n):
                    for tt in range(4):
                        P.wait(sy, "zev", P.val("zev") - 3 + tt)
                        P.wait(sy, "zswr", P.val("zswr"))
                        P.inc("zswr", sy.dma_start(
                            out=ZS[n * 4 + tt, zb, :, :],
                            in_=zsb[:, (tt % 2) * 512:(tt % 2 + 1) * 512]), 16)

        for ob in range(18):
            obi = ob % 2
            is_dt = (ob == 17)
            with BLOCK() as blk:
                @blk.sync
                def _(sy, ob=ob, obi=obi, is_dt=is_dt):
                    P.wait(sy, f"wxbf{obi}", P.val(f"wxbf{obi}"))
                    P.wait(sy, "winb_wr")
                    if is_dt:
                        off = DT0
                        sz = 32
                    else:
                        off = XT0 + ob * 8 * TIL
                        sz = 128
                    P.wait(sy, f"wxbl{obi}", P.val(f"wxbl{obi}"))
                    P.inc(f"wxbl{obi}", sy.dma_start(
                        out=wxb[obi][:, 0:8 * sz].rearrange(
                            "p (k q) -> p k q", k=8),
                        in_=WINB[0:1, off:off + 8 * 128 * sz].rearrange(
                            "o (k p q) -> k (o p) q", k=8, p=128).transpose(
                                [1, 0, 2])), 16)

                @blk.tensor
                def _(t, ob=ob, obi=obi, is_dt=is_dt, n=n):
                    P.wait(t, f"wxbl{obi}")
                    P.wait(t, "xncl")
                    P.wait(t, "xbpe", P.val("xbpe"))
                    P.wait(t, "ugsp0", P.val("ugsp0"))
                    sz = 32 if is_dt else 128
                    last = None
                    for kbi in range(8):
                        last = t.matmul(
                            ps_ug[obi][0:sz, :],
                            wxb[obi][:, kbi * sz:(kbi + 1) * sz],
                            xnc[:, kbi * CH:(kbi + 1) * CH],
                            start=(kbi == 0), stop=(kbi == 7))
                    P.inc("xbmm", last)
                    P.inc(f"wxbf{obi}", last)
                    if is_dt:
                        P.inc("xncf", last)
                    if n == 0 and not is_dt:
                        P.wait(t, "halope", P.val("halope"))
                        last2 = None
                        for kbi in range(8):
                            last2 = t.matmul(
                                ps_zz[:, 2044:2047],
                                wxb[obi][:, kbi * 128:(kbi + 1) * 128],
                                xnh_sb[:, kbi * 4:kbi * 4 + 3],
                                start=(kbi == 0), stop=(kbi == 7))
                        P.inc("halomm", last2)

                @blk.vector
                def _(v, ob=ob, obi=obi, is_dt=is_dt, n=n):
                    P.wait(v, "xbmm")
                    P.wait(v, "mb_init")
                    if is_dt:
                        P.wait(v, "sg_wr", P.val("sg_wr"))
                        last = v.tensor_copy(sdt[:, 0:CH], ps_ug[obi][0:32, :])
                        P.inc("dtraw", last)
                    else:
                        last = v.tensor_copy(
                            xcv[:, ob * 520 + 3:ob * 520 + 3 + CH],
                            ps_ug[obi][:])
                        P.inc("xbev", last)
                    P.inc("xbpe", last)
                    if n == 0 and not is_dt:
                        P.wait(v, "halomm")
                        last2 = v.tensor_copy(
                            xcv[:, ob * 520:ob * 520 + 3],
                            ps_zz[:, 2044:2047])
                        P.inc("haloev", last2)
                        P.inc("halope", last2)

        with BLOCK() as blk:
            @blk.scalar
            def _(s, n=n):
                P.wait(s, "dtraw")
                s.activation(dtc[:], sdt[:], AF.Abs, bias=dtbias[:, 0:1])
                P.inc("dts1", s.activation(slc[:], sdt[:], AF.Relu,
                                           bias=dtbias[:, 0:1]))
        with BLOCK() as blk:
            @blk.scalar
            def _(s):
                P.wait(s, "dts1")
                P.inc("dts2", s.activation(dtc[:], dtc[:], AF.Exp, scale=-1.0))
        with BLOCK() as blk:
            @blk.scalar
            def _(s):
                P.wait(s, "dts2")
                P.inc("dts3", s.activation(dtc[:], dtc[:], AF.Ln, bias=1.0))
        with BLOCK() as blk:
            @blk.vector
            def _(v):
                P.wait(v, "dts3")
                P.inc("dtv1", v.tensor_tensor(dtc[:], dtc[:], slc[:], ALU.add))
        with BLOCK() as blk:
            @blk.vector
            def _(v):
                P.wait(v, "dtv1")
                P.inc("dtv2", v.tensor_scalar(sdt[:], dtc[:], aneg[:, 0:1],
                                              None, ALU.mult))
        with BLOCK() as blk:
            @blk.vector
            def _(v):
                P.wait(v, "dtv2")
                P.wait(v, "sg_wr", P.val("sg_wr"))
                last = None
                for q in range(4):
                    last = v.tensor_tensor_scan(
                        slc[:, q * 128:(q + 1) * 128],
                        ones32[:, 0:128],
                        sdt[:, q * 128:(q + 1) * 128], 0.0,
                        ALU.mult, ALU.add)
                P.inc("dtv3", last)
        with BLOCK() as blk:
            @blk.vector
            def _(v):
                P.wait(v, "dtv3")
                P.wait(v, "dtv4")
                last = None
                for q in range(4):
                    P.inc("dtv4", v.tensor_scalar(
                        sdt[:, q * 128:(q + 1) * 128],
                        slc[:, q * 128:(q + 1) * 128],
                        sgoff[:, 0:1], None, ALU.add))
                    P.wait(v, "dtv4")
                    P.inc("dtv4", v.tensor_copy(
                        sgoff[:, 0:1],
                        sdt[:, q * 128 + 127:q * 128 + 128]))
                    P.wait(v, "dtv4")
        with BLOCK() as blk:
            @blk.scalar
            def _(s):
                P.wait(s, "dtv4")
                P.inc("esl_s", s.activation(eslc[:], slc[:], AF.Exp))

            @blk.sync
            def _(sy, n=n):
                P.wait(sy, "dtv4")
                P.wait(sy, "sg_wr", P.val("sg_wr"))
                P.inc("sg_wr", sy.dma_start(out=SGH[:, n * CH:(n + 1) * CH],
                                            in_=sdt[:]), 16)
                P.wait(sy, "sg_wr", P.val("sg_wr"))
                P.inc("sg_wr", sy.dma_start(out=SLOC[:, n * CH:(n + 1) * CH],
                                            in_=slc[:]), 16)


        for grp, (p0, np_) in enumerate(
                ((16, 1), (0, 4), (4, 4), (8, 4), (12, 4))):
            csem = f"cvt{grp}"
            with BLOCK() as blk:
                @blk.vector
                def _(v, p0=p0, np_=np_, csem=csem, n=n):
                    P.wait(v, "xbev", P.val("xbev"))
                    if n == 0:
                        P.wait(v, "haloev", P.val("haloev"))
                    P.wait(v, csem, P.val(csem))
                    last = None
                    for pb2 in range(np_):
                        ob = p0 + pb2
                        last = v.tensor_scalar(
                            cacc[:, pb2 * CH:(pb2 + 1) * CH],
                            xcv[:, ob * 520 + 0:ob * 520 + CH],
                            convw[:, ob * 4:ob * 4 + 1], None, ALU.mult)
                    P.inc(csem, last)
            for k in range(1, 4):
                with BLOCK() as blk:
                    @blk.vector
                    def _(v, p0=p0, np_=np_, k=k, csem=csem):
                        P.wait(v, csem)
                        last = None
                        for pb2 in range(np_):
                            ob = p0 + pb2
                            last = v.scalar_tensor_tensor(
                                cacc[:, pb2 * CH:(pb2 + 1) * CH],
                                xcv[:, ob * 520 + k:ob * 520 + k + CH],
                                convw[:, ob * 4 + k:ob * 4 + k + 1],
                                cacc[:, pb2 * CH:(pb2 + 1) * CH],
                                ALU.mult, ALU.add)
                        P.inc(csem, last)
            with BLOCK() as blk:
                @blk.scalar
                def _(s, p0=p0, np_=np_, csem=csem):
                    P.wait(s, csem)
                    last = None
                    for pb2 in range(np_):
                        ob = p0 + pb2
                        last = s.activation(
                            xcs[:, ob * CH:(ob + 1) * CH],
                            cacc[:, pb2 * CH:(pb2 + 1) * CH],
                            AF.Silu, bias=convb[:, ob:ob + 1])
                    P.inc(csem, last)
                    P.inc("xcsev", last)

        with BLOCK() as blk:
            @blk.sync
            def _(sy, n=n):
                P.wait(sy, "xcsev")
                P.wait(sy, "csp_wr", P.val("csp_wr"))
                P.inc("csp_wr", sy.dma_start(
                    out=CSP[:, n * CH:(n + 1) * CH],
                    in_=xcs[64:128, 16 * CH:17 * CH]), 16)
                P.wait(sy, "bcev", P.val("bcev"))
                P.wait(sy, "ymm", P.val("ymm"))
                P.wait(sy, "cloc", P.val("cloc"))
                P.inc("cloc", sy.dma_start(
                    out=csb[:], in_=xcs[64:128, 16 * CH:17 * CH]), 16)

            @blk.vector
            def _(v, n=n):
                P.wait(v, "xcsev")
                P.inc("tailcp", v.tensor_copy(
                    xcv[:].rearrange("p (o c) -> p o c", c=520)[:, :, 0:3],
                    xcv[:].rearrange("p (o c) -> p o c", c=520)[:, :,
                                                               CH:CH + 3]))
        with BLOCK() as blk:
            @blk.vector
            def _(v):
                P.wait(v, "tailcp")
                P.inc("xbpe", v.tensor_copy(sgoff[:, 1:2], sgoff[:, 0:1]))

        with BLOCK() as blk:
            @blk.tensor
            def _(t, n=n):
                P.wait(t, "xcsev", P.val("xcsev"))
                P.wait(t, "bcse", P.val("bcse"))
                P.wait(t, "xtev", P.val("xtev"))
                P.wait(t, "btev", P.val("btev"))
                P.wait(t, "cloc")
                P.wait(t, "datv", P.val("datv"))
                last = None
                for q in range(4):
                    last = t.matmul(
                        ps_zz[:, 1536 + q * 128:1536 + (q + 1) * 128],
                        xcs[0:64, 16 * CH + q * 128:16 * CH + (q + 1) * 128],
                        csb[:, q * 128:(q + 1) * 128],
                        start=(q == 0), stop=(q == 3))
                P.inc("bcmm", last)

            @blk.vector
            def _(v):
                P.wait(v, "bcmm")
                last = v.tensor_copy(bcs[:], ps_zz[:, 1536:2048])
                P.inc("bcev", last)
                P.inc("bcse", last)

        for q in range(4):
            _ssd_q(locals(), n, q)

    with BLOCK() as blk:
        @blk.sync
        def _(sy):
            P.wait(sy, "hupd", P.val("hupd"))
            P.wait(sy, "hpub_wr", P.val("hpub_wr"))
            P.inc("hpub_wr", sy.dma_start(
                out=HPUB[0:1, :].rearrange("o (p q) -> p (o q)", p=64),
                in_=hcur[:]), 16)

        @blk.gpsimd
        def _(g):
            P.wait(g, "hpub_wr")
            P.inc("ccah", g.collective_compute(
                "AllGather", ALU.bypass, replica_groups=[list(range(NCORES))],
                ins=[HPUB.opt()], outs=[HALL.opt()]))

    ph4.close()

    fixup_tile, outproj, ph7 = _build_fixup(locals())

    ph5 = ExitStack()
    c1g = [sb(f"c1g{i}", [128, 2048], BF16, ph5) for i in range(2)]
    c1q = sb("c1q", [128, 2048], F32, ph5)
    c1s = sb("c1s", [128, 2048], F32, ph5)
    c1t = [sb(f"c1t{i}", [128, 2048], BF16, ph5) for i in range(2)]

    t3ctr = [0]
    FIX_CTR = [0]
    gsz = min(4, NCH)
    GW = gsz * 512
    strips = [(m_, ng_) for m_ in range(24) for ng_ in range(0, NCH, gsz)]
    for st_i, (m0_, n0_) in enumerate(strips):
        bi = st_i % 2
        with BLOCK() as blk:
            @blk.sync
            def _(sy, bi=bi, m0_=m0_, n0_=n0_):
                P.wait(sy, "c1qv", P.val("c1qv"))
                P.wait(sy, "ugwr")
                P.wait(sy, f"c1gl{bi}", P.val(f"c1gl{bi}"))
                P.inc(f"c1gl{bi}", sy.dma_start(
                    out=c1g[bi][:, 0:GW].rearrange("p (j q) -> p j q", j=gsz),
                    in_=GRAW[m0_, n0_:n0_ + gsz, :, :].transpose([1, 0, 2])),
                    16)
        with BLOCK() as blk:
            @blk.vector
            def _(v, bi=bi):
                P.wait(v, f"c1gl{bi}")
                P.wait(v, "scbc")
                P.wait(v, "c1qf", P.val("c1qf"))
                P.inc("c1qv", v._custom_dve(
                    QROUND, out=c1q[:, 0:GW], in0=c1g[bi][:, 0:GW],
                    s0=scal[:, SC_R2:SC_R2 + 1], s1=1.0, imm2=MAGIC))

            @blk.scalar
            def _(s):
                P.wait(s, "c1qv")
                P.inc("c1sig", s.activation(c1s[:, 0:GW], c1q[:, 0:GW],
                                            AF.Sigmoid,
                                            scale=scal[:, SC_S2:SC_S2 + 1]))
        with BLOCK() as blk:
            @blk.vector
            def _(v, bi=bi):
                P.wait(v, "c1sig")
                P.wait(v, "t3wr", P.val("t3wr"))
                for sl in range(NSLOT):
                    P.wait(v, f"amx{sl}", P.val(f"amx{sl}"))
                last = v._custom_dve(
                    QROUND2, out=c1t[bi][:, 0:GW], in0=c1q[:, 0:GW],
                    in1=c1s[:, 0:GW], s0=1.0, s1=65536.0, imm2=MAGIC)
                P.inc("c1tv", last)
                P.inc("c1qf", last)

            @blk.sync
            def _(sy, bi=bi, m0_=m0_, n0_=n0_):
                P.wait(sy, "c1tv")
                P.wait(sy, "t3wr", P.val("t3wr"))
                P.inc("t3wr", sy.dma_start(
                    out=T3S[m0_, n0_:n0_ + gsz, :, :].transpose([1, 0, 2]),
                    in_=c1t[bi][:, 0:GW].rearrange("p (j q) -> p j q", j=gsz)),
                    16)

            @blk.vector
            def _(v, bi=bi):
                P.wait(v, "c1tv")
                for j in range(gsz):
                    accmax(v, "t3", t3ctr[0],
                           c1t[bi][:, j * 512:(j + 1) * 512])
                    t3ctr[0] += 1
        if st_i < len(strips) - 8 and st_i % 5 == 0:
            tt_ = FIX_CTR[0]
            if tt_ < NTT:
                for _k in range(4):
                    fixup_tile(tt_ + _k)
                outproj(tt_ // 4)
                FIX_CTR[0] = tt_ + 4

    finalize_max("t3", 3, [], t3ctr[0])
    with BLOCK() as blk:
        @blk.sync
        def _(sy):
            P.wait(sy, "fm_t3_r")
            P.wait(sy, "cc2_wr", P.val("cc2_wr"))
            P.inc("cc2_wr", sy.dma_start(out=CC_I[2][0:1, 0:1],
                                         in_=mrow[0:1, 3:4]), 16)

        @blk.gpsimd
        def _(g):
            P.wait(g, "cc2_wr")
            P.inc("cc2", g.collective_compute(
                "AllReduce", ALU.max, replica_groups=[list(range(NCORES))],
                ins=[CC_I[2].opt()], outs=[CC_O[2].opt()]))

    with BLOCK() as blk:
        @blk.sync
        def _(sy):
            P.wait(sy, "cc2")
            P.wait(sy, "scmm", P.val("scmm"))
            P.wait(sy, "ccl2", P.val("ccl2"))
            P.inc("ccl2", sy.dma_start(out=ccs[0:1, 8:9], in_=CC_O[2]), 16)
    with BLOCK() as blk:
        @blk.vector
        def _(v):
            P.wait(v, "ccl2")
            P.inc("c2v", v.tensor_scalar(ccs[0:1, 10:11], ccs[0:1, 2:3],
                                         1.0 / 65536.0, None, ALU.mult))
    with BLOCK() as blk:
        @blk.vector
        def _(v):
            P.wait(v, "c2v")
            P.inc("c2v", v.tensor_tensor(ccs[0:1, 11:12], ccs[0:1, 8:9],
                                         ccs[0:1, 10:11], ALU.mult))
    with BLOCK() as blk:
        @blk.vector
        def _(v):
            P.wait(v, "c2v")
            P.inc("c2v", v.tensor_scalar(ccs[0:1, 4:5], ccs[0:1, 11:12],
                                         1.0 / QMAX, 1e-12, ALU.mult, ALU.max))
    with BLOCK() as blk:
        @blk.vector
        def _(v):
            P.wait(v, "c2v")
            P.inc("c2v", v.reciprocal(ccs[0:1, 5:6], ccs[0:1, 4:5]))
    with BLOCK() as blk:
        @blk.vector
        def _(v):
            P.wait(v, "c2v")
            P.inc("c2v", v.tensor_tensor(ccs[0:1, 6:7], ccs[0:1, 10:11],
                                         ccs[0:1, 5:6], ALU.mult))
    bcast_scal([(6, SC_F3R3), (4, SC_S3S1R4X)], ["c2v"])
    ph5.close()

    ph6 = ExitStack()
    c2a = [sb(f"c2a{i}", [128, 2048], BF16, ph6) for i in range(2)]
    c2b = [sb(f"c2b{i}", [128, 2048], BF16, ph6) for i in range(2)]
    c2t = [sb(f"c2t{i}", [128, 2048], BF16, ph6) for i in range(2)]
    t4ctr = [0]
    for st_i, (m0_, n0_) in enumerate(strips):
        bi = st_i % 2
        with BLOCK() as blk:
            @blk.sync
            def _(sy, bi=bi, m0_=m0_, n0_=n0_):
                P.wait(sy, "c2tv", P.val("c2tv"))
                P.wait(sy, "t3wr")
                P.wait(sy, f"c2al{bi}", P.val(f"c2al{bi}"))
                P.inc(f"c2al{bi}", sy.dma_start(
                    out=c2a[bi][:, 0:GW].rearrange("p (j q) -> p j q", j=gsz),
                    in_=T3S[m0_, n0_:n0_ + gsz, :, :].transpose([1, 0, 2])), 16)
                P.wait(sy, f"c2al{bi}", P.val(f"c2al{bi}"))
                P.inc(f"c2al{bi}", sy.dma_start(
                    out=c2b[bi][:, 0:GW].rearrange("p (j q) -> p j q", j=gsz),
                    in_=UPRAW[m0_, n0_:n0_ + gsz, :, :].transpose([1, 0, 2])),
                    16)
        with BLOCK() as blk:
            @blk.vector
            def _(v, bi=bi):
                P.wait(v, f"c2al{bi}")
                P.wait(v, "scbc")
                P.wait(v, "t4wr", P.val("t4wr"))
                for sl in range(NSLOT):
                    P.wait(v, f"amx{sl}", P.val(f"amx{sl}"))
                last = v._custom_dve(
                    QROUND2, out=c2t[bi][:, 0:GW], in0=c2a[bi][:, 0:GW],
                    in1=c2b[bi][:, 0:GW],
                    s0=scal[:, SC_F3R3:SC_F3R3 + 1],
                    s1=scal[:, SC_R1:SC_R1 + 1], imm2=MAGIC)
                P.inc("c2tv", last)

            @blk.sync
            def _(sy, bi=bi, m0_=m0_, n0_=n0_):
                P.wait(sy, "c2tv")
                P.wait(sy, "t4wr", P.val("t4wr"))
                P.inc("t4wr", sy.dma_start(
                    out=T4S[m0_, n0_:n0_ + gsz, :, :].transpose([1, 0, 2]),
                    in_=c2t[bi][:, 0:GW].rearrange("p (j q) -> p j q", j=gsz)),
                    16)

            @blk.vector
            def _(v, bi=bi):
                P.wait(v, "c2tv")
                for j in range(gsz):
                    accmax(v, "t4", t4ctr[0],
                           c2t[bi][:, j * 512:(j + 1) * 512])
                    t4ctr[0] += 1
        if st_i < len(strips) - 8 and st_i % 5 == 0:
            tt_ = FIX_CTR[0]
            if tt_ < NTT:
                for _k in range(4):
                    fixup_tile(tt_ + _k)
                outproj(tt_ // 4)
                FIX_CTR[0] = tt_ + 4

    finalize_max("t4", 4, [], t4ctr[0])
    with BLOCK() as blk:
        @blk.sync
        def _(sy):
            P.wait(sy, "fm_t4_r")
            P.wait(sy, "cc3_wr", P.val("cc3_wr"))
            P.inc("cc3_wr", sy.dma_start(out=CC_I[3][0:1, 0:1],
                                         in_=mrow[0:1, 4:5]), 16)

        @blk.gpsimd
        def _(g):
            P.wait(g, "cc3_wr")
            P.inc("cc3", g.collective_compute(
                "AllReduce", ALU.max, replica_groups=[list(range(NCORES))],
                ins=[CC_I[3].opt()], outs=[CC_O[3].opt()]))

    with BLOCK() as blk:
        @blk.sync
        def _(sy):
            P.wait(sy, "cc3")
            P.wait(sy, "scmm", P.val("scmm"))
            P.wait(sy, "ccl3", P.val("ccl3"))
            P.inc("ccl3", sy.dma_start(out=ccs[0:1, 8:9], in_=CC_O[3]), 16)
    with BLOCK() as blk:
        @blk.vector
        def _(v):
            P.wait(v, "ccl3")
            P.inc("c3v", v.tensor_tensor(ccs[0:1, 10:11], ccs[0:1, 4:5],
                                         ccs[0:1, 0:1], ALU.mult))
    with BLOCK() as blk:
        @blk.vector
        def _(v):
            P.wait(v, "c3v")
            P.inc("c3v", v.tensor_tensor(ccs[0:1, 11:12], ccs[0:1, 8:9],
                                         ccs[0:1, 10:11], ALU.mult))
    with BLOCK() as blk:
        @blk.vector
        def _(v):
            P.wait(v, "c3v")
            P.inc("c3v", v.tensor_scalar(ccs[0:1, 7:8], ccs[0:1, 11:12],
                                         1.0 / QMAX, 1e-12, ALU.mult, ALU.max))
    with BLOCK() as blk:
        @blk.vector
        def _(v):
            P.wait(v, "c3v")
            P.inc("c3v", v.reciprocal(ccs[0:1, 9:10], ccs[0:1, 7:8]))
    with BLOCK() as blk:
        @blk.vector
        def _(v):
            P.wait(v, "c3v")
            P.inc("c3v", v.tensor_tensor(ccs[0:1, 9:10], ccs[0:1, 10:11],
                                         ccs[0:1, 9:10], ALU.mult))
    bcast_scal([(9, SC_F4R4), (7, SC_S4)], ["c3v"])
    ph6.close()

    _build_final(locals(), fixup_tile, outproj)
    ph7.close()

    P._flush()
    _gblk_cm.__exit__(None, None, None)
    es.close()
    mybir.codegen_inst_isa_subclasses(nc)
    return nc


def _ssd_q(M, n, q):
    nc, P = M["nc"], M["P"]
    BLOCK = M["BLOCK"]
    sb, pst = M["sb"], M["pst"]
    idbf, idf32, trineg, ones_r, d32 = (M["idbf"], M["idf32"], M["trineg"],
                                        M["ones_r"], M["d32"])
    (xcs, bcs, e4b2, g4b2, xtok, xsc, btok, bsc, hcur, ylsb, yint, dtT,
     eslT, egl, sro, darow, dat, dtc, eslc, ps_zz, ps_tb, SLOC, YL,
     CH_, T) = (M["xcs"], M["bcs"], M["e4b2"], M["g4b2"], M["xtok"], M["xsc"],
                M["btok"], M["bsc"], M["hcur"], M["ylsb"], M["yint"],
                M["dtT"], M["eslT"], M["egl"], M["sro"], M["darow"], M["dat"],
                M["dtc"], M["eslc"], M["ps_zz"], M["ps_tb"],
                M["SLOC"], M["YL"], CH, M["T"])
    csb = M["csb"]
    hbf = M["hbf"]
    negr = M["negr"]
    qoff = 16 * CH + q * 128
    upd = n * 4 + q
    hsrc = hcur
    hdst = hcur
    gt = n * CH + q * 128

    with BLOCK() as blk:
        @blk.sync
        def _(sy):
            P.wait(sy, "sg_wr")
            P.wait(sy, "srof", P.val("srof"))
            P.wait(sy, "srol", P.val("srol"))
            P.inc("srol", sy.dma_start(
                out=sro[0:1, :].rearrange("o (h t) -> o h t", h=32),
                in_=SLOC[:, gt:gt + 128]), 16)

    for g4 in range(4):
        with BLOCK() as blk:
            @blk.tensor
            def _(t, g4=g4):
                P.wait(t, "xcsev", P.val("xcsev"))
                P.wait(t, "xtev", P.val("xtev"))
                P.wait(t, "bcev", P.val("bcev"))
                last = None
                for j in range(4):
                    db = g4 * 4 + j
                    last = t.transpose(
                        ps_tb[:, j * 128:(j + 1) * 128],
                        xcs[:, db * CH + q * 128:db * CH + (q + 1) * 128],
                        idbf[:])
                P.inc("xttr", last)

            @blk.vector
            def _(v, g4=g4):
                P.wait(v, "xttr")
                P.inc("xtev", v.tensor_copy(xtok[:, g4 * 512:(g4 + 1) * 512],
                                            ps_tb[:]))

    with BLOCK() as blk:
        @blk.tensor
        def _(t):
            P.wait(t, "xtev", P.val("xtev"))
            P.inc("bttr", t.transpose(ps_tb[:, 0:64],
                                      xcs[0:64, qoff:qoff + 128],
                                      idbf[0:64, 0:64]))

        @blk.vector
        def _(v):
            P.wait(v, "bttr")
            P.inc("btev", v.tensor_copy(btok[:], ps_tb[:, 0:64]))

    with BLOCK() as blk:
        @blk.tensor
        def _(t):
            P.wait(t, "dtv1", P.val("dtv1"))
            P.wait(t, "esl_s", P.val("esl_s"))
            P.wait(t, "ftev", P.val("ftev"))
            P.wait(t, "dpe", P.val("dpe"))
            P.wait(t, "ht2", P.val("ht2"))
            t.transpose(ps_zz[:, 0:32], dtc[:, q * 128:(q + 1) * 128],
                        idf32[0:32, 0:32])
            P.inc("fttr", t.transpose(ps_zz[:, 32:64],
                                      eslc[:, q * 128:(q + 1) * 128],
                                      idf32[0:32, 0:32]))

        @blk.vector
        def _(v):
            P.wait(v, "fttr")
            v.tensor_copy(dtT[:], ps_zz[:, 0:32])
            P.inc("ftev", v.tensor_copy(eslT[:], ps_zz[:, 32:64]))

    with BLOCK() as blk:
        @blk.vector
        def _(v):
            P.wait(v, "xtev", P.val("xtev"))
            P.wait(v, "ftev", P.val("ftev"))
            P.wait(v, "dx_v")
            P.inc("xscv", v.tensor_tensor(
                xsc[:].rearrange("p (h j) -> p h j", h=32),
                xtok[:].rearrange("p (h j) -> p h j", h=32),
                dtT[:].unsqueeze(2).broadcast_to([128, 32, 64]),
                ALU.mult))

    if "GBLD_BASE" not in M:
        M["GBLD_BASE"] = [0]
    GBLD_BASE = M["GBLD_BASE"]
    GBLD_BASE[0] = P.val("gbld")
    YMM_BASE = P.val("ymm")

    def emit_D(grp):
        h0 = grp * 4
        par = grp % 2
        e4b = e4b2[par]
        g4b = g4b2[par]
        dbank = ps_zz[:, 0:512] if par == 0 else ps_zz[:, 1536:2048]
        with BLOCK() as blk:
            @blk.tensor
            def _(t, grp=grp, h0=h0, dbank=dbank):
                P.wait(t, "srol")
                P.wait(t, "dpe", max(0, P.val("dpe") - 1))
                if grp == 0:
                    P.wait(t, "xtev", P.val("xtev"))
                    P.wait(t, "ht2", P.val("ht2"))
                if grp % 2 == 1:
                    P.wait(t, "bcev", P.val("bcev"))
                    P.wait(t, "datv", P.val("datv"))
                t.matmul(dbank[:], ones_r[:, 0:128],
                         sro[0:1, h0 * 128:(h0 + 4) * 128],
                         start=True, stop=False)
                for hh in range(4):
                    h = h0 + hh
                    t.matmul(dbank[:, hh * 128:(hh + 1) * 128],
                             sro[0:1, h * 128:(h + 1) * 128],
                             negr[0:1, 0:128],
                             start=False, stop=False)
                last = t.matmul(dbank[:], idbf[:], trineg[:],
                                start=False, stop=True)
                P.inc("dmm", last)

            @blk.scalar
            def _(s, grp=grp, e4b=e4b, dbank=dbank):
                P.wait(s, "dmm")
                P.wait(s, "eglv", max(0, P.val("eglv") - 1))
                P.wait(s, "gbld", max(0, P.val("gbld") - 1))
                last = s.activation(e4b[:], dbank[:], AF.Exp)
                P.inc("eexp", last)
                P.inc("dpe", last)

            @blk.vector
            def _(v, grp=grp, q=q, e4b=e4b, g4b=g4b):
                P.wait(v, "eexp")
                P.wait(v, "bcev", P.val("bcev"))
                P.wait(v, "gfree", YMM_BASE + 2 * (grp - 1) if grp >= 2
                       else max(0, YMM_BASE - 2))
                last = v.tensor_tensor(
                    g4b[:].rearrange("p (h t) -> p h t", h=4),
                    e4b[:].rearrange("p (h t) -> p h t", h=4),
                    bcs[:, q * 128:(q + 1) * 128].unsqueeze(1).broadcast_to(
                        [128, 4, 128]), ALU.mult)
                P.inc("gbld", last)
        with BLOCK() as blk:
            @blk.vector
            def _(v, grp=grp, h0=h0, e4b=e4b):
                P.wait(v, "gbld")
                P.wait(v, "eglf", P.val("eglf"))
                P.inc("eglv", v.tensor_copy(
                    egl[:, h0:h0 + 4],
                    e4b[:].rearrange("p (h t) -> p h t", h=4)[:, :, 127:128]
                    .squeeze(2)))
    def emit_Y(grp):
        h0 = grp * 4
        par = grp % 2
        g4b = g4b2[par]
        with BLOCK() as blk:
            @blk.tensor
            def _(t, grp=grp, h0=h0, g4b=g4b):
                P.wait(t, "gbld", GBLD_BASE[0] + grp + 1)
                P.wait(t, "xscv", P.val("xscv"))
                P.wait(t, "hupd", P.val("hupd"))
                P.wait(t, "hbfc", P.val("hbfc"))
                P.wait(t, "ype", P.val("ype"))
                first = (grp % 2 == 0)
                lastp = (grp % 2 == 1)
                for hh in range(4):
                    h = h0 + hh
                    cs_ = slice(((grp % 2) * 4 + hh) * 64,
                                ((grp % 2) * 4 + hh + 1) * 64)
                    t.matmul(ps_zz[:, 512:1024][:, cs_],
                             g4b[:, hh * 128:(hh + 1) * 128],
                             xsc[:, h * 64:(h + 1) * 64],
                             start=(first and hh == 0),
                             stop=(lastp and hh == 3))
                cp_ = slice((grp % 2) * 256, (grp % 2) * 256 + 256)
                last = t.matmul(ps_zz[:, 1024:1536][:, cp_],
                                csb[:, q * 128:(q + 1) * 128],
                                hbf[:, h0 * 64:(h0 + 4) * 64],
                                start=first, stop=lastp)
                P.inc("ymm", last)
                P.inc("gfree", last)
        if grp % 2 == 1:
            pr = grp // 2
            with BLOCK() as blk:
                @blk.vector
                def _(v, pr=pr, grp=grp):
                    P.wait(v, "ymm")
                    P.wait(v, "ftev", P.val("ftev"))
                    P.inc("yi1", v.tensor_tensor(
                        yint[:].rearrange("p (h j) -> p h j", h=8),
                        ps_zz[:, 1024:1536].rearrange("p (h j) -> p h j", h=8),
                        eslT[:, (grp - 1) * 4:(grp + 1) * 4].unsqueeze(2)
                        .broadcast_to([128, 8, 64]), ALU.mult))
            with BLOCK() as blk:
                @blk.vector
                def _(v, pr=pr, n=n, q=q):
                    P.wait(v, "yi1")
                    P.wait(v, "ylwr", P.val("ylwr"))
                    P.wait(v, "xscv", P.val("xscv"))
                    sl_ = slice(pr * 512, (pr + 1) * 512)
                    P.inc("ylv", v.tensor_tensor(
                        ylsb[:, sl_].rearrange("p (h j) -> p h j", h=8),
                        xtok[:, sl_].rearrange("p (h j) -> p h j", h=8),
                        d32[:, pr * 8:(pr + 1) * 8].unsqueeze(2)
                        .broadcast_to([128, 8, 64]), ALU.mult))
                    P.wait(v, "ylv")
                    P.inc("ylv", v.tensor_tensor(ylsb[:, sl_], ylsb[:, sl_],
                                                 yint[:], ALU.add))
                    P.wait(v, "ylv")
                    last = v.tensor_tensor(ylsb[:, sl_], ylsb[:, sl_],
                                           ps_zz[:, 512:1024], ALU.add)
                    P.inc("ylv", last)
                    P.inc("ype", last)

    for _g in range(10):
        if _g < 8:
            emit_D(_g)
        if _g >= 2:
            emit_Y(_g - 2)

    with BLOCK() as blk:
        @blk.sync
        def _(sy, n=n, q=q):
            P.wait(sy, "ylv")
            P.wait(sy, "ylwr", P.val("ylwr"))
            P.inc("ylwr", sy.dma_start(
                out=YL[n * 4 + q, :, :, :].transpose([1, 0, 2]),
                in_=ylsb[:].rearrange("p (a c) -> p a c", a=4)), 16)

    with BLOCK() as blk:
        @blk.vector
        def _(v):
            P.wait(v, "eglv", P.val("eglv"))
            P.wait(v, "btev", P.val("btev"))
            P.inc("bscv", v.tensor_tensor(
                bsc[:].rearrange("p (h j) -> p h j", h=32),
                btok[:].unsqueeze(1).broadcast_to([128, 32, 64]),
                egl[:].unsqueeze(2).broadcast_to([128, 32, 64]), ALU.mult))

        @blk.scalar
        def _(s):
            P.wait(s, "srol")
            P.wait(s, "eglv", P.val("eglv"))
            last = s.activation(darow[:], sro[0:1, :].rearrange(
                "o (h t) -> o h t", h=32)[:, :, 127:128].squeeze(2), AF.Exp)
            P.inc("dar", last)
            P.inc("srof", last)

        @blk.tensor
        def _(t):
            P.wait(t, "dar")
            P.wait(t, "datf", P.val("datf"))
            P.wait(t, "bcev", P.val("bcev"))
            P.wait(t, "dpe", P.val("dpe"))
            P.inc("datm", t.matmul(ps_zz[0:64, 1536:1568], ones_r[:, 0:64],
                                   darow[:], start=True, stop=True))

        @blk.vector
        def _(v):
            P.wait(v, "datm")
            P.inc("datv", v.tensor_copy(dat[:], ps_zz[0:64, 1536:1568]))

    for r in range(4):
        with BLOCK() as blk:
            @blk.tensor
            def _(t, r=r):
                P.wait(t, "bscv")
                P.wait(t, "xscv", P.val("xscv"))
                P.wait(t, "ftev", P.val("ftev"))
                P.wait(t, "dpe", P.val("dpe"))
                P.wait(t, "hape", P.val("hape"))
                last = None
                for hh in range(8):
                    h = r * 8 + hh
                    last = t.matmul(ps_zz[0:64, hh * 64:(hh + 1) * 64],
                                    bsc[:, h * 64:(h + 1) * 64],
                                    xsc[:, h * 64:(h + 1) * 64],
                                    start=(hh == 0), stop=(hh == 7))
                P.inc("hamm", last)

            @blk.vector
            def _(v, r=r):
                P.wait(v, "hamm")
                P.wait(v, "datv", P.val("datv"))
                P.wait(v, "hupd", P.val("hupd"))
                P.wait(v, "hbfc", P.val("hbfc"))
                P.wait(v, "ylv", P.val("ylv"))
                P.inc("ht1", v.tensor_tensor(
                    yint[0:64, :].rearrange("p (h j) -> p h j", h=8),
                    hsrc[:, r * 512:(r + 1) * 512].rearrange(
                        "p (h j) -> p h j", h=8),
                    dat[:, r * 8:(r + 1) * 8].unsqueeze(2).broadcast_to(
                        [64, 8, 64]), ALU.mult))
        with BLOCK() as blk:
            @blk.vector
            def _(v, r=r):
                P.wait(v, "ht1")
                last = v.scalar_tensor_tensor(
                    hdst[:, r * 512:(r + 1) * 512], yint[0:64, :], 1.0,
                    ps_zz[0:64, 0:512], ALU.mult, ALU.add)
                P.inc("ht2", last)
                P.inc("hape", last)
                if r == 3:
                    P.inc("hupd", last)
                    P.inc("datf", last)
                    P.inc("eglf", last)

            @blk.scalar
            def _(s, r=r):
                P.wait(s, "ht2")
                P.wait(s, "ymm", P.val("ymm"))
                P.inc("hbfc", s.activation(
                    hbf[:, r * 512:(r + 1) * 512],
                    hdst[:, r * 512:(r + 1) * 512], AF.Copy))


def _build_fixup(M):
    nc, P = M["nc"], M["P"]
    BLOCK = M["BLOCK"]
    sb, pst = M["sb"], M["pst"]
    T, NCH, NTT = M["T"], M["NCH"], M["NTT"]
    idbf, idf32, ones_r, hmask, scal = (M["idbf"], M["idf32"], M["ones_r"],
                                        M["hmask"], M["scal"])
    YL, ZS, CSP, SGH, HALL, SSP, WOUTB = (M["YL"], M["ZS"], M["CSP"], M["SGH"],
                                          M["HALL"], M["SSP"], M["WOUTB"])
    ph7 = ExitStack()
    heff = [sb(f"heff{i}", [64, D_INNER], F32, ph7) for i in range(2)]
    hfb = sb("hfb", [64, D_INNER], BF16, ph7)
    hsl = sb("hsl", [64, D_INNER], F32, ph7)
    sgm = sb("sgm", [32, T], F32, ph7)
    egt = sb("egt", [128, NTT * 32], F32, ph7)
    ylb = [sb(f"ylb{i}", [128, D_INNER], BF16, ph7) for i in range(2)]
    zsb2 = [sb(f"zsb2{i}", [128, D_INNER], BF16, ph7) for i in range(2)]
    ctb = [sb(f"ctb{i}", [64, 128], BF16, ph7) for i in range(2)]
    ygb = sb("ygb", [128, D_INNER], BF16, ph7)
    ynb = sb("ynb", [128, D_INNER], BF16, ph7)
    rstt = sb("rstt", [128, 8], F32, ph7)
    ynT = sb("ynT", [128, 16 * 512], BF16, ph7)
    wob = [sb(f"wob{i}", [128, 16 * 128], BF16, ph7) for i in range(2)]
    spb = [sb(f"spb{i}", [128, 512], BF16, ph7) for i in range(2)]
    yfx = sb("yfx", [128, 512], F32, ph7)
    ps_f = pst("ps_f", [128, 1024], F32, ph7)
    ps_t2 = M["ps_tb"]
    ps_o = pst("ps_o", [128, 512], F32, ph7)

    for j in range(NCORES):
        with BLOCK() as blk:
            @blk.sync
            def _(sy, j=j):
                P.wait(sy, "ccah")
                P.wait(sy, "t4wr")
                P.wait(sy, "ugwr")
                P.wait(sy, "ylwr")
                P.wait(sy, "zswr")
                P.wait(sy, "csp_wr")
                P.wait(sy, "sg_wr")
                P.wait(sy, "xnt_wr")
                P.wait(sy, "hslf", P.val("hslf"))
                P.wait(sy, "hsll", P.val("hsll"))
                P.inc("hsll", sy.dma_start(
                    out=hsl[:].rearrange("p q -> p q"),
                    in_=HALL[j:j + 1, :].rearrange("o (p q) -> p (o q)", p=64)),
                    16)

            @blk.vector
            def _(v, j=j):
                P.wait(v, "hsll")
                if j == 0:
                    last = v.tensor_scalar(heff[0][:], hsl[:],
                                           hmask[:, 0:1], None, ALU.mult)
                else:
                    last = v.scalar_tensor_tensor(
                        heff[j % 2][:], hsl[:], hmask[:, j:j + 1],
                        heff[1 - j % 2][:], ALU.mult, ALU.add)
                P.inc("heffv", last)
                P.inc("hslf", last)
    hfin = heff[(NCORES - 1) % 2]
    with BLOCK() as blk:
        @blk.scalar
        def _(s):
            P.wait(s, "heffv", P.val("heffv"))
            P.inc("hfbc", s.activation(hfb[:], hfin[:], AF.Copy))

    with BLOCK() as blk:
        @blk.sync
        def _(sy):
            P.wait(sy, "sg_wr")
            P.wait(sy, "ugwr")
            P.wait(sy, "ylwr")
            P.wait(sy, "sgl", P.val("sgl"))
            P.inc("sgl", sy.dma_start(out=sgm[:], in_=SGH), 16)

        @blk.scalar
        def _(s):
            P.wait(s, "sgl")
            P.inc("sge", s.activation(sgm[:], sgm[:], AF.Exp))

    egt_vals = []
    for tt in range(NTT):
        with BLOCK() as blk:
            @blk.tensor
            def _(t, tt=tt):
                P.wait(t, "sge")
                P.wait(t, "egtv", P.val("egtv"))
                P.inc("egtt", t.transpose(ps_o[:, 0:32],
                                          sgm[:, tt * 128:(tt + 1) * 128],
                                          idf32[0:32, 0:32]))

            @blk.vector
            def _(v, tt=tt):
                P.wait(v, "egtt")
                P.inc("egtv", v.tensor_copy(egt[:, tt * 32:(tt + 1) * 32],
                                            ps_o[:, 0:32]))
        egt_vals.append(P.val("egtv"))

    def fixup_tile(tt):
        bi = tt % 2
        nch_ = tt // 4
        with BLOCK() as blk:
            @blk.sync
            def _(sy, tt=tt, bi=bi):
                P.wait(sy, "ylwr")
                P.wait(sy, "zswr")
                P.wait(sy, "csp_wr")
                P.wait(sy, "ugwr")
                P.wait(sy, "xnt_wr")
                P.wait(sy, "ysqs", P.val("ysqs"))
                P.wait(sy, f"fxlf{bi}", P.val(f"fxlf{bi}"))
                P.wait(sy, f"fxl{bi}", P.val(f"fxl{bi}"))
                P.inc(f"fxl{bi}", sy.dma_start(
                    out=ylb[bi][:].rearrange("p (a c) -> p a c", a=4),
                    in_=YL[tt, :, :, :].transpose([1, 0, 2])), 16)
                P.wait(sy, f"fxl{bi}", P.val(f"fxl{bi}"))
                P.inc(f"fxl{bi}", sy.dma_start(
                    out=zsb2[bi][:].rearrange("p (a c) -> p a c", a=4),
                    in_=ZS[tt, :, :, :].transpose([1, 0, 2])), 16)
                P.wait(sy, f"fxl{bi}", P.val(f"fxl{bi}"))
                P.inc(f"fxl{bi}", sy.dma_start(
                    out=ctb[bi][:], in_=CSP[:, tt * 128:(tt + 1) * 128]), 16)

        for hb in range(2):
            with BLOCK() as blk:
                @blk.tensor
                def _(t, tt=tt, bi=bi, hb=hb):
                    P.wait(t, f"fxl{bi}")
                    P.wait(t, "hfbc", P.val("hfbc"))
                    P.wait(t, "yfx1", P.val("yfx1"))
                    last = None
                    for hl8 in range(2):
                        last = t.matmul(
                            ps_f[:, hl8 * 512:(hl8 + 1) * 512],
                            ctb[bi][:],
                            hfb[:, (hb * 16 + hl8 * 8) * 64:
                                (hb * 16 + hl8 * 8 + 8) * 64],
                            start=True, stop=True)
                    P.inc("fxmm", last)
            for dg in (hb * 2, hb * 2 + 1):
                with BLOCK() as blk:
                    @blk.vector
                    def _(v, tt=tt, dg=dg, bi=bi):
                        P.wait(v, "fxmm")
                        P.wait(v, "egtv", egt_vals[tt])
                        P.wait(v, "yfx_f", P.val("yfx_f"))
                        P.inc("yfx1", v.tensor_tensor(
                            yfx[:].rearrange("p (h j) -> p h j", h=8),
                            ps_f[:, (dg % 2) * 512:
                                 (dg % 2 + 1) * 512].rearrange(
                                "p (h j) -> p h j", h=8),
                            egt[:, tt * 32 + dg * 8:tt * 32 + (dg + 1) * 8]
                            .unsqueeze(2).broadcast_to([128, 8, 64]),
                            ALU.mult))
                with BLOCK() as blk:
                    @blk.vector
                    def _(v, tt=tt, dg=dg, bi=bi):
                        P.wait(v, "yfx1")
                        P.inc("yfx2", v.scalar_tensor_tensor(
                            yfx[:], yfx[:], 1.0,
                            ylb[bi][:, dg * 512:(dg + 1) * 512],
                            ALU.mult, ALU.add))
                with BLOCK() as blk:
                    @blk.vector
                    def _(v, tt=tt, dg=dg, bi=bi):
                        P.wait(v, "yfx2")
                        last = v.tensor_tensor(
                            ygb[:, dg * 512:(dg + 1) * 512],
                            yfx[:],
                            zsb2[bi][:, dg * 512:(dg + 1) * 512],
                            ALU.mult)
                        P.inc("ygv", last)
                        P.inc("yfx_f", last)
                        if dg == 3:
                            P.inc("yfpe", last)
                            P.inc(f"fxlf{bi}", last)

        with BLOCK() as blk:
            @blk.scalar
            def _(s, tt=tt, bi=bi):
                P.wait(s, "ygv", P.val("ygv"))
                P.wait(s, "ynv", P.val("ynv"))
                P.inc("ysqs", s.activation(ylb[bi][:], ygb[:], AF.Square,
                                           accum_out=rstt[:, 0:1]))
        with BLOCK() as blk:
            @blk.vector
            def _(v):
                P.wait(v, "ysqs")
                P.inc("rch", v.tensor_scalar(rstt[:, 1:2], rstt[:, 0:1],
                                             1.0 / 2048.0, EPS,
                                             ALU.mult, ALU.add))
        with BLOCK() as blk:
            @blk.vector
            def _(v):
                P.wait(v, "rch")
                P.inc("rch", v.reciprocal(rstt[:, 2:3], rstt[:, 1:2]))
        with BLOCK() as blk:
            @blk.scalar
            def _(s):
                P.wait(s, "rch")
                P.inc("rss", s.activation(rstt[:, 3:4], rstt[:, 2:3], AF.Sqrt))

            @blk.vector
            def _(v, tt=tt):
                P.wait(v, "rss")
                P.inc("ynv", v.tensor_scalar(ynb[:], ygb[:], rstt[:, 3:4],
                                             None, ALU.mult))

        for g4 in range(4):
            with BLOCK() as blk:
                @blk.tensor
                def _(t, tt=tt, g4=g4):
                    P.wait(t, "ynv", P.val("ynv"))
                    P.wait(t, "xtev", P.val("xtev"))
                    P.wait(t, "ytev", P.val("ytev"))
                    last = None
                    for j in range(4):
                        kb = g4 * 4 + j
                        last = t.transpose(ps_t2[:, j * 128:(j + 1) * 128],
                                           ynb[:, kb * 128:(kb + 1) * 128],
                                           idbf[:])
                    P.inc("yttr", last)

                @blk.vector
                def _(v, tt=tt, g4=g4):
                    P.wait(v, "yttr")
                    last = None
                    for j in range(4):
                        kb = g4 * 4 + j
                        last = v.tensor_copy(
                            ynT[:, kb * 512 + (tt % 4) * 128:
                                kb * 512 + (tt % 4 + 1) * 128],
                            ps_t2[:, j * 128:(j + 1) * 128])
                    P.inc("ytev", last)

    def outproj(nch_):
        if True:
            for ob in range(8):
                obi = ob % 2
                with BLOCK() as blk:
                    @blk.sync
                    def _(sy, ob=ob, obi=obi):
                        P.wait(sy, f"wobf{obi}", P.val(f"wobf{obi}"))
                        P.wait(sy, "woutb_wr")
                        P.wait(sy, f"wobl{obi}", P.val(f"wobl{obi}"))
                        P.inc(f"wobl{obi}", sy.dma_start(
                            out=wob[obi][:].rearrange("p (k q) -> p k q", k=16),
                            in_=WOUTB[ob, :, :, :].transpose([1, 0, 2])), 16)

                    @blk.tensor
                    def _(t, ob=ob, obi=obi, nch_=nch_):
                        P.wait(t, f"wobl{obi}")
                        P.wait(t, "egtv", egt_vals[-1])
                        P.wait(t, "ytev", P.val("ytev"))
                        P.wait(t, "ope", P.val("ope"))
                        last = None
                        for kb in range(16):
                            last = t.matmul(
                                ps_o[:],
                                wob[obi][:, kb * 128:(kb + 1) * 128],
                                ynT[:, kb * 512:(kb + 1) * 512],
                                start=(kb == 0), stop=(kb == 15))
                        P.inc("opmm", last)
                        P.inc(f"wobf{obi}", last)

                    @blk.scalar
                    def _(s, ob=ob, obi=obi):
                        P.wait(s, "opmm")
                        P.wait(s, "sspwr", P.val("sspwr"))
                        last = s.copy(spb[obi][:], ps_o[:])
                        P.inc("opev", last)
                        P.inc("ope", last)

                    @blk.sync
                    def _(sy, ob=ob, obi=obi, nch_=nch_):
                        P.wait(sy, "opev")
                        P.wait(sy, "sspwr", P.val("sspwr"))
                        P.inc("sspwr", sy.dma_start(
                            out=SSP[ob, nch_, :, :], in_=spb[obi][:]), 16)
            with BLOCK() as blk:
                @blk.vector
                def _(v):
                    P.wait(v, "opmm", P.val("opmm"))
                    P.inc("ytev", v.tensor_copy(rstt[:, 5:6], rstt[:, 3:4]))

    return fixup_tile, outproj, ph7


def _build_final(M, fixup_tile, outproj):
    nc, P = M["nc"], M["P"]
    BLOCK = M["BLOCK"]
    sb, pst = M["sb"], M["pst"]
    T, NCH = M["T"], M["NCH"]
    idf32, scal = M["idf32"], M["scal"]
    T4S, SSP, AGW_O, x_in, out_t = (M["T4S"], M["SSP"], M["AGW_O"], M["x_in"],
                                    M["out_t"])
    ph8 = ExitStack()
    wdt = sb("wdt", [128, 24 * 8 * 128], BF16, ph8)
    t4b = [sb(f"t4b{i}", [128, 2048], BF16, ph8) for i in range(2)]
    ob_sb = sb("ob_sb", [128, 24 * 512], BF16, ph8)
    gsb2 = [sb(f"gsb{i}", [128, 512], BF16, ph8) for i in range(2)]
    ssb = [sb(f"ssb{i}", [128, 512], BF16, ph8) for i in range(2)]
    fsb2 = [sb(f"fsb{i}", [128, 512], F32, ph8) for i in range(2)]
    xtb = [sb(f"xtb{i}", [128, 128], F32, ph8) for i in range(2)]
    osb = [sb(f"osb{i}", [128, 128], F32, ph8) for i in range(2)]
    ps_dn = pst("ps_dn", [128, 1024], F32, ph8)
    ps_ft = pst("ps_ft", [128, 512], F32, ph8)

    with BLOCK() as blk:
        @blk.sync
        def _(sy):
            P.wait(sy, "ccagw")
            P.wait(sy, "zswr")
            P.wait(sy, "ylwr")
            P.wait(sy, "csp_wr")
            P.wait(sy, "t4wr")
            P.wait(sy, "t3wr")
            for r in range(8):
                P.wait(sy, "wdtl", P.val("wdtl"))
                P.inc("wdtl", sy.dma_start(
                    out=wdt[:, r * 24 * 128:(r + 1) * 24 * 128].rearrange(
                        "p (k q) -> p k q", k=24),
                    in_=AGW_O[r, AGW_DOWN:AGW_DOWN + 24 * TIL].rearrange(
                        "(k p q) -> k p q", k=24, p=128).transpose([1, 0, 2])),
                    16)

    OV_VALS = []
    for n in range(NCH):
        bi = n % 2
        for j4 in range(4):
            fixup_tile(n * 4 + j4)
        outproj(n)
        OBV_VALS = []
        for sj in range(6):
            with BLOCK() as blk:
                @blk.sync
                def _(sy, n=n, sj=sj, bi=bi):
                    P.wait(sy, "t4wr")
                    P.wait(sy, f"t4f{bi}", P.val(f"t4f{bi}"))
                    P.wait(sy, f"t4l{bi}", P.val(f"t4l{bi}"))
                    P.inc(f"t4l{bi}", sy.dma_start(
                        out=t4b[bi][:].rearrange("p (j q) -> p j q", j=4),
                        in_=T4S[sj * 4:(sj + 1) * 4, n, :, :]
                        .transpose([1, 0, 2])), 16)

                @blk.vector
                def _(v, n=n, sj=sj, bi=bi):
                    P.wait(v, f"t4l{bi}")
                    P.wait(v, "scbc")
                    P.wait(v, "obf", P.val("obf"))
                    last = v._custom_dve(
                        QROUND, out=ob_sb[:, sj * 4 * 512:(sj + 1) * 4 * 512],
                        in0=t4b[bi][:], s0=scal[:, SC_F4R4:SC_F4R4 + 1],
                        s1=scal[:, SC_S4:SC_S4 + 1], imm2=MAGIC)
                    P.inc("obv", last)
                    P.inc(f"t4f{bi}", last)
            OBV_VALS.append(P.val("obv"))

        for ob in range(8):
            sbi = ob % 2
            with BLOCK() as blk:
                @blk.sync
                def _(sy, n=n, ob=ob, sbi=sbi):
                    P.wait(sy, "sspwr")
                    P.wait(sy, f"ssbf{sbi}", P.val(f"ssbf{sbi}"))
                    P.wait(sy, f"ssbl{sbi}", P.val(f"ssbl{sbi}"))
                    P.inc(f"ssbl{sbi}", sy.dma_start(
                        out=ssb[sbi][:], in_=SSP[ob, n, :, :]), 16)

                @blk.tensor
                def _(t, n=n, ob=ob, sbi=sbi, OBV_VALS=OBV_VALS):
                    P.wait(t, "wdtl")
                    P.wait(t, "dnpe", max(0, P.val("dnpe") - 1))
                    last = None
                    prev_sj = -1
                    for kb in range(24):
                        sj_ = kb // 4
                        if sj_ != prev_sj:
                            P.wait(t, "obv", OBV_VALS[sj_])
                            prev_sj = sj_
                        last = t.matmul(
                            ps_dn[:, sbi * 512:(sbi + 1) * 512],
                            wdt[:, (ob * 24 + kb) * 128:(ob * 24 + kb + 1) * 128],
                            ob_sb[:, kb * 512:(kb + 1) * 512],
                            start=(kb == 0), stop=(kb == 23))
                    P.inc("dnmm", last)
                    if ob == 7:
                        P.inc("obf", last)

                @blk.scalar
                def _(s, ob=ob, sbi=sbi):
                    P.wait(s, "dnmm")
                    P.wait(s, "fv", max(0, P.val("fv") - 1))
                    last = s.activation(gsb2[sbi][:],
                                        ps_dn[:, sbi * 512:(sbi + 1) * 512],
                                        AF.Silu)
                    P.inc("gev", last)
                    P.inc("dnpe", last)

                @blk.vector
                def _(v, ob=ob, sbi=sbi):
                    P.wait(v, "gev", P.val("gev"))
                    P.wait(v, f"ssbl{sbi}")
                    P.wait(v, "ftmm", max(0, P.val("ftmm") - 4))
                    last = v.tensor_tensor(fsb2[sbi][:], gsb2[sbi][:],
                                           ssb[sbi][:], ALU.mult)
                    P.inc("fv", last)
                    P.inc(f"ssbf{sbi}", last)

            for j in range(4):
                xbi = j % 2
                with BLOCK() as blk:
                    @blk.sync
                    def _(sy, n=n, ob=ob, j=j, xbi=xbi):
                        P.wait(sy, f"xtbf{xbi}", P.val(f"xtbf{xbi}"))
                        tt = n * 4 + j
                        P.wait(sy, f"xtbl{xbi}", P.val(f"xtbl{xbi}"))
                        P.inc(f"xtbl{xbi}", sy.dma_start(
                            out=xtb[xbi][:, 0:128],
                            in_=x_in[tt * 128:(tt + 1) * 128,
                                     ob * 128:(ob + 1) * 128]), 16)

                    @blk.tensor
                    def _(t, n=n, ob=ob, j=j, sbi=sbi,
                          ovv=(OV_VALS[-3] if len(OV_VALS) >= 3 else 0)):
                        P.wait(t, "fv", P.val("fv"))
                        P.wait(t, "ftpe", ovv)
                        P.inc("ftmm", t.transpose(
                            ps_ft[:, j * 128:(j + 1) * 128],
                            fsb2[sbi][:, j * 128:(j + 1) * 128],
                            idf32[:]))

                    @blk.vector
                    def _(v, n=n, ob=ob, j=j, xbi=xbi):
                        P.wait(v, "ftmm", P.val("ftmm"))
                        P.wait(v, f"xtbl{xbi}")
                        P.wait(v, "outwr", max(0, P.val("outwr") - 32))
                        last = v.scalar_tensor_tensor(
                            osb[j % 2][:, 0:128], ps_ft[:, j * 128:(j + 1) * 128],
                            1.0, xtb[xbi][:, 0:128], ALU.mult, ALU.add)
                        P.inc("ov", last)
                        P.inc("ftpe", last)
                        P.inc(f"xtbf{xbi}", last)
                        OV_VALS.append(P.val("ov"))

                    @blk.sync
                    def _(sy, n=n, ob=ob, j=j):
                        P.wait(sy, "ov")
                        tt = n * 4 + j
                        P.wait(sy, "outwr", P.val("outwr"))
                        last = sy.dma_start(
                            out=out_t[tt * 128:(tt + 1) * 128,
                                      ob * 128:(ob + 1) * 128],
                            in_=osb[j % 2][:, 0:128])
                        P.inc("outwr", last, 16)

    with BLOCK() as blk:
        @blk.sync
        def _(sy):
            P.wait(sy, "outwr")
            P.wait(sy, "zswr", P.val("zswr"))

    ph8.close()


_NC_CACHE = {}


def kernel(**inputs):
    x = np.ascontiguousarray(inputs["L0_attn_x"], dtype=np.float32)
    W_in = np.ascontiguousarray(inputs["L0_attn_W_in"], dtype=np.float32)
    conv_w = np.ascontiguousarray(inputs["L0_attn_conv_w"], dtype=np.float32)
    conv_b = np.ascontiguousarray(inputs["L0_attn_conv_b"], dtype=np.float32)
    A_log = np.ascontiguousarray(inputs["L0_attn_A_log"], dtype=np.float32)
    D_param = np.ascontiguousarray(inputs["L0_attn_D_param"], dtype=np.float32)
    dt_bias = np.ascontiguousarray(inputs["L0_attn_dt_bias"], dtype=np.float32)
    norm_w = np.ascontiguousarray(inputs["L0_attn_norm_w"], dtype=np.float32)
    W_out = np.ascontiguousarray(inputs["L0_attn_W_out"], dtype=np.float32)
    w_gate = np.ascontiguousarray(inputs["L0_ffn_w_gate"], dtype=np.float32)
    w_up = np.ascontiguousarray(inputs["L0_ffn_w_up"], dtype=np.float32)
    w_down = np.ascontiguousarray(inputs["L0_ffn_w_down"], dtype=np.float32)

    B, L_, D = x.shape
    T = (B * L_) // NCORES
    if T not in _NC_CACHE:
        _NC_CACHE[T] = build_nc(T=T)
    nc = _NC_CACHE[T]

    ident = np.eye(128, dtype=np.float32)
    tri = np.where(np.arange(128)[:, None] > np.arange(128)[None, :],
                   np.float32(-1e9), np.float32(0.0))
    trineg4 = np.tile(tri, (1, 4)).astype(BF)
    convw_sb = conv_w[:, 0, :].reshape(17, 128, 4).transpose(1, 0, 2).reshape(128, 68)
    convb_sb = conv_b.reshape(17, 128).T.copy()
    normw_sb = norm_w.reshape(16, 128).T.copy()

    in_maps = []
    for k in range(NCORES):
        b, half = k // 2, k % 2
        t0 = half * T
        halo = (np.zeros((3, D), np.float32) if half == 0 else x[b, t0 - 3:t0])
        hm = np.zeros((64, 8), np.float32)
        if half == 1:
            hm[:, k - 1] = 1.0
        in_maps.append({
            "x": x[b, t0:t0 + T], "x_halo": halo, "w_in": W_in, "w_out": W_out,
            "wup_sh": w_up[k * 384:(k + 1) * 384],
            "wgate_sh": w_gate[k * 384:(k + 1) * 384],
            "wdown_sh": w_down[k * 128:(k + 1) * 128],
            "conv_w_sb": convw_sb, "conv_b_sb": convb_sb, "normw_sb": normw_sb,
            "a_log": A_log.reshape(32, 1), "dt_bias": dt_bias.reshape(32, 1),
            "d_par": D_param.reshape(1, 32), "hmask": hm,
            "ident_bf": ident.astype(BF), "ident_f32": ident,
            "trineg4": trineg4,
        })
    res = run_bass_kernel_spmd(nc, in_maps, core_ids=list(range(NCORES)))
    out = np.empty((B, L_, D), np.float32)
    for k in range(NCORES):
        b, half = k // 2, k % 2
        out[b, half * T:(half + 1) * T] = res.results[k]["out"]
    return out


# version 20
# speedup vs baseline: 1.1423x; 1.1423x over previous
from contextlib import ExitStack

import numpy as np
import ml_dtypes
import concourse.bass as bass
import concourse.mybir as mybir
from concourse.bass_utils import run_bass_kernel_spmd

F32 = mybir.dt.float32
BF16 = mybir.dt.bfloat16
ALU = mybir.AluOpType
AF = mybir.ActivationFunctionType
AX = mybir.AxisListType
BF = ml_dtypes.bfloat16

MAGIC = 12582912.0
QMAX = 32767.0

import concourse.dve_ops as dve_ops
from concourse.dve_op import DveOp
from concourse.dve_spec import Spec, Src0, Src1, C0, C1, C2, lower as dve_lower


def _register_op(name, body, ref):
    for o in dve_ops.OPS:
        if o.name == name:
            return o
    sha = dve_lower(Spec(body=body), ver="v3", perf_en=False, name=name).sha("v3")
    op = DveOp(name, Spec(body=body), subdim=False, uops_sha={"v3": sha},
               reference=ref)
    dve_ops.OPS.append(op)
    dve_ops.CUSTOM_DVE_SPECS[name] = op.spec
    dve_ops.CUSTOM_DVE_REFS[name] = op.ref
    dve_ops._SUB_OPCODE_FOR_NAME[name] = (
        dve_ops._CUSTOM_DVE_ROW_BASE + len(dve_ops._SUB_OPCODE_FOR_NAME))
    assert max(dve_ops._SUB_OPCODE_FOR_NAME.values()) < 0x20
    return op


QROUND = _register_op(
    "ANT_QROUND_SCALE",
    ((Src0 * C0 + C2) - C2) * C1,
    lambda in0, in1, s0, s1, imm2: np.rint(in0 * s0) * s1,
)
QROUND2 = _register_op(
    "ANT_QROUND2_MUL",
    ((Src0 * C0 + C2) - C2) * ((Src1 * C1 + C2) - C2),
    lambda in0, in1, s0, s1, imm2: np.rint(in0 * s0) * np.rint(in1 * s1),
)

D_MODEL = 1024
N_HEADS = 32
D_INNER = 2048
PROJ = 4256
EPS = 1e-5
NCORES = 8
KB = 128
CH = 512
TIL = KB * KB
AGW_UP = 0
AGW_GATE = 3 * 8 * TIL
AGW_DOWN = 2 * 3 * 8 * TIL
AGW_N = AGW_DOWN + 24 * TIL
ZT0 = 0
XT0 = 4 * 8 * (128 * 512)
DT0 = XT0 + 17 * 8 * TIL
WINB_N = DT0 + 8 * 128 * 32
SC_S0, SC_R0, SC_S1, SC_R1, SC_S2, SC_R2, SC_F3R3, SC_S3S1R4X, SC_F4R4, SC_S4 = range(10)

MAXKINDS = ("xn", "up", "ga", "t3", "t4")
NSLOT = 4



_SEM_ALIAS = {}
for _n in ("ones_v an_s an_v dx_t dx_v dx_v2 dq hcv hcs hch hch2 hnv htr "
           "hev trmm trv sge egtt egtv mb_init").split():
    _SEM_ALIAS[_n] = "p0c"
for _n in "wsh_ld hld xnh_wr xnh_ld sgl hsll winb_wr woutb_wr agw_wr".split():
    _SEM_ALIAS[_n] = "p0h"
for _n in "dtraw dts1 dts2 dts3 dtv1 dtv2 dtv3 dtv4 esl_s".split():
    _SEM_ALIAS[_n] = "dtch"
for _n in "xsv0 xss0".split():
    _SEM_ALIAS[_n] = "xst"
for _n in "xsv1 xss1".split():
    _SEM_ALIAS[_n] = "xst1"
_SEM_ALIAS["xrsd"] = "xch"
_SEM_ALIAS["xch0"] = "xch"
_SEM_ALIAS["xld0"] = "xld"
_SEM_ALIAS["xnbv0"] = "xnbv"
_SEM_ALIAS["xrsd"] = "xch"
for _n in "yfx1 yfx2 ygv yfx_f yfpe fxlf0 fxlf1".split():
    _SEM_ALIAS[_n] = "yfxc"
for _n in "ysqs rch rss ynv".split():
    _SEM_ALIAS[_n] = "rmsc"
for _k in ("xn", "up", "ga", "t3", "t4"):
    for _sfx, _c in (("_l", "fmL"), ("_l2", "fmL"), ("_v", "fmV"),
                     ("_v2", "fmV"), ("_v3", "fmV"), ("_v4", "fmV"),
                     ("_t", "fmT"), ("_r", "fmR")):
        _SEM_ALIAS[f"fm_{_k}{_sfx}"] = _c
for _n in "ccl0 ccl1 ccl2 ccl3".split():
    _SEM_ALIAS[_n] = "cclh"
for _n in "ccv0 ccv1 ccv2 ccv3 c2v c3v".split():
    _SEM_ALIAS[_n] = "cclv"
for _n in "cc0 cc1 cc2 cc3 ccagw ccah".split():
    _SEM_ALIAS[_n] = "ccp"
for _n in "cc0_wr cc1_wr cc2_wr cc3_wr".split():
    _SEM_ALIAS[_n] = "ccw"
_SEM_ALIAS.update(dict(
    dpe="eexp", gfree="ymm", ype="ylv", hape="ht2", datf="ht2", eglf="ht2",
    c1qf="c1tv", ope="opev", dnpe="opev", zpe="zev", xbpe="xbev",
    halope="haloev", srof="dar", xncf="xbmm", bcse="bcev", hslf="heffv",
    ftpe="ytev", obf="fxmm", sconst="p0c", scmm="p0c",
    wzbf0="zmm", wxbf0="xbmm", wxbf1="xbmm",
    ssbf0="ytev", ssbf1="ytev", t4f0="ytev", t4f1="ytev",
    wobf0="fxmm", wobf1="fxmm", xtbf0="ytev", xtbf1="ytev",
    cvt0="cvt", cvt1="cvt", cvt2="cvt", cvt3="cvt", cvt4="cvt",
    wstg0="swl0", wstg1="swl1", w2l0="swl0", w2l1="swl1",
    wstgl0="swl0", wstgl1="swl1",
    cloc="csp_wr", hbfc="dar", hfbc="opev",
    yttr="fxmm", opmm="fxmm", dnmm="fxmm", ftmm="fxmm",
    obv="ytev", fv="ytev", ov="ytev",
    gev="opev",
    sspwr="fxl0", wdtl="fxl0", outwr="fxl0",
    amx1="amx0", amx2="amx0", amx3="amx0", xntr1="xntr0",
    hpub_wr="ylwr",
    bttr="xttr", fttr="xttr", datm="xttr",
    btev="xtev", ftev="xtev",
    wtbf0="ugmm0", wtbf1="ugmm0", ugmm1="ugmm0", ugsp1="ugsp0",
    mxchain="fmR", hupd="ht2",
    cvt="xcsev", dtch="xbev",
))


class Prog:
    def __init__(self, nc, es):
        self.nc = nc
        self.es = es
        self._sems = {}
        self._pinst = None
        self._pend = {}

    def _res(self, name):
        while name in _SEM_ALIAS:
            name = _SEM_ALIAS[name]
        return name

    def sem(self, name):
        name = self._res(name)
        if name not in self._sems:
            self._sems[name] = [self.es.enter_context(self.nc.semaphore(name)), 0]
        return self._sems[name][0]

    def _flush(self):
        if self._pinst is not None:
            if len(self._pend) > 1:
                import os
                if os.environ.get("SEM_DEBUG"):
                    print("MULTI-INC:", sorted(self._pend.keys()), flush=True)
            for nm, k in self._pend.items():
                self._pinst.then_inc(self._sems[nm][0], k)
        self._pinst = None
        self._pend = {}

    def inc(self, name, inst, k=1):
        self.sem(name)
        nm = self._res(name)
        ent = self._sems[nm]
        if inst is not self._pinst:
            self._flush()
            self._pinst = inst
        self._pend[nm] = self._pend.get(nm, 0) + k
        ent[1] += k
        return ent[1]

    def val(self, name):
        self.sem(name)
        return self._sems[self._res(name)][1]

    def wait(self, eng, name, v=None):
        h = self.sem(name)
        eng.wait_ge(h, self.val(name) if v is None else v)


def build_nc(T=4096):
    nc = bass.Bass("TRN2", target_bir_lowering=False, debug=False)
    _gblk_cm = nc.Block()
    _gblk = _gblk_cm.__enter__()

    class _BlkProxy:
        def __enter__(self):
            return _gblk

        def __exit__(self, *a):
            return False

    def BLOCK():
        return _BlkProxy()
    NCH = T // CH
    NTT = T // KB

    def din(name, shape, dt=F32):
        return nc.dram_tensor(name, shape, dt, kind="ExternalInput").ap()

    def dint(name, shape, dt=F32):
        return nc.dram_tensor(name, shape, dt).ap()

    x_in = din("x", [T, D_MODEL])
    xh_in = din("x_halo", [3, D_MODEL])
    w_in = din("w_in", [D_MODEL, PROJ])
    w_out = din("w_out", [D_INNER, D_MODEL])
    wup_sh = din("wup_sh", [384, 1024])
    wgate_sh = din("wgate_sh", [384, 1024])
    wdown_sh = din("wdown_sh", [128, 3072])
    conv_w_in = din("conv_w_sb", [128, 68])
    conv_b_in = din("conv_b_sb", [128, 17])
    normw_in = din("normw_sb", [128, 16])
    alog_in = din("a_log", [32, 1])
    dtb_in = din("dt_bias", [32, 1])
    dpar_in = din("d_par", [1, 32])
    hmask_in = din("hmask", [64, 8])
    idbf_in = din("ident_bf", [128, 128], BF16)
    idf_in = din("ident_f32", [128, 128], F32)
    trineg_in = din("trineg4", [128, 512], BF16)

    out_t = nc.dram_tensor("out", [T, D_MODEL], F32, kind="ExternalOutput").ap()

    XNT = dint("XNT", [8, NTT, KB, KB], BF16)
    XNH = dint("XNH", [8, KB, 4], BF16)
    WINB = dint("WINB", [1, WINB_N], BF16)
    WOUTB = dint("WOUTB", [8, 16, KB, KB], BF16)
    AGW_I = dint("AGW_I", [1, AGW_N], BF16)
    AGW_O = dint("AGW_O", [NCORES, AGW_N], BF16)
    CC_I = [dint(f"CC{i}_I", [1, 2 if i == 1 else 1]) for i in range(4)]
    CC_O = [dint(f"CC{i}_O", [1, 2 if i == 1 else 1]) for i in range(4)]
    HPUB = dint("HPUB", [1, 64 * D_INNER])
    HALL = dint("HALL", [NCORES, 64 * D_INNER])
    UPRAW = dint("UPRAW", [24, NCH, KB, CH], BF16)
    GRAW = dint("GRAW", [24, NCH, KB, CH], BF16)
    T3S = dint("T3S", [24, NCH, KB, CH], BF16)
    T4S = dint("T4S", [24, NCH, KB, CH], BF16)
    ZS = dint("ZS", [NTT, 4, KB, CH], BF16)
    YL = dint("YL", [NTT, 4, KB, CH], BF16)
    CSP = dint("CSP", [64, T], BF16)
    SGH = dint("SGH", [32, T])
    SLOC = dint("SLOC", [32, T])
    SSP = dint("SSP", [8, NCH, KB, CH], BF16)


    es = ExitStack()
    P = Prog(nc, es)

    def sb(name, shape, dt=F32, st=None):
        return (st or es).enter_context(nc.sbuf_tensor(name, shape, dt))

    def pst(name, shape, dt=F32, st=None):
        return (st or es).enter_context(nc.psum_tensor(name, shape, dt))

    idbf = sb("idbf", [128, 128], BF16)
    idf32 = sb("idf32", [128, 128], F32)
    trineg = sb("trineg", [128, 512], BF16)
    ones_r = sb("ones_r", [1, 512], F32)
    negr = sb("negr", [1, 128], F32)
    convw = sb("convw", [128, 68], F32)
    convb = sb("convb", [128, 17], F32)
    normw = sb("normw", [128, 16], F32)
    alog = sb("alog", [32, 1], F32)
    aneg = sb("aneg", [32, 1], F32)
    dtbias = sb("dtbias", [32, 1], F32)
    dparr = sb("dparr", [1, 32], F32)
    hmask = sb("hmask_sb", [64, 8], F32)
    scal = sb("scal", [128, 12], F32)
    ccs = sb("ccs", [1, 12], F32)
    mred = sb("mred", [128, 12], F32)
    mrow = sb("mrow", [1, 8], F32)
    _mcb = [sb(f"maxc_{i}", [128, 200], F32) for i in range(3)]
    MAXCOL = {"xn": _mcb[0], "up": _mcb[1], "ga": _mcb[2],
              "t3": _mcb[0], "t4": _mcb[1]}
    ps_b = pst("ps_b", [128, 128], F32)
    ps_tb = pst("ps_tb", [128, 512], BF16)

    def accmax(v, kind, idx, src):
        assert idx < 200
        last = v.tensor_reduce(MAXCOL[kind][:, idx:idx + 1], src, AX.X,
                               ALU.max, apply_absolute_value=True)
        P.inc("amx0", last)
        return last

    def finalize_max(kind, outcol, waits, nt):
        c2 = 2 * outcol
        tg = f"fm_{kind}"
        with BLOCK() as blk:
            @blk.vector
            def _(v):
                for w in waits:
                    P.wait(v, w)
                P.wait(v, "amx0", P.val("amx0"))
                P.inc(f"{tg}_v", v.tensor_reduce(
                    mred[:, c2:c2 + 1], MAXCOL[kind][:, 0:nt], AX.X, ALU.max))

            @blk.tensor
            def _(t):
                P.wait(t, f"{tg}_v")
                P.wait(t, "mxchain", P.val("mxchain"))
                P.wait(t, "scbc", P.val("scbc"))
                P.inc(f"{tg}_t", t.transpose(ps_b[0:1, 0:128],
                                             mred[:, c2:c2 + 1], idf32[:]))

            @blk.vector
            def _(v):
                P.wait(v, f"{tg}_t")
                last = v.tensor_reduce(mrow[0:1, outcol:outcol + 1],
                                       ps_b[0:1, 0:128], AX.X, ALU.max)
                P.inc(f"{tg}_r", last)
                P.inc("mxchain", last)

    def bcast_scal(pairs, waits):
        with BLOCK() as blk:
            @blk.tensor
            def _(t):
                for w in waits:
                    P.wait(t, w)
                P.wait(t, "scbc", P.val("scbc"))
                last = None
                for sc_, dc_ in pairs:
                    last = t.matmul(ps_b[:, dc_:dc_ + 1], ones_r[:, 0:128],
                                    ccs[0:1, sc_:sc_ + 1], start=True, stop=True)
                P.inc("scmm", last)

            @blk.vector
            def _(v):
                P.wait(v, "scmm")
                last = None
                for sc_, dc_ in pairs:
                    last = v.tensor_copy(scal[:, dc_:dc_ + 1],
                                         ps_b[:, dc_:dc_ + 1])
                P.inc("scbc", last)

    def vchain(sem, *steps):
        for stp in steps:
            with BLOCK() as blk:
                @blk.vector
                def _(v, stp=stp):
                    P.wait(v, sem)
                    last = stp(v)
                    P.inc(sem, last)

    with BLOCK() as blk:
        @blk.sync
        def _(sy):
            for ap_, src in ((idbf[:], idbf_in), (idf32[:], idf_in),
                             (trineg[:], trineg_in), (convw[:], conv_w_in),
                             (convb[:], conv_b_in), (normw[:], normw_in),
                             (alog[:], alog_in), (dtbias[:], dtb_in),
                             (dparr[:], dpar_in), (hmask[:], hmask_in)):
                P.wait(sy, "c_ld", P.val("c_ld"))
                P.inc("c_ld", sy.dma_start(out=ap_, in_=src), 16)

        @blk.vector
        def _(v):
            v.memset(negr[:], -1.0)
            P.inc("ones_v", v.memset(ones_r[:], 1.0))

        @blk.scalar
        def _(s):
            P.wait(s, "c_ld")
            P.inc("an_s", s.activation(aneg[:], alog[:], AF.Exp))

    with BLOCK() as blk:
        @blk.vector
        def _(v):
            P.wait(v, "an_s")
            P.inc("an_v", v.tensor_scalar(aneg[:], aneg[:], -1.0, None, ALU.mult))

        @blk.tensor
        def _(t):
            P.wait(t, "c_ld")
            P.wait(t, "ones_v")
            P.inc("dx_t", t.matmul(ps_b[:, 0:32], ones_r[:, 0:128], dparr[:],
                                   start=True, stop=True))

    d32 = sb("d32", [128, 32], F32)
    with BLOCK() as blk:
        @blk.vector
        def _(v):
            P.wait(v, "dx_t")
            P.inc("dx_v", v.tensor_copy(d32[:], ps_b[:, 0:32]))

    ph0 = ExitStack()
    wstgf = [sb(f"wstgf{i}", [128, PROJ], F32, ph0) for i in range(2)]
    wstg = [sb(f"wstg{i}", [128, PROJ], BF16, ph0) for i in range(2)]
    for fb in range(8):
        bi = fb % 2
        with BLOCK() as blk:
            @blk.sync
            def _(sy, fb=fb, bi=bi):
                P.wait(sy, f"wstgl{bi}", P.val(f"wstgl{bi}"))
                P.inc(f"wstgl{bi}", sy.dma_start(
                    out=wstgf[bi][:], in_=w_in[fb * 128:(fb + 1) * 128, :]),
                    16)

            @blk.scalar
            def _(s, fb=fb, bi=bi):
                P.wait(s, f"wstgl{bi}")
                P.wait(s, "winb_wr", P.val("winb_wr"))
                P.inc(f"wstg{bi}", s.activation(wstg[bi][:], wstgf[bi][:],
                                                AF.Copy))

            @blk.sync
            def _(sy, fb=fb, bi=bi):
                P.wait(sy, f"wstg{bi}")
                P.wait(sy, "winb_wr", P.val("winb_wr"))
                P.inc("winb_wr", sy.dma_start(
                    out=WINB[0:1, ZT0:ZT0 + 4 * 8 * 128 * 512].rearrange(
                        "o (z e p q) -> (z e) p (o q)", z=4, e=8, p=128
                    )[fb::8].transpose([1, 0, 2]),
                    in_=wstg[bi][:, 0:2048].rearrange(
                        "p (z q) -> p z q", z=4)), 16)
                off = XT0 + fb * TIL
                P.wait(sy, "winb_wr", P.val("winb_wr"))
                P.inc("winb_wr", sy.dma_start(
                    out=WINB[0:1, XT0:XT0 + 17 * 8 * TIL].rearrange(
                        "o (b e p q) -> (b e) p (o q)", b=17, e=8, p=128
                    )[fb::8].transpose([1, 0, 2]),
                    in_=wstg[bi][:, 2048:2048 + 17 * 128].rearrange(
                        "p (b q) -> p b q", b=17)), 16)
                off = DT0 + fb * (128 * 32)
                P.wait(sy, "winb_wr", P.val("winb_wr"))
                P.inc("winb_wr", sy.dma_start(
                    out=WINB[0:1, off:off + 128 * 32].rearrange(
                        "o (p q) -> p (o q)", p=128),
                    in_=wstg[bi][:, 4224:4256]), 16)

    wstg2f = [sb(f"wstg2f{i}", [128, 1024], F32, ph0) for i in range(2)]
    wstg2 = [sb(f"wstg2{i}", [128, 1024], BF16, ph0) for i in range(2)]
    for kb in range(16):
        bi = kb % 2
        with BLOCK() as blk:
            @blk.sync
            def _(sy, kb=kb, bi=bi):
                P.wait(sy, f"w2l{bi}", P.val(f"w2l{bi}"))
                P.inc(f"w2l{bi}", sy.dma_start(
                    out=wstg2f[bi][:], in_=w_out[kb * 128:(kb + 1) * 128, :]),
                    16)

            @blk.vector
            def _(v, kb=kb, bi=bi):
                P.wait(v, f"w2l{bi}")
                P.wait(v, "c_ld")
                P.wait(v, "woutb_wr", P.val("woutb_wr"))
                P.inc(f"w2n{bi}", v.tensor_scalar(
                    wstg2[bi][:], wstg2f[bi][:], normw[:, kb:kb + 1], None,
                    ALU.mult))

            @blk.sync
            def _(sy, kb=kb, bi=bi):
                P.wait(sy, f"w2n{bi}")
                P.wait(sy, "woutb_wr", P.val("woutb_wr"))
                P.inc("woutb_wr", sy.dma_start(
                    out=WOUTB[:, kb, :, :].transpose([1, 0, 2]),
                    in_=wstg2[bi][:].rearrange("p (b q) -> p b q", b=8)), 16)

    ph1 = ExitStack()
    wsh = sb("wsh", [128, 6144], F32, ph1)
    wshd = sb("wshd", [128, 3072], F32, ph1)
    rqb = sb("rqb", [128, 6144], F32, ph1)
    rqbd = sb("rqbd", [128, 3072], F32, ph1)
    wqb = sb("wqb", [128, 6144], BF16, ph1)
    wqbd = sb("wqbd", [128, 3072], BF16, ph1)
    amax = sb("amax", [128, 288], F32, ph1)
    sc7 = sb("sc7", [128, 288], F32, ph1)
    rcp7 = sb("rcp7", [128, 288], F32, ph1)
    trsb = sb("trsb", [128, 512], BF16, ph1)
    ps_tr = [pst(f"ps_tr{i}", [128, 512], BF16, ph1) for i in range(2)]

    with BLOCK() as blk:
        @blk.sync
        def _(sy):
            for i in range(3):
                P.wait(sy, "wsh_ld", P.val("wsh_ld"))
                P.inc("wsh_ld", sy.dma_start(
                    out=wsh[:, i * 1024:(i + 1) * 1024],
                    in_=wup_sh[i * 128:(i + 1) * 128, :]), 16)
                P.wait(sy, "wsh_ld", P.val("wsh_ld"))
                P.inc("wsh_ld", sy.dma_start(
                    out=wsh[:, (3 + i) * 1024:(4 + i) * 1024],
                    in_=wgate_sh[i * 128:(i + 1) * 128, :]), 16)
            P.wait(sy, "wsh_ld", P.val("wsh_ld"))
            P.inc("wsh_ld", sy.dma_start(out=wshd[:], in_=wdown_sh), 16)

        @blk.vector
        def _(v):
            P.wait(v, "wsh_ld")
            v.tensor_reduce(amax[:, 0:192].unsqueeze(2),
                            wsh[:].rearrange("p (b e) -> p b e", e=32),
                            AX.X, ALU.max, apply_absolute_value=True)
            P.inc("dq", v.tensor_reduce(
                amax[:, 192:288].unsqueeze(2),
                wshd[:].rearrange("p (b e) -> p b e", e=32),
                AX.X, ALU.max, apply_absolute_value=True))

    vchain(
        "dq",
        lambda v: v.tensor_scalar(sc7[:], amax[:], 1.0 / 7.0, 1e-12,
                                  ALU.mult, ALU.max),
        lambda v: v.reciprocal(rcp7[:], sc7[:]),
        lambda v: (v.tensor_tensor(
            rqb[:].rearrange("p (b e) -> p b e", e=32),
            wsh[:].rearrange("p (b e) -> p b e", e=32),
            rcp7[:, 0:192].unsqueeze(2).broadcast_to([128, 192, 32]), ALU.mult),
            v.tensor_tensor(
                rqbd[:].rearrange("p (b e) -> p b e", e=32),
                wshd[:].rearrange("p (b e) -> p b e", e=32),
                rcp7[:, 192:288].unsqueeze(2).broadcast_to([128, 96, 32]),
                ALU.mult))[-1],
        lambda v: (v._custom_dve(QROUND, out=rqb[:], in0=rqb[:], s0=1.0,
                                 s1=1.0, imm2=MAGIC),
                   v._custom_dve(QROUND, out=rqbd[:], in0=rqbd[:], s0=1.0,
                                 s1=1.0, imm2=MAGIC))[-1],
        lambda v: (v.tensor_tensor(
            wqb[:].rearrange("p (b e) -> p b e", e=32),
            rqb[:].rearrange("p (b e) -> p b e", e=32),
            sc7[:, 0:192].unsqueeze(2).broadcast_to([128, 192, 32]), ALU.mult),
            v.tensor_tensor(
                wqbd[:].rearrange("p (b e) -> p b e", e=32),
                rqbd[:].rearrange("p (b e) -> p b e", e=32),
                sc7[:, 192:288].unsqueeze(2).broadcast_to([128, 96, 32]),
                ALU.mult))[-1],
    )

    tiles = []
    for pb in range(6):
        mat, m3 = pb // 3, pb % 3
        for kbi in range(8):
            tiles.append((wqb[:, pb * 1024 + kbi * 128:pb * 1024 + (kbi + 1) * 128],
                          (AGW_UP if mat == 0 else AGW_GATE) + (m3 * 8 + kbi) * TIL))
    for kbi in range(24):
        tiles.append((wqbd[:, kbi * 128:(kbi + 1) * 128], AGW_DOWN + kbi * TIL))

    for gi in range(0, len(tiles), 4):
        grp = tiles[gi:gi + 4]
        bank = ps_tr[(gi // 4) % 2]
        with BLOCK() as blk:
            @blk.tensor
            def _(t, grp=grp, bank=bank, gi=gi):
                P.wait(t, "dq")
                P.wait(t, "trv", P.val("trv"))
                last = None
                for j, (src, off) in enumerate(grp):
                    last = t.transpose(bank[:, j * 128:(j + 1) * 128], src,
                                       idbf[:])
                P.inc("trmm", last)

            @blk.vector
            def _(v, grp=grp, bank=bank):
                P.wait(v, "trmm")
                P.wait(v, "agw_wr", P.val("agw_wr"))
                P.inc("trv", v.tensor_copy(trsb[:, 0:len(grp) * 128],
                                           bank[:, 0:len(grp) * 128]))

            @blk.sync
            def _(sy, grp=grp):
                P.wait(sy, "trv")
                for j, (src, off) in enumerate(grp):
                    P.wait(sy, "agw_wr", P.val("agw_wr"))
                    P.inc("agw_wr", sy.dma_start(
                        out=AGW_I[0:1, off:off + TIL].rearrange(
                            "o (a b) -> a (o b)", a=128),
                        in_=trsb[:, j * 128:(j + 1) * 128]), 16)

    with BLOCK() as blk:
        @blk.gpsimd
        def _(g):
            P.wait(g, "agw_wr")
            P.inc("ccagw", g.collective_compute(
                "AllGather", ALU.bypass, replica_groups=[list(range(NCORES))],
                ins=[AGW_I.opt()], outs=[AGW_O.opt()]))

    ph1.close()
    ph0.close()

    ph2 = ExitStack()
    xgrp2 = [sb(f"xgrp{i}", [128, 4096], F32, ph2) for i in range(2)]
    xnb2 = [sb(f"xnb{i}", [128, 4096], BF16, ph2) for i in range(2)]
    stats2 = [sb(f"stats{i}", [128, 24], F32, ph2) for i in range(2)]
    sqjunk = sb("sqjunk", [128, 1024], BF16, ph2)
    xnev = sb("xnev", [128, 1024], BF16, ph2)
    ps_xn = [pst(f"ps_xn{i}", [128, 512], BF16, ph2) for i in range(2)]

    xn_tile_ctr = [0]
    NG = max(1, NTT // 4)
    for g_i in range(NG):
        pi = g_i % 2
        xgrp, xnb, stats = xgrp2[pi], xnb2[pi], stats2[pi]
        with BLOCK() as blk:
            @blk.sync
            def _(sy, g_i=g_i, pi=pi, xgrp=xgrp):
                if g_i == 0:
                    P.wait(sy, "winb_wr")
                    P.wait(sy, "woutb_wr")
                if g_i >= 2:
                    P.wait(sy, f"xnbv{pi}", P.val(f"xnbv{pi}"))
                    P.wait(sy, "xntr0", P.val("xntr0"))
                    P.wait(sy, "xntr1", P.val("xntr1"))
                    P.wait(sy, "amx0", P.val("amx0"))
                for j in range(4):
                    ti = g_i * 4 + j
                    P.wait(sy, f"xld{pi}", P.val(f"xld{pi}"))
                    P.inc(f"xld{pi}", sy.dma_start(
                        out=xgrp[:, j * 1024:(j + 1) * 1024],
                        in_=x_in[ti * 128:(ti + 1) * 128, :]), 16)

            @blk.vector
            def _(v, pi=pi, xgrp=xgrp, stats=stats):
                P.wait(v, f"xld{pi}")
                last = None
                for j in range(4):
                    last = v.tensor_reduce(stats[:, j * 6:j * 6 + 1],
                                           xgrp[:, j * 1024:(j + 1) * 1024],
                                           AX.X, ALU.add)
                P.inc(f"xsv{pi}", last)

            @blk.scalar
            def _(s, pi=pi, xgrp=xgrp, stats=stats):
                P.wait(s, f"xld{pi}")
                P.wait(s, "xss0", P.val("xss0"))
                P.wait(s, "xss1", P.val("xss1"))
                for j in range(4):
                    if j:
                        P.wait(s, f"xss{pi}")
                    P.inc(f"xss{pi}", s.activation(
                        sqjunk[:], xgrp[:, j * 1024:(j + 1) * 1024],
                        AF.Square,
                        accum_out=stats[:, j * 6 + 1:j * 6 + 2]))

        with BLOCK() as blk:
            @blk.vector
            def _(v, pi=pi, stats=stats):
                P.wait(v, f"xsv{pi}")
                P.wait(v, f"xss{pi}")
                last = None
                for j in range(4):
                    last = v.tensor_scalar(stats[:, j * 6 + 2:j * 6 + 3],
                                           stats[:, j * 6:j * 6 + 1],
                                           1.0 / 1024.0, None, ALU.mult)
                P.inc(f"xch{pi}", last)
        vchain(
            f"xch{pi}",
            lambda v, stats=stats: [v.tensor_tensor(
                stats[:, j * 6 + 3:j * 6 + 4],
                stats[:, j * 6 + 2:j * 6 + 3],
                stats[:, j * 6 + 2:j * 6 + 3], ALU.mult)
                for j in range(4)][-1],
            lambda v, stats=stats: [v.scalar_tensor_tensor(
                stats[:, j * 6 + 4:j * 6 + 5],
                stats[:, j * 6 + 1:j * 6 + 2],
                1.0 / 1024.0,
                stats[:, j * 6 + 3:j * 6 + 4],
                ALU.mult, ALU.subtract)
                for j in range(4)][-1],
            lambda v, stats=stats: [v.tensor_scalar(
                stats[:, j * 6 + 4:j * 6 + 5],
                stats[:, j * 6 + 4:j * 6 + 5], EPS,
                None, ALU.add) for j in range(4)][-1],
            lambda v, stats=stats: [v.reciprocal(
                stats[:, j * 6 + 5:j * 6 + 6],
                stats[:, j * 6 + 4:j * 6 + 5])
                for j in range(4)][-1],
        )
        with BLOCK() as blk:
            @blk.scalar
            def _(s, pi=pi, stats=stats):
                P.wait(s, f"xch{pi}")
                last = None
                for j in range(4):
                    last = s.activation(stats[:, j * 6 + 4:j * 6 + 5],
                                        stats[:, j * 6 + 5:j * 6 + 6], AF.Sqrt)
                P.inc(f"xch{pi}", last)

            @blk.vector
            def _(v, pi=pi, g_i=g_i, xgrp=xgrp, xnb=xnb, stats=stats):
                P.wait(v, f"xch{pi}")
                if g_i >= 2:
                    P.wait(v, "xntr0", P.val("xntr0"))
                    P.wait(v, "xntr1", P.val("xntr1"))
                    P.wait(v, "amx0", P.val("amx0"))
                last = None
                for j in range(4):
                    last = v.tensor_scalar(xnb[:, j * 1024:(j + 1) * 1024],
                                           xgrp[:, j * 1024:(j + 1) * 1024],
                                           stats[:, j * 6 + 2:j * 6 + 3],
                                           stats[:, j * 6 + 4:j * 6 + 5],
                                           ALU.subtract, ALU.mult)
                P.inc(f"xnbv{pi}", last)

        for j in range(4):
            ti = g_i * 4 + j
            for half in range(2):
                bank = ps_xn[half]
                with BLOCK() as blk:
                    @blk.tensor
                    def _(t, j=j, half=half, bank=bank, pi=pi, xnb=xnb):
                        P.wait(t, f"xnbv{pi}", P.val(f"xnbv{pi}"))
                        P.wait(t, f"xnevd{half}", P.val(f"xnevd{half}"))
                        last = None
                        for fb4 in range(4):
                            fb = half * 4 + fb4
                            last = t.transpose(
                                bank[:, fb4 * 128:(fb4 + 1) * 128],
                                xnb[:, j * 1024 + fb * 128:
                                    j * 1024 + (fb + 1) * 128],
                                idbf[:])
                        P.inc(f"xntr{half}", last)

                    @blk.vector
                    def _(v, half=half, bank=bank):
                        P.wait(v, f"xntr{half}")
                        P.wait(v, "xnt_wr", P.val("xnt_wr"))
                        P.inc(f"xnevd{half}", v.tensor_copy(
                            xnev[:, half * 512:(half + 1) * 512], bank[:]))

                    @blk.sync
                    def _(sy, ti=ti, half=half):
                        P.wait(sy, f"xnevd{half}")
                        P.wait(sy, "xnt_wr", P.val("xnt_wr"))
                        last = sy.dma_start(
                            out=XNT[half * 4:(half + 1) * 4, ti, :, :]
                            .transpose([1, 0, 2]),
                            in_=xnev[:, half * 512:(half + 1) * 512].rearrange(
                                "p (f q) -> p f q", f=4))
                        P.inc("xnt_wr", last, 16)

        with BLOCK() as blk:
            @blk.vector
            def _(v, pi=pi, xnb=xnb):
                P.wait(v, f"xnbv{pi}", P.val(f"xnbv{pi}"))
                for a in range(8):
                    accmax(v, "xn", xn_tile_ctr[0],
                           xnb[:, a * 512:(a + 1) * 512])
                    xn_tile_ctr[0] += 1

    xhb = sb("xhb", [3, 1024], F32, ph2)
    xhn = sb("xhn", [3, 1024], BF16, ph2)
    hstt = sb("hstt", [3, 8], F32, ph2)
    xhT = sb("xhT", [128, 32], BF16, ph2)
    with BLOCK() as blk:
        @blk.sync
        def _(sy):
            P.wait(sy, "hld", P.val("hld"))
            P.inc("hld", sy.dma_start(out=xhb[:], in_=xh_in), 16)

        @blk.vector
        def _(v):
            P.wait(v, "hld")
            P.inc("hcv", v.tensor_reduce(hstt[:, 0:1], xhb[:], AX.X, ALU.add))

        @blk.scalar
        def _(s):
            P.wait(s, "hld")
            P.inc("hcs", s.activation(sqjunk[0:3, :], xhb[:], AF.Square,
                                      accum_out=hstt[:, 1:2]))

    with BLOCK() as blk:
        @blk.vector
        def _(v):
            P.wait(v, "hcv")
            P.wait(v, "hcs")
            P.inc("hch", v.tensor_scalar(hstt[:, 2:3], hstt[:, 0:1],
                                         1.0 / 1024.0, None, ALU.mult))
    vchain(
        "hch",
        lambda v: v.tensor_tensor(hstt[:, 3:4], hstt[:, 2:3], hstt[:, 2:3],
                                  ALU.mult),
        lambda v: v.scalar_tensor_tensor(hstt[:, 4:5], hstt[:, 1:2],
                                         1.0 / 1024.0, hstt[:, 3:4],
                                         ALU.mult, ALU.subtract),
        lambda v: v.tensor_scalar(hstt[:, 4:5], hstt[:, 4:5], EPS, None,
                                  ALU.add),
        lambda v: v.reciprocal(hstt[:, 5:6], hstt[:, 4:5]),
    )
    with BLOCK() as blk:
        @blk.scalar
        def _(s):
            P.wait(s, "hch")
            P.inc("hch2", s.activation(hstt[:, 4:5], hstt[:, 5:6], AF.Sqrt))

        @blk.vector
        def _(v):
            P.wait(v, "hch2")
            P.inc("hnv", v.tensor_scalar(xhn[:], xhb[:], hstt[:, 2:3],
                                         hstt[:, 4:5], ALU.subtract, ALU.mult))

    with BLOCK() as blk:
        @blk.vector
        def _(v):
            P.inc("hev", v.memset(xhT[:], 0.0))
    for fb in range(8):
        with BLOCK() as blk:
            @blk.tensor
            def _(t, fb=fb):
                P.wait(t, "hnv")
                P.wait(t, "hev", P.val("hev"))
                P.inc("htr", t.transpose(ps_xn[0][:, 0:3],
                                         xhn[:, fb * 128:(fb + 1) * 128],
                                         idbf[0:3, 0:3]))

            @blk.vector
            def _(v, fb=fb):
                P.wait(v, "htr")
                P.inc("hev", v.tensor_copy(xhT[:, fb * 4:fb * 4 + 3],
                                           ps_xn[0][:, 0:3]))
    with BLOCK() as blk:
        @blk.sync
        def _(sy):
            P.wait(sy, "hev")
            P.wait(sy, "xnh_wr", P.val("xnh_wr"))
            P.inc("xnh_wr", sy.dma_start(
                out=XNH.transpose([1, 0, 2]),
                in_=xhT[:].rearrange("p (f q) -> p f q", f=8)), 16)

    finalize_max("xn", 0, ["xng_done"], xn_tile_ctr[0])
    with BLOCK() as blk:
        @blk.sync
        def _(sy):
            P.wait(sy, "fm_xn_r")
            P.wait(sy, "cc0_wr", P.val("cc0_wr"))
            P.inc("cc0_wr", sy.dma_start(out=CC_I[0][0:1, 0:1],
                                         in_=mrow[0:1, 0:1]), 16)

        @blk.gpsimd
        def _(g):
            P.wait(g, "cc0_wr")
            P.inc("cc0", g.collective_compute(
                "AllReduce", ALU.max, replica_groups=[list(range(NCORES))],
                ins=[CC_I[0].opt()], outs=[CC_O[0].opt()]))

    def load_cc(idx, specs, ccsem):
        with BLOCK() as blk:
            @blk.sync
            def _(sy):
                P.wait(sy, ccsem)
                P.wait(sy, "scmm", P.val("scmm"))
                P.wait(sy, f"ccl{idx}", P.val(f"ccl{idx}"))
                P.inc(f"ccl{idx}", sy.dma_start(
                    out=ccs[0:1, 8:8 + CC_O[idx].shape[1]],
                    in_=CC_O[idx]), 16)
        for i, (srcc, scs, scr, fac) in enumerate(specs):
            with BLOCK() as blk:
                @blk.vector
                def _(v, i=i, srcc=srcc, fac=fac):
                    P.wait(v, f"ccl{idx}")
                    P.wait(v, f"ccv{idx}", P.val(f"ccv{idx}"))
                    P.inc(f"ccv{idx}", v.tensor_scalar(
                        ccs[0:1, 2 * i:2 * i + 1], ccs[0:1, 8 + srcc:9 + srcc],
                        fac / QMAX, 1e-12, ALU.mult, ALU.max))
            with BLOCK() as blk:
                @blk.vector
                def _(v, i=i):
                    P.wait(v, f"ccv{idx}")
                    P.inc(f"ccv{idx}", v.reciprocal(
                        ccs[0:1, 2 * i + 1:2 * i + 2],
                        ccs[0:1, 2 * i:2 * i + 1]))
            bcast_scal([(2 * i, scs), (2 * i + 1, scr)], [f"ccv{idx}"])

    load_cc(0, [(0, SC_S0, SC_R0, 1.0)], "cc0")

    ph2.close()

    ph4 = ExitStack()
    xq_sb = sb("xq", [128, 8 * T], BF16, ph4)
    xcv = sb("xcv", [128, 17 * 520], BF16, ph4)
    xnst = xcv
    wtb = [sb(f"wtb{i}", [128, 1024], BF16, ph4) for i in range(2)]
    sp_t = [sb(f"sp_t{i}", [128, 512], BF16, ph4) for i in range(2)]
    ps_ug = [pst(f"ps_ug{i}", [128, 512], F32, ph4) for i in range(2)]

    for fb in range(8):
        with BLOCK() as blk:
            @blk.sync
            def _(sy, fb=fb):
                if fb >= 1:
                    P.wait(sy, "xqv", fb)
                P.wait(sy, "xnt_wr")
                P.wait(sy, "xnl", P.val("xnl"))
                P.inc("xnl", sy.dma_start(
                    out=xnst[:, 0:T].rearrange("p (n q) -> p n q", q=128),
                    in_=XNT[fb, :, :, :].transpose([1, 0, 2])), 16)

            @blk.vector
            def _(v, fb=fb):
                P.wait(v, "xnl")
                P.wait(v, "scbc")
                P.inc("xqv", v._custom_dve(
                    QROUND, out=xq_sb[:, fb * T:(fb + 1) * T],
                    in0=xnst[:, 0:T],
                    s0=scal[:, SC_R0:SC_R0 + 1], s1=scal[:, SC_S0:SC_S0 + 1],
                    imm2=MAGIC))

    ugctr = {"up": [0], "ga": [0]}
    UGIT = [0]

    def ug_m(gidx):
        wi = gidx // 24
        kind, SP, base = (("up", UPRAW, AGW_UP),
                          ("ga", GRAW, AGW_GATE))[wi]
        m = gidx % 24
        if True:
            rank, m3 = m // 3, m % 3
            bi = m % 2
            with BLOCK() as blk:
                @blk.sync
                def _(sy, rank=rank, m3=m3, bi=bi, base=base, wi=wi, m=m):
                    P.wait(sy, f"wtbf{bi}", P.val(f"wtbf{bi}"))
                    if wi == 0 and m == 0:
                        P.wait(sy, "ccagw")
                    off = base + m3 * 8 * TIL
                    P.wait(sy, f"wtb{bi}", P.val(f"wtb{bi}"))
                    P.inc(f"wtb{bi}", sy.dma_start(
                        out=wtb[bi][:].rearrange("p (k q) -> p k q", k=8),
                        in_=AGW_O[rank, off:off + 8 * TIL].rearrange(
                            "(k p q) -> k p q", k=8, p=128).transpose(
                                [1, 0, 2])), 16)
            for n in range(NCH):
                slot = UGIT[0] % 2
                pslot = UGIT[0] % 2
                with BLOCK() as blk:
                    @blk.tensor
                    def _(t, bi=bi, slot=slot, pslot=pslot, n=n, m=m):
                        P.wait(t, f"wtb{bi}")
                        P.wait(t, "xqv", 8)
                        P.wait(t, "ugsp0", max(0, P.val("ugsp0") - 1))
                        P.wait(t, "xbev", P.val("xbev"))
                        last = None
                        for kbi in range(8):
                            last = t.matmul(
                                ps_ug[pslot][:],
                                wtb[bi][:, kbi * 128:(kbi + 1) * 128],
                                xq_sb[:, kbi * T + n * CH:kbi * T + (n + 1) * CH],
                                start=(kbi == 0), stop=(kbi == 7))
                        P.inc("ugmm0", last)
                        if n == NCH - 1:
                            P.inc(f"wtbf{bi}", last)

                    @blk.scalar
                    def _(s, slot=slot, pslot=pslot, kind=kind):
                        P.wait(s, "ugmm0")
                        P.wait(s, "ugwr", max(0, P.val("ugwr") - 16))
                        P.wait(s, "amx0", max(0, P.val("amx0") - 1))
                        P.inc("ugsp0", s.copy(sp_t[slot][:], ps_ug[pslot][:]))

                    @blk.sync
                    def _(sy, m=m, n=n, slot=slot, SP=SP):
                        P.wait(sy, "ugsp0")
                        P.wait(sy, "ugwr", P.val("ugwr"))
                        P.inc("ugwr", sy.dma_start(out=SP[m, n, :, :],
                                                   in_=sp_t[slot][:]), 16)

                    @blk.vector
                    def _(v, slot=slot, kind=kind):
                        P.wait(v, "ugsp0")
                        accmax(v, kind, ugctr[kind][0], sp_t[slot][:])
                        ugctr[kind][0] += 1
                UGIT[0] += 1

    def emit_m1m2():
        finalize_max("up", 1, [], ugctr["up"][0])
        finalize_max("ga", 2, [], ugctr["ga"][0])
        with BLOCK() as blk:
            @blk.sync
            def _(sy):
                P.wait(sy, "fm_up_r")
                P.wait(sy, "fm_ga_r")
                P.wait(sy, "cc1_wr", P.val("cc1_wr"))
                P.inc("cc1_wr", sy.dma_start(out=CC_I[1][0:1, 0:2],
                                             in_=mrow[0:1, 1:3]), 16)

            @blk.gpsimd
            def _(g):
                P.wait(g, "cc1_wr")
                P.inc("cc1", g.collective_compute(
                    "AllReduce", ALU.max,
                    replica_groups=[list(range(NCORES))],
                    ins=[CC_I[1].opt()], outs=[CC_O[1].opt()]))

        load_cc(1, [(0, SC_S1, SC_R1, 1.0), (1, SC_S2, SC_R2, 1.0)], "cc1")

    xnc = sb("xnc", [128, 8 * CH], BF16, ph4)
    xnh_sb = sb("xnh_sb", [128, 32], BF16, ph4)
    wzb = [sb(f"wzb{i}", [128, 4 * 512], BF16, ph4) for i in range(3)]
    wxb = [sb(f"wxb{i}", [128, 8 * 128], BF16, ph4) for i in range(2)]
    cacc = sb("cacc", [128, 4 * CH], F32, ph4)
    xcs = sb("xcs", [128, 17 * CH], BF16, ph4)
    zsb = sb("zsb", [128, 1024], BF16, ph4)
    dtc = sb("dtc", [32, CH], F32, ph4)
    sdt = sb("sdt", [32, CH], F32, ph4)
    slc = sb("slc", [32, CH], F32, ph4)
    sgoff = sb("sgoff", [32, 2], F32, ph4)
    eslc = sb("eslc", [32, CH], F32, ph4)
    ones32 = sb("ones32", [32, 128], F32, ph4)
    sro = sb("sro", [1, 4096], F32, ph4)
    dtT = sb("dtT", [128, 32], F32, ph4)
    eslT = sb("eslT", [128, 32], F32, ph4)
    egl = sb("egl", [128, 32], BF16, ph4)
    darow = sb("darow", [1, 32], F32, ph4)
    dat = sb("dat", [64, 32], F32, ph4)
    e4b2 = [sb(f"e4b{i}", [128, 512], BF16, ph4) for i in range(2)]
    g4b2 = [sb(f"g4b{i}", [128, 512], BF16, ph4) for i in range(2)]
    bcs = sb("bcs", [128, 512], BF16, ph4)
    xtok = sb("xtok", [128, D_INNER], BF16, ph4)
    xsc = sb("xsc", [128, D_INNER], BF16, ph4)
    btok = sb("btok", [128, 64], BF16, ph4)
    csb = sb("csb", [64, CH], BF16, ph4)
    bsc = sb("bsc", [128, D_INNER], BF16, ph4)
    hcur = sb("hcur0", [64, D_INNER], F32, ph4)
    hbf = sb("hbf0", [64, D_INNER], BF16, ph4)
    ylsb = sb("ylsb", [128, D_INNER], BF16, ph4)
    yint = sb("yint", [128, 512], F32, ph4)
    ps_zz = pst("ps_zz", [128, 2048], F32, ph4)

    with BLOCK() as blk:
        @blk.vector
        def _(v):
            P.wait(v, "xnh_wr")
            P.wait(v, "xqv", 8)
            v.memset(hcur[:], 0.0)
            v.memset(hbf[:], 0.0)
            v.memset(sgoff[:], 0.0)
            v.memset(ones32[:], 1.0)
            P.inc("mb_init", v.memset(xcv[:], 0.0))

        @blk.sync
        def _(sy):
            P.wait(sy, "xnh_wr")
            P.wait(sy, "xnh_ld", P.val("xnh_ld"))
            P.inc("xnh_ld", sy.dma_start(
                out=xnh_sb[:].rearrange("p (f q) -> p f q", f=8),
                in_=XNH.transpose([1, 0, 2])), 16)

    UGPC = 8
    for n in range(NCH):
        for gidx in range(n * UGPC, min(48, (n + 1) * UGPC)):
            ug_m(gidx)
        if n == min(6, NCH - 1):
            emit_m1m2()
        with BLOCK() as blk:
            @blk.sync
            def _(sy, n=n):
                P.wait(sy, "xncf", P.val("xncf"))
                P.wait(sy, "xnt_wr")
                for f in range(8):
                    P.wait(sy, "xncl", P.val("xncl"))
                    P.inc("xncl", sy.dma_start(
                        out=xnc[:, f * 512:(f + 1) * 512].rearrange(
                            "p (j q) -> p j q", j=4),
                        in_=XNT[f, n * 4:(n + 1) * 4, :, :]
                        .transpose([1, 0, 2])), 16)

        for zb in range(4):
            zs0 = (n * 8 + zb * 2) % 3
            zs1 = (n * 8 + zb * 2 + 1) % 3
            with BLOCK() as blk:
                @blk.sync
                def _(sy, zb=zb, n=n, zs0=zs0, zs1=zs1):
                    P.wait(sy, "wzbf0", max(0, P.val("wzbf0") - 1))
                    P.wait(sy, "winb_wr")
                    for hf, zs_ in ((0, zs0), (1, zs1)):
                        off = ZT0 + (zb * 8 + hf * 4) * (128 * 512)
                        P.wait(sy, f"wzbl{hf}", P.val(f"wzbl{hf}"))
                        P.inc(f"wzbl{hf}", sy.dma_start(
                            out=wzb[zs_][:].rearrange("p (k q) -> p k q", k=4),
                            in_=WINB[0:1, off:off + 4 * 128 * 512].rearrange(
                                "o (k p q) -> k (o p) q", k=4, p=128)
                            .transpose([1, 0, 2])), 16)

                @blk.tensor
                def _(t, zb=zb, n=n, zs0=zs0, zs1=zs1):
                    P.wait(t, "wzbl0")
                    P.wait(t, "wzbl1")
                    P.wait(t, "xncl")
                    P.wait(t, "zpe", P.val("zpe"))
                    P.wait(t, "ype", P.val("ype"))
                    P.wait(t, "dpe", P.val("dpe"))
                    P.wait(t, "bcev", P.val("bcev"))
                    last = None
                    for tt in range(4):
                        for kbi in range(8):
                            last = t.matmul(
                                ps_zz[:, tt * 512:(tt + 1) * 512],
                                xnc[:, kbi * CH + tt * 128:
                                    kbi * CH + (tt + 1) * 128],
                                wzb[(zs0, zs1)[kbi // 4]][:, (kbi % 4) * 512:
                                                          (kbi % 4 + 1) * 512],
                                start=(kbi == 0), stop=(kbi == 7))
                    P.inc("zmm", last)
                    P.inc("wzbf0", last)

                @blk.scalar
                def _(s, zb=zb, n=n, zwbase=None):
                    P.wait(s, "zmm")
                    zwbase = P.val("zswr")
                    for tt in range(4):
                        P.wait(s, "zswr",
                               zwbase if tt < 2 else zwbase + 16 * (tt - 1))
                        P.inc("zev", s.activation(
                            zsb[:, (tt % 2) * 512:(tt % 2 + 1) * 512],
                            ps_zz[:, tt * 512:(tt + 1) * 512],
                            AF.Silu))

                @blk.sync
                def _(sy, zb=zb, n=n):
                    for tt in range(4):
                        P.wait(sy, "zev", P.val("zev") - 3 + tt)
                        P.wait(sy, "zswr", P.val("zswr"))
                        P.inc("zswr", sy.dma_start(
                            out=ZS[n * 4 + tt, zb, :, :],
                            in_=zsb[:, (tt % 2) * 512:(tt % 2 + 1) * 512]), 16)

        for ob in range(18):
            obi = ob % 2
            is_dt = (ob == 17)
            with BLOCK() as blk:
                @blk.sync
                def _(sy, ob=ob, obi=obi, is_dt=is_dt):
                    P.wait(sy, f"wxbf{obi}", P.val(f"wxbf{obi}"))
                    P.wait(sy, "winb_wr")
                    if is_dt:
                        off = DT0
                        sz = 32
                    else:
                        off = XT0 + ob * 8 * TIL
                        sz = 128
                    P.wait(sy, f"wxbl{obi}", P.val(f"wxbl{obi}"))
                    P.inc(f"wxbl{obi}", sy.dma_start(
                        out=wxb[obi][:, 0:8 * sz].rearrange(
                            "p (k q) -> p k q", k=8),
                        in_=WINB[0:1, off:off + 8 * 128 * sz].rearrange(
                            "o (k p q) -> k (o p) q", k=8, p=128).transpose(
                                [1, 0, 2])), 16)

                @blk.tensor
                def _(t, ob=ob, obi=obi, is_dt=is_dt, n=n):
                    P.wait(t, f"wxbl{obi}")
                    P.wait(t, "xncl")
                    P.wait(t, "xbpe", P.val("xbpe"))
                    P.wait(t, "ugsp0", P.val("ugsp0"))
                    sz = 32 if is_dt else 128
                    last = None
                    for kbi in range(8):
                        last = t.matmul(
                            ps_ug[obi][0:sz, :],
                            wxb[obi][:, kbi * sz:(kbi + 1) * sz],
                            xnc[:, kbi * CH:(kbi + 1) * CH],
                            start=(kbi == 0), stop=(kbi == 7))
                    P.inc("xbmm", last)
                    P.inc(f"wxbf{obi}", last)
                    if is_dt:
                        P.inc("xncf", last)
                    if n == 0 and not is_dt:
                        P.wait(t, "halope", P.val("halope"))
                        last2 = None
                        for kbi in range(8):
                            last2 = t.matmul(
                                ps_zz[:, 2044:2047],
                                wxb[obi][:, kbi * 128:(kbi + 1) * 128],
                                xnh_sb[:, kbi * 4:kbi * 4 + 3],
                                start=(kbi == 0), stop=(kbi == 7))
                        P.inc("halomm", last2)

                @blk.vector
                def _(v, ob=ob, obi=obi, is_dt=is_dt, n=n):
                    P.wait(v, "xbmm")
                    P.wait(v, "mb_init")
                    if is_dt:
                        P.wait(v, "sg_wr", P.val("sg_wr"))
                        last = v.tensor_copy(sdt[:, 0:CH], ps_ug[obi][0:32, :])
                        P.inc("dtraw", last)
                    else:
                        last = v.tensor_copy(
                            xcv[:, ob * 520 + 3:ob * 520 + 3 + CH],
                            ps_ug[obi][:])
                        P.inc("xbev", last)
                    P.inc("xbpe", last)
                    if n == 0 and not is_dt:
                        P.wait(v, "halomm")
                        last2 = v.tensor_copy(
                            xcv[:, ob * 520:ob * 520 + 3],
                            ps_zz[:, 2044:2047])
                        P.inc("haloev", last2)
                        P.inc("halope", last2)

        with BLOCK() as blk:
            @blk.scalar
            def _(s, n=n):
                P.wait(s, "dtraw")
                s.activation(dtc[:], sdt[:], AF.Abs, bias=dtbias[:, 0:1])
                P.inc("dts1", s.activation(slc[:], sdt[:], AF.Relu,
                                           bias=dtbias[:, 0:1]))
        with BLOCK() as blk:
            @blk.scalar
            def _(s):
                P.wait(s, "dts1")
                P.inc("dts2", s.activation(dtc[:], dtc[:], AF.Exp, scale=-1.0))
        with BLOCK() as blk:
            @blk.scalar
            def _(s):
                P.wait(s, "dts2")
                P.inc("dts3", s.activation(dtc[:], dtc[:], AF.Ln, bias=1.0))
        with BLOCK() as blk:
            @blk.vector
            def _(v):
                P.wait(v, "dts3")
                P.inc("dtv1", v.tensor_tensor(dtc[:], dtc[:], slc[:], ALU.add))
        with BLOCK() as blk:
            @blk.vector
            def _(v):
                P.wait(v, "dtv1")
                P.inc("dtv2", v.tensor_scalar(sdt[:], dtc[:], aneg[:, 0:1],
                                              None, ALU.mult))
        with BLOCK() as blk:
            @blk.vector
            def _(v):
                P.wait(v, "dtv2")
                P.wait(v, "sg_wr", P.val("sg_wr"))
                last = None
                for q in range(4):
                    last = v.tensor_tensor_scan(
                        slc[:, q * 128:(q + 1) * 128],
                        ones32[:, 0:128],
                        sdt[:, q * 128:(q + 1) * 128], 0.0,
                        ALU.mult, ALU.add)
                P.inc("dtv3", last)
        with BLOCK() as blk:
            @blk.vector
            def _(v):
                P.wait(v, "dtv3")
                P.wait(v, "dtv4")
                last = None
                for q in range(4):
                    P.inc("dtv4", v.tensor_scalar(
                        sdt[:, q * 128:(q + 1) * 128],
                        slc[:, q * 128:(q + 1) * 128],
                        sgoff[:, 0:1], None, ALU.add))
                    P.wait(v, "dtv4")
                    P.inc("dtv4", v.tensor_copy(
                        sgoff[:, 0:1],
                        sdt[:, q * 128 + 127:q * 128 + 128]))
                    P.wait(v, "dtv4")
        with BLOCK() as blk:
            @blk.scalar
            def _(s):
                P.wait(s, "dtv4")
                P.inc("esl_s", s.activation(eslc[:], slc[:], AF.Exp))

            @blk.sync
            def _(sy, n=n):
                P.wait(sy, "dtv4")
                P.wait(sy, "sg_wr", P.val("sg_wr"))
                P.inc("sg_wr", sy.dma_start(out=SGH[:, n * CH:(n + 1) * CH],
                                            in_=sdt[:]), 16)
                P.wait(sy, "sg_wr", P.val("sg_wr"))
                P.inc("sg_wr", sy.dma_start(out=SLOC[:, n * CH:(n + 1) * CH],
                                            in_=slc[:]), 16)


        for grp, (p0, np_) in enumerate(
                ((16, 1), (0, 4), (4, 4), (8, 4), (12, 4))):
            csem = f"cvt{grp}"
            with BLOCK() as blk:
                @blk.vector
                def _(v, p0=p0, np_=np_, csem=csem, n=n):
                    P.wait(v, "xbev", P.val("xbev"))
                    if n == 0:
                        P.wait(v, "haloev", P.val("haloev"))
                    P.wait(v, csem, P.val(csem))
                    last = None
                    for pb2 in range(np_):
                        ob = p0 + pb2
                        last = v.tensor_scalar(
                            cacc[:, pb2 * CH:(pb2 + 1) * CH],
                            xcv[:, ob * 520 + 0:ob * 520 + CH],
                            convw[:, ob * 4:ob * 4 + 1], None, ALU.mult)
                    P.inc(csem, last)
            for k in range(1, 4):
                with BLOCK() as blk:
                    @blk.vector
                    def _(v, p0=p0, np_=np_, k=k, csem=csem):
                        P.wait(v, csem)
                        last = None
                        for pb2 in range(np_):
                            ob = p0 + pb2
                            last = v.scalar_tensor_tensor(
                                cacc[:, pb2 * CH:(pb2 + 1) * CH],
                                xcv[:, ob * 520 + k:ob * 520 + k + CH],
                                convw[:, ob * 4 + k:ob * 4 + k + 1],
                                cacc[:, pb2 * CH:(pb2 + 1) * CH],
                                ALU.mult, ALU.add)
                        P.inc(csem, last)
            with BLOCK() as blk:
                @blk.scalar
                def _(s, p0=p0, np_=np_, csem=csem):
                    P.wait(s, csem)
                    last = None
                    for pb2 in range(np_):
                        ob = p0 + pb2
                        last = s.activation(
                            xcs[:, ob * CH:(ob + 1) * CH],
                            cacc[:, pb2 * CH:(pb2 + 1) * CH],
                            AF.Silu, bias=convb[:, ob:ob + 1])
                    P.inc(csem, last)
                    P.inc("xcsev", last)

        with BLOCK() as blk:
            @blk.sync
            def _(sy, n=n):
                P.wait(sy, "xcsev")
                P.wait(sy, "csp_wr", P.val("csp_wr"))
                P.inc("csp_wr", sy.dma_start(
                    out=CSP[:, n * CH:(n + 1) * CH],
                    in_=xcs[64:128, 16 * CH:17 * CH]), 16)
                P.wait(sy, "bcev", P.val("bcev"))
                P.wait(sy, "ymm", P.val("ymm"))
                P.wait(sy, "cloc", P.val("cloc"))
                P.inc("cloc", sy.dma_start(
                    out=csb[:], in_=xcs[64:128, 16 * CH:17 * CH]), 16)

            @blk.vector
            def _(v, n=n):
                P.wait(v, "xcsev")
                P.inc("tailcp", v.tensor_copy(
                    xcv[:].rearrange("p (o c) -> p o c", c=520)[:, :, 0:3],
                    xcv[:].rearrange("p (o c) -> p o c", c=520)[:, :,
                                                               CH:CH + 3]))
        with BLOCK() as blk:
            @blk.vector
            def _(v):
                P.wait(v, "tailcp")
                P.inc("xbpe", v.tensor_copy(sgoff[:, 1:2], sgoff[:, 0:1]))

        with BLOCK() as blk:
            @blk.tensor
            def _(t, n=n):
                P.wait(t, "xcsev", P.val("xcsev"))
                P.wait(t, "bcse", P.val("bcse"))
                P.wait(t, "xtev", P.val("xtev"))
                P.wait(t, "btev", P.val("btev"))
                P.wait(t, "cloc")
                P.wait(t, "datv", P.val("datv"))
                last = None
                for q in range(4):
                    last = t.matmul(
                        ps_zz[:, 1536 + q * 128:1536 + (q + 1) * 128],
                        xcs[0:64, 16 * CH + q * 128:16 * CH + (q + 1) * 128],
                        csb[:, q * 128:(q + 1) * 128],
                        start=(q == 0), stop=(q == 3))
                P.inc("bcmm", last)

            @blk.vector
            def _(v):
                P.wait(v, "bcmm")
                last = v.tensor_copy(bcs[:], ps_zz[:, 1536:2048])
                P.inc("bcev", last)
                P.inc("bcse", last)

        for q in range(4):
            _ssd_q(locals(), n, q)

    with BLOCK() as blk:
        @blk.sync
        def _(sy):
            P.wait(sy, "hupd", P.val("hupd"))
            P.wait(sy, "hpub_wr", P.val("hpub_wr"))
            P.inc("hpub_wr", sy.dma_start(
                out=HPUB[0:1, :].rearrange("o (p q) -> p (o q)", p=64),
                in_=hcur[:]), 16)

        @blk.gpsimd
        def _(g):
            P.wait(g, "hpub_wr")
            P.inc("ccah", g.collective_compute(
                "AllGather", ALU.bypass, replica_groups=[list(range(NCORES))],
                ins=[HPUB.opt()], outs=[HALL.opt()]))

    ph4.close()

    fixup_tile, outproj, ph7 = _build_fixup(locals())

    ph5 = ExitStack()
    c1g = [sb(f"c1g{i}", [128, 2048], BF16, ph5) for i in range(2)]
    c1q = sb("c1q", [128, 2048], F32, ph5)
    c1s = sb("c1s", [128, 2048], F32, ph5)
    c1t = [sb(f"c1t{i}", [128, 2048], BF16, ph5) for i in range(2)]

    t3ctr = [0]
    FIX_CTR = [0]
    gsz = min(4, NCH)
    GW = gsz * 512
    strips = [(m_, ng_) for m_ in range(24) for ng_ in range(0, NCH, gsz)]
    for st_i, (m0_, n0_) in enumerate(strips):
        bi = st_i % 2
        with BLOCK() as blk:
            @blk.sync
            def _(sy, bi=bi, m0_=m0_, n0_=n0_):
                P.wait(sy, "c1qv", P.val("c1qv"))
                P.wait(sy, "ugwr")
                P.wait(sy, f"c1gl{bi}", P.val(f"c1gl{bi}"))
                P.inc(f"c1gl{bi}", sy.dma_start(
                    out=c1g[bi][:, 0:GW].rearrange("p (j q) -> p j q", j=gsz),
                    in_=GRAW[m0_, n0_:n0_ + gsz, :, :].transpose([1, 0, 2])),
                    16)
        with BLOCK() as blk:
            @blk.vector
            def _(v, bi=bi):
                P.wait(v, f"c1gl{bi}")
                P.wait(v, "scbc")
                P.wait(v, "c1qf", P.val("c1qf"))
                P.inc("c1qv", v._custom_dve(
                    QROUND, out=c1q[:, 0:GW], in0=c1g[bi][:, 0:GW],
                    s0=scal[:, SC_R2:SC_R2 + 1], s1=1.0, imm2=MAGIC))

            @blk.scalar
            def _(s):
                P.wait(s, "c1qv")
                P.inc("c1sig", s.activation(c1s[:, 0:GW], c1q[:, 0:GW],
                                            AF.Sigmoid,
                                            scale=scal[:, SC_S2:SC_S2 + 1]))
        with BLOCK() as blk:
            @blk.vector
            def _(v, bi=bi):
                P.wait(v, "c1sig")
                P.wait(v, "t3wr", P.val("t3wr"))
                for sl in range(NSLOT):
                    P.wait(v, f"amx{sl}", P.val(f"amx{sl}"))
                last = v._custom_dve(
                    QROUND2, out=c1t[bi][:, 0:GW], in0=c1q[:, 0:GW],
                    in1=c1s[:, 0:GW], s0=1.0, s1=65536.0, imm2=MAGIC)
                P.inc("c1tv", last)
                P.inc("c1qf", last)

            @blk.sync
            def _(sy, bi=bi, m0_=m0_, n0_=n0_):
                P.wait(sy, "c1tv")
                P.wait(sy, "t3wr", P.val("t3wr"))
                P.inc("t3wr", sy.dma_start(
                    out=T3S[m0_, n0_:n0_ + gsz, :, :].transpose([1, 0, 2]),
                    in_=c1t[bi][:, 0:GW].rearrange("p (j q) -> p j q", j=gsz)),
                    16)

            @blk.vector
            def _(v, bi=bi):
                P.wait(v, "c1tv")
                for j in range(gsz):
                    accmax(v, "t3", t3ctr[0],
                           c1t[bi][:, j * 512:(j + 1) * 512])
                    t3ctr[0] += 1
    finalize_max("t3", 3, [], t3ctr[0])
    with BLOCK() as blk:
        @blk.sync
        def _(sy):
            P.wait(sy, "fm_t3_r")
            P.wait(sy, "cc2_wr", P.val("cc2_wr"))
            P.inc("cc2_wr", sy.dma_start(out=CC_I[2][0:1, 0:1],
                                         in_=mrow[0:1, 3:4]), 16)

        @blk.gpsimd
        def _(g):
            P.wait(g, "cc2_wr")
            P.inc("cc2", g.collective_compute(
                "AllReduce", ALU.max, replica_groups=[list(range(NCORES))],
                ins=[CC_I[2].opt()], outs=[CC_O[2].opt()]))

    with BLOCK() as blk:
        @blk.sync
        def _(sy):
            P.wait(sy, "cc2")
            P.wait(sy, "scmm", P.val("scmm"))
            P.wait(sy, "ccl2", P.val("ccl2"))
            P.inc("ccl2", sy.dma_start(out=ccs[0:1, 8:9], in_=CC_O[2]), 16)
    with BLOCK() as blk:
        @blk.vector
        def _(v):
            P.wait(v, "ccl2")
            P.inc("c2v", v.tensor_scalar(ccs[0:1, 10:11], ccs[0:1, 2:3],
                                         1.0 / 65536.0, None, ALU.mult))
    with BLOCK() as blk:
        @blk.vector
        def _(v):
            P.wait(v, "c2v")
            P.inc("c2v", v.tensor_tensor(ccs[0:1, 11:12], ccs[0:1, 8:9],
                                         ccs[0:1, 10:11], ALU.mult))
    with BLOCK() as blk:
        @blk.vector
        def _(v):
            P.wait(v, "c2v")
            P.inc("c2v", v.tensor_scalar(ccs[0:1, 4:5], ccs[0:1, 11:12],
                                         1.0 / QMAX, 1e-12, ALU.mult, ALU.max))
    with BLOCK() as blk:
        @blk.vector
        def _(v):
            P.wait(v, "c2v")
            P.inc("c2v", v.reciprocal(ccs[0:1, 5:6], ccs[0:1, 4:5]))
    with BLOCK() as blk:
        @blk.vector
        def _(v):
            P.wait(v, "c2v")
            P.inc("c2v", v.tensor_tensor(ccs[0:1, 6:7], ccs[0:1, 10:11],
                                         ccs[0:1, 5:6], ALU.mult))
    bcast_scal([(6, SC_F3R3), (4, SC_S3S1R4X)], ["c2v"])
    ph5.close()

    ph6 = ExitStack()
    c2a = [sb(f"c2a{i}", [128, 2048], BF16, ph6) for i in range(2)]
    c2b = [sb(f"c2b{i}", [128, 2048], BF16, ph6) for i in range(2)]
    c2t = [sb(f"c2t{i}", [128, 2048], BF16, ph6) for i in range(2)]
    t4ctr = [0]
    for st_i, (m0_, n0_) in enumerate(strips):
        bi = st_i % 2
        with BLOCK() as blk:
            @blk.sync
            def _(sy, bi=bi, m0_=m0_, n0_=n0_):
                P.wait(sy, "c2tv", P.val("c2tv"))
                P.wait(sy, "t3wr")
                P.wait(sy, f"c2al{bi}", P.val(f"c2al{bi}"))
                P.inc(f"c2al{bi}", sy.dma_start(
                    out=c2a[bi][:, 0:GW].rearrange("p (j q) -> p j q", j=gsz),
                    in_=T3S[m0_, n0_:n0_ + gsz, :, :].transpose([1, 0, 2])), 16)
                P.wait(sy, f"c2al{bi}", P.val(f"c2al{bi}"))
                P.inc(f"c2al{bi}", sy.dma_start(
                    out=c2b[bi][:, 0:GW].rearrange("p (j q) -> p j q", j=gsz),
                    in_=UPRAW[m0_, n0_:n0_ + gsz, :, :].transpose([1, 0, 2])),
                    16)
        with BLOCK() as blk:
            @blk.vector
            def _(v, bi=bi):
                P.wait(v, f"c2al{bi}")
                P.wait(v, "scbc")
                P.wait(v, "t4wr", P.val("t4wr"))
                for sl in range(NSLOT):
                    P.wait(v, f"amx{sl}", P.val(f"amx{sl}"))
                last = v._custom_dve(
                    QROUND2, out=c2t[bi][:, 0:GW], in0=c2a[bi][:, 0:GW],
                    in1=c2b[bi][:, 0:GW],
                    s0=scal[:, SC_F3R3:SC_F3R3 + 1],
                    s1=scal[:, SC_R1:SC_R1 + 1], imm2=MAGIC)
                P.inc("c2tv", last)

            @blk.sync
            def _(sy, bi=bi, m0_=m0_, n0_=n0_):
                P.wait(sy, "c2tv")
                P.wait(sy, "t4wr", P.val("t4wr"))
                P.inc("t4wr", sy.dma_start(
                    out=T4S[m0_, n0_:n0_ + gsz, :, :].transpose([1, 0, 2]),
                    in_=c2t[bi][:, 0:GW].rearrange("p (j q) -> p j q", j=gsz)),
                    16)

            @blk.vector
            def _(v, bi=bi):
                P.wait(v, "c2tv")
                for j in range(gsz):
                    accmax(v, "t4", t4ctr[0],
                           c2t[bi][:, j * 512:(j + 1) * 512])
                    t4ctr[0] += 1

    finalize_max("t4", 4, [], t4ctr[0])
    with BLOCK() as blk:
        @blk.sync
        def _(sy):
            P.wait(sy, "fm_t4_r")
            P.wait(sy, "cc3_wr", P.val("cc3_wr"))
            P.inc("cc3_wr", sy.dma_start(out=CC_I[3][0:1, 0:1],
                                         in_=mrow[0:1, 4:5]), 16)

        @blk.gpsimd
        def _(g):
            P.wait(g, "cc3_wr")
            P.inc("cc3", g.collective_compute(
                "AllReduce", ALU.max, replica_groups=[list(range(NCORES))],
                ins=[CC_I[3].opt()], outs=[CC_O[3].opt()]))

    with BLOCK() as blk:
        @blk.sync
        def _(sy):
            P.wait(sy, "cc3")
            P.wait(sy, "scmm", P.val("scmm"))
            P.wait(sy, "ccl3", P.val("ccl3"))
            P.inc("ccl3", sy.dma_start(out=ccs[0:1, 8:9], in_=CC_O[3]), 16)
    with BLOCK() as blk:
        @blk.vector
        def _(v):
            P.wait(v, "ccl3")
            P.inc("c3v", v.tensor_tensor(ccs[0:1, 10:11], ccs[0:1, 4:5],
                                         ccs[0:1, 0:1], ALU.mult))
    with BLOCK() as blk:
        @blk.vector
        def _(v):
            P.wait(v, "c3v")
            P.inc("c3v", v.tensor_tensor(ccs[0:1, 11:12], ccs[0:1, 8:9],
                                         ccs[0:1, 10:11], ALU.mult))
    with BLOCK() as blk:
        @blk.vector
        def _(v):
            P.wait(v, "c3v")
            P.inc("c3v", v.tensor_scalar(ccs[0:1, 7:8], ccs[0:1, 11:12],
                                         1.0 / QMAX, 1e-12, ALU.mult, ALU.max))
    with BLOCK() as blk:
        @blk.vector
        def _(v):
            P.wait(v, "c3v")
            P.inc("c3v", v.reciprocal(ccs[0:1, 9:10], ccs[0:1, 7:8]))
    with BLOCK() as blk:
        @blk.vector
        def _(v):
            P.wait(v, "c3v")
            P.inc("c3v", v.tensor_tensor(ccs[0:1, 9:10], ccs[0:1, 10:11],
                                         ccs[0:1, 9:10], ALU.mult))
    bcast_scal([(9, SC_F4R4), (7, SC_S4)], ["c3v"])
    ph6.close()

    _build_final(locals(), fixup_tile, outproj)
    ph7.close()

    P._flush()
    _gblk_cm.__exit__(None, None, None)
    es.close()
    mybir.codegen_inst_isa_subclasses(nc)
    return nc


def _ssd_q(M, n, q):
    nc, P = M["nc"], M["P"]
    BLOCK = M["BLOCK"]
    sb, pst = M["sb"], M["pst"]
    idbf, idf32, trineg, ones_r, d32 = (M["idbf"], M["idf32"], M["trineg"],
                                        M["ones_r"], M["d32"])
    (xcs, bcs, e4b2, g4b2, xtok, xsc, btok, bsc, hcur, ylsb, yint, dtT,
     eslT, egl, sro, darow, dat, dtc, eslc, ps_zz, ps_tb, SLOC, YL,
     CH_, T) = (M["xcs"], M["bcs"], M["e4b2"], M["g4b2"], M["xtok"], M["xsc"],
                M["btok"], M["bsc"], M["hcur"], M["ylsb"], M["yint"],
                M["dtT"], M["eslT"], M["egl"], M["sro"], M["darow"], M["dat"],
                M["dtc"], M["eslc"], M["ps_zz"], M["ps_tb"],
                M["SLOC"], M["YL"], CH, M["T"])
    csb = M["csb"]
    hbf = M["hbf"]
    negr = M["negr"]
    qoff = 16 * CH + q * 128
    upd = n * 4 + q
    hsrc = hcur
    hdst = hcur
    gt = n * CH + q * 128

    with BLOCK() as blk:
        @blk.sync
        def _(sy):
            P.wait(sy, "sg_wr")
            P.wait(sy, "srof", P.val("srof"))
            P.wait(sy, "srol", P.val("srol"))
            P.inc("srol", sy.dma_start(
                out=sro[0:1, :].rearrange("o (h t) -> o h t", h=32),
                in_=SLOC[:, gt:gt + 128]), 16)

    for g4 in range(4):
        with BLOCK() as blk:
            @blk.tensor
            def _(t, g4=g4):
                P.wait(t, "xcsev", P.val("xcsev"))
                P.wait(t, "xtev", P.val("xtev"))
                P.wait(t, "bcev", P.val("bcev"))
                last = None
                for j in range(4):
                    db = g4 * 4 + j
                    last = t.transpose(
                        ps_tb[:, j * 128:(j + 1) * 128],
                        xcs[:, db * CH + q * 128:db * CH + (q + 1) * 128],
                        idbf[:])
                P.inc("xttr", last)

            @blk.vector
            def _(v, g4=g4):
                P.wait(v, "xttr")
                P.inc("xtev", v.tensor_copy(xtok[:, g4 * 512:(g4 + 1) * 512],
                                            ps_tb[:]))

    with BLOCK() as blk:
        @blk.tensor
        def _(t):
            P.wait(t, "xtev", P.val("xtev"))
            P.inc("bttr", t.transpose(ps_tb[:, 0:64],
                                      xcs[0:64, qoff:qoff + 128],
                                      idbf[0:64, 0:64]))

        @blk.vector
        def _(v):
            P.wait(v, "bttr")
            P.inc("btev", v.tensor_copy(btok[:], ps_tb[:, 0:64]))

    with BLOCK() as blk:
        @blk.tensor
        def _(t):
            P.wait(t, "dtv1", P.val("dtv1"))
            P.wait(t, "esl_s", P.val("esl_s"))
            P.wait(t, "ftev", P.val("ftev"))
            P.wait(t, "dpe", P.val("dpe"))
            P.wait(t, "ht2", P.val("ht2"))
            t.transpose(ps_zz[:, 0:32], dtc[:, q * 128:(q + 1) * 128],
                        idf32[0:32, 0:32])
            P.inc("fttr", t.transpose(ps_zz[:, 32:64],
                                      eslc[:, q * 128:(q + 1) * 128],
                                      idf32[0:32, 0:32]))

        @blk.vector
        def _(v):
            P.wait(v, "fttr")
            v.tensor_copy(dtT[:], ps_zz[:, 0:32])
            P.inc("ftev", v.tensor_copy(eslT[:], ps_zz[:, 32:64]))

    with BLOCK() as blk:
        @blk.vector
        def _(v):
            P.wait(v, "xtev", P.val("xtev"))
            P.wait(v, "ftev", P.val("ftev"))
            P.wait(v, "dx_v")
            P.inc("xscv", v.tensor_tensor(
                xsc[:].rearrange("p (h j) -> p h j", h=32),
                xtok[:].rearrange("p (h j) -> p h j", h=32),
                dtT[:].unsqueeze(2).broadcast_to([128, 32, 64]),
                ALU.mult))

    if "GBLD_BASE" not in M:
        M["GBLD_BASE"] = [0]
    GBLD_BASE = M["GBLD_BASE"]
    GBLD_BASE[0] = P.val("gbld")
    YMM_BASE = P.val("ymm")

    def emit_D(grp):
        h0 = grp * 4
        par = grp % 2
        e4b = e4b2[par]
        g4b = g4b2[par]
        dbank = ps_zz[:, 0:512] if par == 0 else ps_zz[:, 1536:2048]
        with BLOCK() as blk:
            @blk.tensor
            def _(t, grp=grp, h0=h0, dbank=dbank):
                P.wait(t, "srol")
                P.wait(t, "dpe", max(0, P.val("dpe") - 1))
                if grp == 0:
                    P.wait(t, "xtev", P.val("xtev"))
                    P.wait(t, "ht2", P.val("ht2"))
                if grp % 2 == 1:
                    P.wait(t, "bcev", P.val("bcev"))
                    P.wait(t, "datv", P.val("datv"))
                t.matmul(dbank[:], ones_r[:, 0:128],
                         sro[0:1, h0 * 128:(h0 + 4) * 128],
                         start=True, stop=False)
                for hh in range(4):
                    h = h0 + hh
                    t.matmul(dbank[:, hh * 128:(hh + 1) * 128],
                             sro[0:1, h * 128:(h + 1) * 128],
                             negr[0:1, 0:128],
                             start=False, stop=False)
                last = t.matmul(dbank[:], idbf[:], trineg[:],
                                start=False, stop=True)
                P.inc("dmm", last)

            @blk.scalar
            def _(s, grp=grp, e4b=e4b, dbank=dbank):
                P.wait(s, "dmm")
                P.wait(s, "eglv", max(0, P.val("eglv") - 1))
                P.wait(s, "gbld", max(0, P.val("gbld") - 1))
                last = s.activation(e4b[:], dbank[:], AF.Exp)
                P.inc("eexp", last)
                P.inc("dpe", last)

            @blk.vector
            def _(v, grp=grp, q=q, e4b=e4b, g4b=g4b):
                P.wait(v, "eexp")
                P.wait(v, "bcev", P.val("bcev"))
                P.wait(v, "gfree", YMM_BASE + 2 * (grp - 1) if grp >= 2
                       else max(0, YMM_BASE - 2))
                last = v.tensor_tensor(
                    g4b[:].rearrange("p (h t) -> p h t", h=4),
                    e4b[:].rearrange("p (h t) -> p h t", h=4),
                    bcs[:, q * 128:(q + 1) * 128].unsqueeze(1).broadcast_to(
                        [128, 4, 128]), ALU.mult)
                P.inc("gbld", last)
        with BLOCK() as blk:
            @blk.vector
            def _(v, grp=grp, h0=h0, e4b=e4b):
                P.wait(v, "gbld")
                P.wait(v, "eglf", P.val("eglf"))
                P.inc("eglv", v.tensor_copy(
                    egl[:, h0:h0 + 4],
                    e4b[:].rearrange("p (h t) -> p h t", h=4)[:, :, 127:128]
                    .squeeze(2)))
    def emit_Y(grp):
        h0 = grp * 4
        par = grp % 2
        g4b = g4b2[par]
        with BLOCK() as blk:
            @blk.tensor
            def _(t, grp=grp, h0=h0, g4b=g4b):
                P.wait(t, "gbld", GBLD_BASE[0] + grp + 1)
                P.wait(t, "xscv", P.val("xscv"))
                P.wait(t, "hupd", P.val("hupd"))
                P.wait(t, "hbfc", P.val("hbfc"))
                P.wait(t, "ype", P.val("ype"))
                first = (grp % 2 == 0)
                lastp = (grp % 2 == 1)
                for hh in range(4):
                    h = h0 + hh
                    cs_ = slice(((grp % 2) * 4 + hh) * 64,
                                ((grp % 2) * 4 + hh + 1) * 64)
                    t.matmul(ps_zz[:, 512:1024][:, cs_],
                             g4b[:, hh * 128:(hh + 1) * 128],
                             xsc[:, h * 64:(h + 1) * 64],
                             start=(first and hh == 0),
                             stop=(lastp and hh == 3))
                cp_ = slice((grp % 2) * 256, (grp % 2) * 256 + 256)
                last = t.matmul(ps_zz[:, 1024:1536][:, cp_],
                                csb[:, q * 128:(q + 1) * 128],
                                hbf[:, h0 * 64:(h0 + 4) * 64],
                                start=first, stop=lastp)
                P.inc("ymm", last)
                P.inc("gfree", last)
        if grp % 2 == 1:
            pr = grp // 2
            with BLOCK() as blk:
                @blk.vector
                def _(v, pr=pr, grp=grp):
                    P.wait(v, "ymm")
                    P.wait(v, "ftev", P.val("ftev"))
                    P.inc("yi1", v.tensor_tensor(
                        yint[:].rearrange("p (h j) -> p h j", h=8),
                        ps_zz[:, 1024:1536].rearrange("p (h j) -> p h j", h=8),
                        eslT[:, (grp - 1) * 4:(grp + 1) * 4].unsqueeze(2)
                        .broadcast_to([128, 8, 64]), ALU.mult))
            with BLOCK() as blk:
                @blk.vector
                def _(v, pr=pr, n=n, q=q):
                    P.wait(v, "yi1")
                    P.wait(v, "ylwr", P.val("ylwr"))
                    P.wait(v, "xscv", P.val("xscv"))
                    sl_ = slice(pr * 512, (pr + 1) * 512)
                    P.inc("ylv", v.tensor_tensor(
                        ylsb[:, sl_].rearrange("p (h j) -> p h j", h=8),
                        xtok[:, sl_].rearrange("p (h j) -> p h j", h=8),
                        d32[:, pr * 8:(pr + 1) * 8].unsqueeze(2)
                        .broadcast_to([128, 8, 64]), ALU.mult))
                    P.wait(v, "ylv")
                    P.inc("ylv", v.tensor_tensor(ylsb[:, sl_], ylsb[:, sl_],
                                                 yint[:], ALU.add))
                    P.wait(v, "ylv")
                    last = v.tensor_tensor(ylsb[:, sl_], ylsb[:, sl_],
                                           ps_zz[:, 512:1024], ALU.add)
                    P.inc("ylv", last)
                    P.inc("ype", last)

    for _g in range(10):
        if _g < 8:
            emit_D(_g)
        if _g >= 2:
            emit_Y(_g - 2)

    with BLOCK() as blk:
        @blk.sync
        def _(sy, n=n, q=q):
            P.wait(sy, "ylv")
            P.wait(sy, "ylwr", P.val("ylwr"))
            P.inc("ylwr", sy.dma_start(
                out=YL[n * 4 + q, :, :, :].transpose([1, 0, 2]),
                in_=ylsb[:].rearrange("p (a c) -> p a c", a=4)), 16)

    with BLOCK() as blk:
        @blk.vector
        def _(v):
            P.wait(v, "eglv", P.val("eglv"))
            P.wait(v, "btev", P.val("btev"))
            P.inc("bscv", v.tensor_tensor(
                bsc[:].rearrange("p (h j) -> p h j", h=32),
                btok[:].unsqueeze(1).broadcast_to([128, 32, 64]),
                egl[:].unsqueeze(2).broadcast_to([128, 32, 64]), ALU.mult))

        @blk.scalar
        def _(s):
            P.wait(s, "srol")
            P.wait(s, "eglv", P.val("eglv"))
            last = s.activation(darow[:], sro[0:1, :].rearrange(
                "o (h t) -> o h t", h=32)[:, :, 127:128].squeeze(2), AF.Exp)
            P.inc("dar", last)
            P.inc("srof", last)

        @blk.tensor
        def _(t):
            P.wait(t, "dar")
            P.wait(t, "datf", P.val("datf"))
            P.wait(t, "bcev", P.val("bcev"))
            P.wait(t, "dpe", P.val("dpe"))
            P.inc("datm", t.matmul(ps_zz[0:64, 1536:1568], ones_r[:, 0:64],
                                   darow[:], start=True, stop=True))

        @blk.vector
        def _(v):
            P.wait(v, "datm")
            P.inc("datv", v.tensor_copy(dat[:], ps_zz[0:64, 1536:1568]))

    for r in range(4):
        with BLOCK() as blk:
            @blk.tensor
            def _(t, r=r):
                P.wait(t, "bscv")
                P.wait(t, "xscv", P.val("xscv"))
                P.wait(t, "ftev", P.val("ftev"))
                P.wait(t, "dpe", P.val("dpe"))
                P.wait(t, "hape", P.val("hape"))
                last = None
                for hh in range(8):
                    h = r * 8 + hh
                    last = t.matmul(ps_zz[0:64, hh * 64:(hh + 1) * 64],
                                    bsc[:, h * 64:(h + 1) * 64],
                                    xsc[:, h * 64:(h + 1) * 64],
                                    start=(hh == 0), stop=(hh == 7))
                P.inc("hamm", last)

            @blk.vector
            def _(v, r=r):
                P.wait(v, "hamm")
                P.wait(v, "datv", P.val("datv"))
                P.wait(v, "hupd", P.val("hupd"))
                P.wait(v, "hbfc", P.val("hbfc"))
                P.wait(v, "ylv", P.val("ylv"))
                P.inc("ht1", v.tensor_tensor(
                    yint[0:64, :].rearrange("p (h j) -> p h j", h=8),
                    hsrc[:, r * 512:(r + 1) * 512].rearrange(
                        "p (h j) -> p h j", h=8),
                    dat[:, r * 8:(r + 1) * 8].unsqueeze(2).broadcast_to(
                        [64, 8, 64]), ALU.mult))
        with BLOCK() as blk:
            @blk.vector
            def _(v, r=r):
                P.wait(v, "ht1")
                last = v.scalar_tensor_tensor(
                    hdst[:, r * 512:(r + 1) * 512], yint[0:64, :], 1.0,
                    ps_zz[0:64, 0:512], ALU.mult, ALU.add)
                P.inc("ht2", last)
                P.inc("hape", last)
                if r == 3:
                    P.inc("hupd", last)
                    P.inc("datf", last)
                    P.inc("eglf", last)

            @blk.scalar
            def _(s, r=r):
                P.wait(s, "ht2")
                P.wait(s, "ymm", P.val("ymm"))
                P.inc("hbfc", s.activation(
                    hbf[:, r * 512:(r + 1) * 512],
                    hdst[:, r * 512:(r + 1) * 512], AF.Copy))


def _build_fixup(M):
    nc, P = M["nc"], M["P"]
    BLOCK = M["BLOCK"]
    sb, pst = M["sb"], M["pst"]
    T, NCH, NTT = M["T"], M["NCH"], M["NTT"]
    idbf, idf32, ones_r, hmask, scal = (M["idbf"], M["idf32"], M["ones_r"],
                                        M["hmask"], M["scal"])
    YL, ZS, CSP, SGH, HALL, SSP, WOUTB = (M["YL"], M["ZS"], M["CSP"], M["SGH"],
                                          M["HALL"], M["SSP"], M["WOUTB"])
    ph7 = ExitStack()
    heff = [sb(f"heff{i}", [64, D_INNER], F32, ph7) for i in range(2)]
    hfb = sb("hfb", [64, D_INNER], BF16, ph7)
    hsl = sb("hsl", [64, D_INNER], F32, ph7)
    sgm = sb("sgm", [32, T], F32, ph7)
    egt = sb("egt", [128, NTT * 32], F32, ph7)
    ylb = [sb(f"ylb{i}", [128, D_INNER], BF16, ph7) for i in range(2)]
    zsb2 = [sb(f"zsb2{i}", [128, D_INNER], BF16, ph7) for i in range(2)]
    ctb = [sb(f"ctb{i}", [64, 128], BF16, ph7) for i in range(2)]
    ygb = sb("ygb", [128, D_INNER], BF16, ph7)
    ynb = sb("ynb", [128, D_INNER], BF16, ph7)
    rstt = sb("rstt", [128, 8], F32, ph7)
    ynT = sb("ynT", [128, 16 * 512], BF16, ph7)
    wob = [sb(f"wob{i}", [128, 16 * 128], BF16, ph7) for i in range(2)]
    spb = [sb(f"spb{i}", [128, 512], BF16, ph7) for i in range(2)]
    yfx = sb("yfx", [128, 512], F32, ph7)
    ps_f = pst("ps_f", [128, 1024], F32, ph7)
    ps_t2 = M["ps_tb"]
    ps_o = pst("ps_o", [128, 512], F32, ph7)

    for j in range(NCORES):
        with BLOCK() as blk:
            @blk.sync
            def _(sy, j=j):
                P.wait(sy, "ccah")
                P.wait(sy, "t4wr")
                P.wait(sy, "ugwr")
                P.wait(sy, "ylwr")
                P.wait(sy, "zswr")
                P.wait(sy, "csp_wr")
                P.wait(sy, "sg_wr")
                P.wait(sy, "xnt_wr")
                P.wait(sy, "hslf", P.val("hslf"))
                P.wait(sy, "hsll", P.val("hsll"))
                P.inc("hsll", sy.dma_start(
                    out=hsl[:].rearrange("p q -> p q"),
                    in_=HALL[j:j + 1, :].rearrange("o (p q) -> p (o q)", p=64)),
                    16)

            @blk.vector
            def _(v, j=j):
                P.wait(v, "hsll")
                if j == 0:
                    last = v.tensor_scalar(heff[0][:], hsl[:],
                                           hmask[:, 0:1], None, ALU.mult)
                else:
                    last = v.scalar_tensor_tensor(
                        heff[j % 2][:], hsl[:], hmask[:, j:j + 1],
                        heff[1 - j % 2][:], ALU.mult, ALU.add)
                P.inc("heffv", last)
                P.inc("hslf", last)
    hfin = heff[(NCORES - 1) % 2]
    with BLOCK() as blk:
        @blk.scalar
        def _(s):
            P.wait(s, "heffv", P.val("heffv"))
            P.inc("hfbc", s.activation(hfb[:], hfin[:], AF.Copy))

    with BLOCK() as blk:
        @blk.sync
        def _(sy):
            P.wait(sy, "sg_wr")
            P.wait(sy, "ugwr")
            P.wait(sy, "ylwr")
            P.wait(sy, "sgl", P.val("sgl"))
            P.inc("sgl", sy.dma_start(out=sgm[:], in_=SGH), 16)

        @blk.scalar
        def _(s):
            P.wait(s, "sgl")
            P.inc("sge", s.activation(sgm[:], sgm[:], AF.Exp))

    egt_vals = []
    for tt in range(NTT):
        with BLOCK() as blk:
            @blk.tensor
            def _(t, tt=tt):
                P.wait(t, "sge")
                P.wait(t, "egtv", P.val("egtv"))
                P.inc("egtt", t.transpose(ps_o[:, 0:32],
                                          sgm[:, tt * 128:(tt + 1) * 128],
                                          idf32[0:32, 0:32]))

            @blk.vector
            def _(v, tt=tt):
                P.wait(v, "egtt")
                P.inc("egtv", v.tensor_copy(egt[:, tt * 32:(tt + 1) * 32],
                                            ps_o[:, 0:32]))
        egt_vals.append(P.val("egtv"))

    def fixup_tile(tt):
        bi = tt % 2
        nch_ = tt // 4
        with BLOCK() as blk:
            @blk.sync
            def _(sy, tt=tt, bi=bi):
                P.wait(sy, "ylwr")
                P.wait(sy, "zswr")
                P.wait(sy, "csp_wr")
                P.wait(sy, "ugwr")
                P.wait(sy, "xnt_wr")
                P.wait(sy, "ysqs", P.val("ysqs"))
                P.wait(sy, f"fxlf{bi}", P.val(f"fxlf{bi}"))
                P.wait(sy, f"fxl{bi}", P.val(f"fxl{bi}"))
                P.inc(f"fxl{bi}", sy.dma_start(
                    out=ylb[bi][:].rearrange("p (a c) -> p a c", a=4),
                    in_=YL[tt, :, :, :].transpose([1, 0, 2])), 16)
                P.wait(sy, f"fxl{bi}", P.val(f"fxl{bi}"))
                P.inc(f"fxl{bi}", sy.dma_start(
                    out=zsb2[bi][:].rearrange("p (a c) -> p a c", a=4),
                    in_=ZS[tt, :, :, :].transpose([1, 0, 2])), 16)
                P.wait(sy, f"fxl{bi}", P.val(f"fxl{bi}"))
                P.inc(f"fxl{bi}", sy.dma_start(
                    out=ctb[bi][:], in_=CSP[:, tt * 128:(tt + 1) * 128]), 16)

        for hb in range(2):
            with BLOCK() as blk:
                @blk.tensor
                def _(t, tt=tt, bi=bi, hb=hb):
                    P.wait(t, f"fxl{bi}")
                    P.wait(t, "hfbc", P.val("hfbc"))
                    P.wait(t, "yfx1", P.val("yfx1"))
                    last = None
                    for hl8 in range(2):
                        last = t.matmul(
                            ps_f[:, hl8 * 512:(hl8 + 1) * 512],
                            ctb[bi][:],
                            hfb[:, (hb * 16 + hl8 * 8) * 64:
                                (hb * 16 + hl8 * 8 + 8) * 64],
                            start=True, stop=True)
                    P.inc("fxmm", last)
            for dg in (hb * 2, hb * 2 + 1):
                with BLOCK() as blk:
                    @blk.vector
                    def _(v, tt=tt, dg=dg, bi=bi):
                        P.wait(v, "fxmm")
                        P.wait(v, "egtv", egt_vals[tt])
                        P.wait(v, "yfx_f", P.val("yfx_f"))
                        P.inc("yfx1", v.tensor_tensor(
                            yfx[:].rearrange("p (h j) -> p h j", h=8),
                            ps_f[:, (dg % 2) * 512:
                                 (dg % 2 + 1) * 512].rearrange(
                                "p (h j) -> p h j", h=8),
                            egt[:, tt * 32 + dg * 8:tt * 32 + (dg + 1) * 8]
                            .unsqueeze(2).broadcast_to([128, 8, 64]),
                            ALU.mult))
                with BLOCK() as blk:
                    @blk.vector
                    def _(v, tt=tt, dg=dg, bi=bi):
                        P.wait(v, "yfx1")
                        P.inc("yfx2", v.scalar_tensor_tensor(
                            yfx[:], yfx[:], 1.0,
                            ylb[bi][:, dg * 512:(dg + 1) * 512],
                            ALU.mult, ALU.add))
                with BLOCK() as blk:
                    @blk.vector
                    def _(v, tt=tt, dg=dg, bi=bi):
                        P.wait(v, "yfx2")
                        last = v.tensor_tensor(
                            ygb[:, dg * 512:(dg + 1) * 512],
                            yfx[:],
                            zsb2[bi][:, dg * 512:(dg + 1) * 512],
                            ALU.mult)
                        P.inc("ygv", last)
                        P.inc("yfx_f", last)
                        if dg == 3:
                            P.inc("yfpe", last)
                            P.inc(f"fxlf{bi}", last)

        with BLOCK() as blk:
            @blk.scalar
            def _(s, tt=tt, bi=bi):
                P.wait(s, "ygv", P.val("ygv"))
                P.wait(s, "ynv", P.val("ynv"))
                P.inc("ysqs", s.activation(ylb[bi][:], ygb[:], AF.Square,
                                           accum_out=rstt[:, 0:1]))
        with BLOCK() as blk:
            @blk.vector
            def _(v):
                P.wait(v, "ysqs")
                P.inc("rch", v.tensor_scalar(rstt[:, 1:2], rstt[:, 0:1],
                                             1.0 / 2048.0, EPS,
                                             ALU.mult, ALU.add))
        with BLOCK() as blk:
            @blk.vector
            def _(v):
                P.wait(v, "rch")
                P.inc("rch", v.reciprocal(rstt[:, 2:3], rstt[:, 1:2]))
        with BLOCK() as blk:
            @blk.scalar
            def _(s):
                P.wait(s, "rch")
                P.inc("rss", s.activation(rstt[:, 3:4], rstt[:, 2:3], AF.Sqrt))

            @blk.vector
            def _(v, tt=tt):
                P.wait(v, "rss")
                P.inc("ynv", v.tensor_scalar(ynb[:], ygb[:], rstt[:, 3:4],
                                             None, ALU.mult))

        for g4 in range(4):
            with BLOCK() as blk:
                @blk.tensor
                def _(t, tt=tt, g4=g4):
                    P.wait(t, "ynv", P.val("ynv"))
                    P.wait(t, "xtev", P.val("xtev"))
                    P.wait(t, "ytev", P.val("ytev"))
                    last = None
                    for j in range(4):
                        kb = g4 * 4 + j
                        last = t.transpose(ps_t2[:, j * 128:(j + 1) * 128],
                                           ynb[:, kb * 128:(kb + 1) * 128],
                                           idbf[:])
                    P.inc("yttr", last)

                @blk.vector
                def _(v, tt=tt, g4=g4):
                    P.wait(v, "yttr")
                    last = None
                    for j in range(4):
                        kb = g4 * 4 + j
                        last = v.tensor_copy(
                            ynT[:, kb * 512 + (tt % 4) * 128:
                                kb * 512 + (tt % 4 + 1) * 128],
                            ps_t2[:, j * 128:(j + 1) * 128])
                    P.inc("ytev", last)

    def outproj(nch_):
        if True:
            for ob in range(8):
                obi = ob % 2
                with BLOCK() as blk:
                    @blk.sync
                    def _(sy, ob=ob, obi=obi):
                        P.wait(sy, f"wobf{obi}", P.val(f"wobf{obi}"))
                        P.wait(sy, "woutb_wr")
                        P.wait(sy, f"wobl{obi}", P.val(f"wobl{obi}"))
                        P.inc(f"wobl{obi}", sy.dma_start(
                            out=wob[obi][:].rearrange("p (k q) -> p k q", k=16),
                            in_=WOUTB[ob, :, :, :].transpose([1, 0, 2])), 16)

                    @blk.tensor
                    def _(t, ob=ob, obi=obi, nch_=nch_):
                        P.wait(t, f"wobl{obi}")
                        P.wait(t, "egtv", egt_vals[-1])
                        P.wait(t, "ytev", P.val("ytev"))
                        P.wait(t, "ope", P.val("ope"))
                        last = None
                        for kb in range(16):
                            last = t.matmul(
                                ps_o[:],
                                wob[obi][:, kb * 128:(kb + 1) * 128],
                                ynT[:, kb * 512:(kb + 1) * 512],
                                start=(kb == 0), stop=(kb == 15))
                        P.inc("opmm", last)
                        P.inc(f"wobf{obi}", last)

                    @blk.scalar
                    def _(s, ob=ob, obi=obi):
                        P.wait(s, "opmm")
                        P.wait(s, "sspwr", P.val("sspwr"))
                        last = s.copy(spb[obi][:], ps_o[:])
                        P.inc("opev", last)
                        P.inc("ope", last)

                    @blk.sync
                    def _(sy, ob=ob, obi=obi, nch_=nch_):
                        P.wait(sy, "opev")
                        P.wait(sy, "sspwr", P.val("sspwr"))
                        P.inc("sspwr", sy.dma_start(
                            out=SSP[ob, nch_, :, :], in_=spb[obi][:]), 16)
            with BLOCK() as blk:
                @blk.vector
                def _(v):
                    P.wait(v, "opmm", P.val("opmm"))
                    P.inc("ytev", v.tensor_copy(rstt[:, 5:6], rstt[:, 3:4]))

    return fixup_tile, outproj, ph7


def _build_final(M, fixup_tile, outproj):
    nc, P = M["nc"], M["P"]
    BLOCK = M["BLOCK"]
    sb, pst = M["sb"], M["pst"]
    T, NCH = M["T"], M["NCH"]
    idf32, scal = M["idf32"], M["scal"]
    T4S, SSP, AGW_O, x_in, out_t = (M["T4S"], M["SSP"], M["AGW_O"], M["x_in"],
                                    M["out_t"])
    ph8 = ExitStack()
    wdt = sb("wdt", [128, 24 * 8 * 128], BF16, ph8)
    t4b = [sb(f"t4b{i}", [128, 2048], BF16, ph8) for i in range(2)]
    ob_sb = sb("ob_sb", [128, 24 * 512], BF16, ph8)
    gsb2 = [sb(f"gsb{i}", [128, 512], BF16, ph8) for i in range(2)]
    ssb = [sb(f"ssb{i}", [128, 512], BF16, ph8) for i in range(2)]
    fsb2 = [sb(f"fsb{i}", [128, 512], F32, ph8) for i in range(2)]
    xtb = [sb(f"xtb{i}", [128, 128], F32, ph8) for i in range(2)]
    osb = [sb(f"osb{i}", [128, 128], F32, ph8) for i in range(2)]
    ps_dn = pst("ps_dn", [128, 1024], F32, ph8)
    ps_ft = pst("ps_ft", [128, 512], F32, ph8)

    with BLOCK() as blk:
        @blk.sync
        def _(sy):
            P.wait(sy, "ccagw")
            P.wait(sy, "zswr")
            P.wait(sy, "ylwr")
            P.wait(sy, "csp_wr")
            P.wait(sy, "t4wr")
            P.wait(sy, "t3wr")
            for r in range(8):
                P.wait(sy, "wdtl", P.val("wdtl"))
                P.inc("wdtl", sy.dma_start(
                    out=wdt[:, r * 24 * 128:(r + 1) * 24 * 128].rearrange(
                        "p (k q) -> p k q", k=24),
                    in_=AGW_O[r, AGW_DOWN:AGW_DOWN + 24 * TIL].rearrange(
                        "(k p q) -> k p q", k=24, p=128).transpose([1, 0, 2])),
                    16)

    OV_VALS = []
    for n in range(NCH):
        bi = n % 2
        for j4 in range(4):
            fixup_tile(n * 4 + j4)
        outproj(n)
        OBV_VALS = []
        for sj in range(6):
            with BLOCK() as blk:
                @blk.sync
                def _(sy, n=n, sj=sj, bi=bi):
                    P.wait(sy, "t4wr")
                    P.wait(sy, f"t4f{bi}", P.val(f"t4f{bi}"))
                    P.wait(sy, f"t4l{bi}", P.val(f"t4l{bi}"))
                    P.inc(f"t4l{bi}", sy.dma_start(
                        out=t4b[bi][:].rearrange("p (j q) -> p j q", j=4),
                        in_=T4S[sj * 4:(sj + 1) * 4, n, :, :]
                        .transpose([1, 0, 2])), 16)

                @blk.vector
                def _(v, n=n, sj=sj, bi=bi):
                    P.wait(v, f"t4l{bi}")
                    P.wait(v, "scbc")
                    P.wait(v, "obf", P.val("obf"))
                    last = v._custom_dve(
                        QROUND, out=ob_sb[:, sj * 4 * 512:(sj + 1) * 4 * 512],
                        in0=t4b[bi][:], s0=scal[:, SC_F4R4:SC_F4R4 + 1],
                        s1=scal[:, SC_S4:SC_S4 + 1], imm2=MAGIC)
                    P.inc("obv", last)
                    P.inc(f"t4f{bi}", last)
            OBV_VALS.append(P.val("obv"))

        for ob in range(8):
            sbi = ob % 2
            with BLOCK() as blk:
                @blk.sync
                def _(sy, n=n, ob=ob, sbi=sbi):
                    P.wait(sy, "sspwr")
                    P.wait(sy, f"ssbf{sbi}", P.val(f"ssbf{sbi}"))
                    P.wait(sy, f"ssbl{sbi}", P.val(f"ssbl{sbi}"))
                    P.inc(f"ssbl{sbi}", sy.dma_start(
                        out=ssb[sbi][:], in_=SSP[ob, n, :, :]), 16)

                @blk.tensor
                def _(t, n=n, ob=ob, sbi=sbi, OBV_VALS=OBV_VALS):
                    P.wait(t, "wdtl")
                    P.wait(t, "dnpe", max(0, P.val("dnpe") - 1))
                    last = None
                    prev_sj = -1
                    for kb in range(24):
                        sj_ = kb // 4
                        if sj_ != prev_sj:
                            P.wait(t, "obv", OBV_VALS[sj_])
                            prev_sj = sj_
                        last = t.matmul(
                            ps_dn[:, sbi * 512:(sbi + 1) * 512],
                            wdt[:, (ob * 24 + kb) * 128:(ob * 24 + kb + 1) * 128],
                            ob_sb[:, kb * 512:(kb + 1) * 512],
                            start=(kb == 0), stop=(kb == 23))
                    P.inc("dnmm", last)
                    if ob == 7:
                        P.inc("obf", last)

                @blk.scalar
                def _(s, ob=ob, sbi=sbi):
                    P.wait(s, "dnmm")
                    P.wait(s, "fv", max(0, P.val("fv") - 1))
                    last = s.activation(gsb2[sbi][:],
                                        ps_dn[:, sbi * 512:(sbi + 1) * 512],
                                        AF.Silu)
                    P.inc("gev", last)
                    P.inc("dnpe", last)

                @blk.vector
                def _(v, ob=ob, sbi=sbi):
                    P.wait(v, "gev", P.val("gev"))
                    P.wait(v, f"ssbl{sbi}")
                    P.wait(v, "ftmm", max(0, P.val("ftmm") - 4))
                    last = v.tensor_tensor(fsb2[sbi][:], gsb2[sbi][:],
                                           ssb[sbi][:], ALU.mult)
                    P.inc("fv", last)
                    P.inc(f"ssbf{sbi}", last)

            for j in range(4):
                xbi = j % 2
                with BLOCK() as blk:
                    @blk.sync
                    def _(sy, n=n, ob=ob, j=j, xbi=xbi):
                        P.wait(sy, f"xtbf{xbi}", P.val(f"xtbf{xbi}"))
                        tt = n * 4 + j
                        P.wait(sy, f"xtbl{xbi}", P.val(f"xtbl{xbi}"))
                        P.inc(f"xtbl{xbi}", sy.dma_start(
                            out=xtb[xbi][:, 0:128],
                            in_=x_in[tt * 128:(tt + 1) * 128,
                                     ob * 128:(ob + 1) * 128]), 16)

                    @blk.tensor
                    def _(t, n=n, ob=ob, j=j, sbi=sbi,
                          ovv=(OV_VALS[-3] if len(OV_VALS) >= 3 else 0)):
                        P.wait(t, "fv", P.val("fv"))
                        P.wait(t, "ftpe", ovv)
                        P.inc("ftmm", t.transpose(
                            ps_ft[:, j * 128:(j + 1) * 128],
                            fsb2[sbi][:, j * 128:(j + 1) * 128],
                            idf32[:]))

                    @blk.vector
                    def _(v, n=n, ob=ob, j=j, xbi=xbi):
                        P.wait(v, "ftmm", P.val("ftmm"))
                        P.wait(v, f"xtbl{xbi}")
                        P.wait(v, "outwr", max(0, P.val("outwr") - 32))
                        last = v.scalar_tensor_tensor(
                            osb[j % 2][:, 0:128], ps_ft[:, j * 128:(j + 1) * 128],
                            1.0, xtb[xbi][:, 0:128], ALU.mult, ALU.add)
                        P.inc("ov", last)
                        P.inc("ftpe", last)
                        P.inc(f"xtbf{xbi}", last)
                        OV_VALS.append(P.val("ov"))

                    @blk.sync
                    def _(sy, n=n, ob=ob, j=j):
                        P.wait(sy, "ov")
                        tt = n * 4 + j
                        P.wait(sy, "outwr", P.val("outwr"))
                        last = sy.dma_start(
                            out=out_t[tt * 128:(tt + 1) * 128,
                                      ob * 128:(ob + 1) * 128],
                            in_=osb[j % 2][:, 0:128])
                        P.inc("outwr", last, 16)

    with BLOCK() as blk:
        @blk.sync
        def _(sy):
            P.wait(sy, "outwr")
            P.wait(sy, "zswr", P.val("zswr"))

    ph8.close()


_NC_CACHE = {}


def kernel(**inputs):
    x = np.ascontiguousarray(inputs["L0_attn_x"], dtype=np.float32)
    W_in = np.ascontiguousarray(inputs["L0_attn_W_in"], dtype=np.float32)
    conv_w = np.ascontiguousarray(inputs["L0_attn_conv_w"], dtype=np.float32)
    conv_b = np.ascontiguousarray(inputs["L0_attn_conv_b"], dtype=np.float32)
    A_log = np.ascontiguousarray(inputs["L0_attn_A_log"], dtype=np.float32)
    D_param = np.ascontiguousarray(inputs["L0_attn_D_param"], dtype=np.float32)
    dt_bias = np.ascontiguousarray(inputs["L0_attn_dt_bias"], dtype=np.float32)
    norm_w = np.ascontiguousarray(inputs["L0_attn_norm_w"], dtype=np.float32)
    W_out = np.ascontiguousarray(inputs["L0_attn_W_out"], dtype=np.float32)
    w_gate = np.ascontiguousarray(inputs["L0_ffn_w_gate"], dtype=np.float32)
    w_up = np.ascontiguousarray(inputs["L0_ffn_w_up"], dtype=np.float32)
    w_down = np.ascontiguousarray(inputs["L0_ffn_w_down"], dtype=np.float32)

    B, L_, D = x.shape
    T = (B * L_) // NCORES
    if T not in _NC_CACHE:
        _NC_CACHE[T] = build_nc(T=T)
    nc = _NC_CACHE[T]

    ident = np.eye(128, dtype=np.float32)
    tri = np.where(np.arange(128)[:, None] > np.arange(128)[None, :],
                   np.float32(-1e9), np.float32(0.0))
    trineg4 = np.tile(tri, (1, 4)).astype(BF)
    convw_sb = conv_w[:, 0, :].reshape(17, 128, 4).transpose(1, 0, 2).reshape(128, 68)
    convb_sb = conv_b.reshape(17, 128).T.copy()
    normw_sb = norm_w.reshape(16, 128).T.copy()

    in_maps = []
    for k in range(NCORES):
        b, half = k // 2, k % 2
        t0 = half * T
        halo = (np.zeros((3, D), np.float32) if half == 0 else x[b, t0 - 3:t0])
        hm = np.zeros((64, 8), np.float32)
        if half == 1:
            hm[:, k - 1] = 1.0
        in_maps.append({
            "x": x[b, t0:t0 + T], "x_halo": halo, "w_in": W_in, "w_out": W_out,
            "wup_sh": w_up[k * 384:(k + 1) * 384],
            "wgate_sh": w_gate[k * 384:(k + 1) * 384],
            "wdown_sh": w_down[k * 128:(k + 1) * 128],
            "conv_w_sb": convw_sb, "conv_b_sb": convb_sb, "normw_sb": normw_sb,
            "a_log": A_log.reshape(32, 1), "dt_bias": dt_bias.reshape(32, 1),
            "d_par": D_param.reshape(1, 32), "hmask": hm,
            "ident_bf": ident.astype(BF), "ident_f32": ident,
            "trineg4": trineg4,
        })
    res = run_bass_kernel_spmd(nc, in_maps, core_ids=list(range(NCORES)))
    out = np.empty((B, L_, D), np.float32)
    for k in range(NCORES):
        b, half = k // 2, k % 2
        out[b, half * T:(half + 1) * T] = res.results[k]["out"]
    return out
